# version 08
# baseline (speedup 1.0000x reference)
import numpy as np
import ml_dtypes
from contextlib import ExitStack

import concourse.bass as bass
import concourse.mybir as mybir
from concourse.bass_utils import run_bass_kernel_spmd

F32 = mybir.dt.float32
BF16 = mybir.dt.bfloat16
AF = mybir.ActivationFunctionType
ALU = mybir.AluOpType
AX = mybir.AxisListType
BF = ml_dtypes.bfloat16

B, L, D = 4, 4096, 1024
H, DV = 4, 256
NE, HF = 8, 512
CH = 128
SEG = 512
TOWN = 2048
NSEG = TOWN // SEG
NCH = SEG // CH
DCH = D // 128
VW = SEG + 64
PADC = 4
PREW = SEG + PADC
POW = SEG + 4

DBG_POINT = None
DBG_SHAPE = None
STAGE = 9
DSTAGE = 9
OSTAGE = 9
GSKIP = 0
HBAR = 0
N_OWN = NSEG
N_PRE = NSEG
RUN_FFN = True
RACE_CHECK = False


def _cp(E, d, s):
    if hasattr(E, "copy"):
        return E.copy(d, s)
    return E.tensor_copy(d, s)


class Ctr:
    def __init__(self, ctx, nc, name, is_dma=False):
        self.h = ctx.enter_context(nc.semaphore(name))
        self.v = 0
        self.name = name
        self.is_dma = is_dma


class Prog:
    ENGINES = ("tensor", "vector", "scalar", "gpsimd", "sync")

    def __init__(self):
        self.q = {e: [] for e in self.ENGINES}
        self.lastwait = {e: {} for e in self.ENGINES}

    def op(self, eng, fn, waits=(), inc=None, dn=1):
        ws = []
        for (c, v) in waits:
            if v <= 0:
                continue
            if self.lastwait[eng].get(id(c), 0) < v:
                ws.append((c.h, v))
                self.lastwait[eng][id(c)] = v
        tgt = None
        if inc is not None:
            inc.v += dn
            tgt = (inc.h, dn)

        def thunk(E, ws=tuple(ws), tgt=tgt, fn=fn):
            for (h, v) in ws:
                E.wait_ge(h, v)
            ins = fn(E)
            if tgt is not None:
                if ins is None:
                    ins = E.nop()
                ins.then_inc(tgt[0], tgt[1])

        self.q[eng].append(thunk)


class BK:
    def __init__(self):
        self.nc = bass.Bass("TRN2", target_bir_lowering=False,
                            detect_race_conditions=RACE_CHECK)
        self.P = Prog()
        self.ctx = ExitStack()
        nc = self.nc
        self.sT = Ctr(self.ctx, nc, "sT")
        self.sV = Ctr(self.ctx, nc, "sV")
        self.sS = Ctr(self.ctx, nc, "sS")
        self.sG = Ctr(self.ctx, nc, "sG")
        NQ = 6
        self.qsems = {q: [Ctr(self.ctx, nc, f"d{q}{i}", is_dma=True)
                          for i in range(NQ)] for q in ("sync", "scalar",
                                                        "gpsimd")}
        self.qrr = {q: 0 for q in self.qsems}
        self.dbg_done = False

    def t(self, fn, waits=(), inc=False):
        self.P.op("tensor", fn, waits, self.sT if inc else None)
        return self.sT.v

    def v(self, fn, waits=(), inc=False):
        self.P.op("vector", fn, waits, self.sV if inc else None)
        return self.sV.v

    def s(self, fn, waits=(), inc=False):
        self.P.op("scalar", fn, waits, self.sS if inc else None)
        return self.sS.v

    def g(self, fn, waits=(), inc=False):
        self.P.op("gpsimd", fn, waits, self.sG if inc else None)
        return self.sG.v

    def dma(self, q, dst, src, waits=(), slow=False):
        ring = self.qsems[q]
        ctr = ring[self.qrr[q]]
        self.qrr[q] = (self.qrr[q] + 1) % len(ring)
        chain = [(ctr, ctr.v)]
        self.P.op(q, lambda E, d=dst, s=src, slow=slow: E.dma_start(
            out=d, in_=s, allow_slow_non_contiguous=slow),
            list(waits) + chain, ctr, dn=16)
        return [(ctr, ctr.v)]

    def qtok(self, q=None):
        qs = [q] if q else list(self.qsems)
        return [(c, c.v) for qq in qs for c in self.qsems[qq]]

    def dmaT(self, dst, src, waits=(), q="sync"):
        ring = self.qsems[q]
        ctr = ring[self.qrr[q]]
        self.qrr[q] = (self.qrr[q] + 1) % len(ring)
        chain = [(ctr, ctr.v)]
        self.P.op(q,
                  lambda E, d=dst, s=src: E.dma_start(out=d, in_=s,
                                                      transpose=True),
                  list(waits) + chain, ctr, dn=16)
        return [(ctr, ctr.v)]

    def relay(self, src_ap, dst_ap, N, waits=()):
        off = self.relay_off
        self.relay_off += (N + 127) & ~127
        scr = self.relay_h[off:off + N]
        wv = self.dma("gpsimd", scr, src_ap, waits=waits)
        return self.dma("gpsimd", dst_ap,
                        scr.rearrange("(c p) -> p c", p=128),
                        waits=wv, slow=True)

    def sbuf(self, name, shape, dt=BF16, ctx=None):
        return (ctx or self.ctx).enter_context(
            self.nc.sbuf_tensor(name, list(shape), dt))

    def snap(self):
        return [(self.sT, self.sT.v), (self.sV, self.sV.v), (self.sS, self.sS.v),
                (self.sG, self.sG.v)] + self.qtok()

    def barrier(self):
        snap = self.snap()
        for eng in Prog.ENGINES:
            self.P.op(eng, lambda E: None, waits=snap)

    def maybe_dbg(self, point, ap_fn):
        if DBG_POINT != point or self.dbg_done:
            return
        self.dbg_done = True
        ap = ap_fn()
        n0, n1 = DBG_SHAPE
        vv2 = self.v(lambda E, ap=ap: E.tensor_copy(
            self.dbgscr[0:n0, 0:n1], ap), waits=self.snap(), inc=True)
        self.dma("sync", self.dbg_h[:, :], self.dbgscr[0:n0, 0:n1],
                 waits=[(self.sV, vv2)])
        self.barrier()

    def declare_io(self):
        nc = self.nc

        def din(name, shape, dtype=F32):
            return nc.dram_tensor(name, list(shape), dtype, kind="ExternalInput").ap()

        self.xT = din("xT", [D, L])
        self.pflag = din("pflag", [128, 1])
        self.wq_h = din("wq", [D, D]); self.wk_h = din("wk", [D, D])
        self.wv_h = din("wv", [D, D]); self.wb_h = din("wb", [D, H])
        self.convf_h = din("convf", [128, 3 * DCH * 4])
        self.firl_h = din("firl", [128, DCH * 64])
        self.firs_h = din("firs", [128, DCH * 5])
        self.w1a_h = din("w1a", [D, D]); self.w1b_h = din("w1b", [16, D])
        self.w1bv_h = din("w1bv", [4, D])
        self.b1_h = din("b1", [128, DCH]); self.w2_h = din("w2", [D, H])
        self.b2row_h = din("b2row", [1, 4]); self.bbrow_h = din("bbrow", [1, 16])
        self.ltrow_h = din("ltrow", [1, 4]); self.crlrow_h = din("crlrow", [1, 4])
        self.rmsw_h = din("rmsw", [128, 2]); self.wo_h = din("wo", [D, D])
        self.gw_h = din("gw", [D, NE])
        self.w1e_h = din("w1e", [NE, D, HF])
        self.b1e_h = din("b1e", [128, NE * (HF // 128)])
        self.w2e_h = din("w2e", [NE, HF, D])
        self.b2e_h = din("b2e", [NE, D])
        self.ws1_h = din("ws1", [D, HF]); self.bs1_h = din("bs1", [128, HF // 128])
        self.ws2_h = din("ws2", [HF, D]); self.bs2row_h = din("bs2row", [D])
        self.rsc_h = din("rsc", [1, 1])
        self.c_h = din("cvec", [128, DCH])
        self.wmod_h = din("wmod", [D, 6 * D])
        self.out_h = nc.dram_tensor("out", [D, TOWN], F32, kind="ExternalOutput").ap()
        self.relay_h = nc.dram_tensor("relayscr", [1 << 18], F32,
                                      kind="Internal").ap()
        self.relay_off = 0
        self.xmid_h = nc.dram_tensor("xmid", [D, TOWN], F32, kind="Internal").ap()
        self.dbg_h = None
        if DBG_POINT is not None:
            self.dbg_h = nc.dram_tensor("dbg", list(DBG_SHAPE), F32,
                                        kind="ExternalOutput").ap()

        self.iden_c = nc.inline_tensor(np.eye(128, dtype=BF), "iden_c").ap()
        self.neglow_c = nc.inline_tensor(
            (-np.tril(np.ones((128, 128), np.float32), -1)).astype(BF),
            "neglow_c").ap()
        self.upin_c = nc.inline_tensor(
            np.triu(np.ones((128, 128), np.float32)).astype(BF), "upin_c").ap()
        epsp = np.zeros((128, 16), np.float32)
        epsp[:, 0::4] = 0.05
        epsp[:, 1::4] = 0.05
        self.epsp_c = nc.inline_tensor(epsp, "epsp_c").ap()
        est = np.zeros((128, 12 * 36), np.float32)
        for r in range(12):
            est[:, r * 36 + r] = 1.0
            if 8 <= r < 12:
                est[:, r * 36 + 24 + r] = 1.0
        self.estat_c = nc.inline_tensor(est.astype(BF), "estat_c").ap()
        e4 = np.zeros((128, 16), np.float32)
        for r in range(4):
            e4[:, r * 4 + r] = 1.0
        self.e4_c = nc.inline_tensor(e4.astype(BF), "e4_c").ap()

    def alloc_persistent(self):
        sb = self.sbuf
        self.iden = sb("iden", [128, 128])
        self.neglow = sb("neglow", [128, 128])
        self.upin = sb("upin", [128, 128])
        self.estat = sb("estat", [128, 12 * 36])
        self.e4 = sb("e4", [128, 16])
        self.epsc = sb("epsc", [128, 16], F32)
        self.onescol = sb("onescol", [128, 1])
        self.onerow = sb("onerow", [1, 128])
        self.onerow_f = sb("onerow_f", [1, 128], F32)
        self.convf = sb("convf_s", [128, 3 * DCH * 4], F32)
        self.firlf = sb("firlf", [128, DCH * 64], F32)
        self.firsf = sb("firsf", [128, DCH * 5], F32)
        self.w1b = sb("w1b_s", [16, D])
        self.w1bv = sb("w1bv_s", [4, D])
        self.wbW = sb("wbW", [128, DCH * H])
        self.gwW = sb("gwW", [128, DCH * NE])
        self.w2s = sb("w2s", [128, DCH * H])
        self.b1s = sb("b1s", [128, DCH], F32)
        self.b1es = sb("b1es", [128, NE * (HF // 128)], F32)
        self.bs1s = sb("bs1s", [128, HF // 128], F32)
        self.rmsw = sb("rmsw_s", [128, 2], F32)
        self.cvec = sb("cvec_s", [128, DCH])
        self.pfl = sb("pfl", [128, 1], F32)
        self.carow = sb("carow", [1, D], F32)
        self.sa_p = sb("sa_p", [128, DCH], F32)
        self.ga_p = sb("ga_p", [128, DCH], F32)
        self.sbp = sb("sbp", [128, DCH], F32)
        self.gb_p = sb("gb_p", [128, DCH], F32)
        self.S_sb = [sb(f"S_sb{h}", [128, 512]) for h in range(H)]
        self.vtail = sb("vtail", [128, DCH * 64])
        self.pret = [sb(f"pret{i}", [128, DCH * PADC]) for i in range(3)]
        self.wslot = [sb(f"wslot{i}", [128, DCH * 1024]) for i in range(2)]
        self.smallrow = sb("smallrow", [1, 64], F32)
        self.invtemp = sb("invtemp", [128, 16], F32)
        self.bbrep = sb("bbrep", [128, 16], F32)
        self.crlrep = sb("crlrep", [128, 4], F32)
        self.b2e_s = sb("b2e_s", [NE, D])
        self.r4scr = sb("r4scr", [4, 6 * SEG], F32)
        self.silc = sb("silc", [128, DCH])
        self.xf32 = [sb(f"xf32_{i}", [128, SEG], F32) for i in range(2)]
        self.xmo = [sb(f"xmo_{i}", [128, SEG], F32) for i in range(2)]
        self.rsc8 = sb("rsc8", [NE, 1], F32)
        self.kcon = sb("kcon", [128, 4], F32)
        self.bs2p = self.sbuf("bs2p", [128, DCH], F32)

        if DBG_POINT is not None:
            self.dbgscr = sb("dbgscr", [128, 2048], F32)
        self.PW = self.ctx.enter_context(self.nc.psum_tensor("PW", [128, 4096], F32))

    def pw(self, i, n=512, off=0):
        return self.PW[:, i * 512 + off:i * 512 + off + n]

    def brw(self, *bs):
        return [e for b in bs for e in self.bankr[b]]

    def brs(self, b, *ents):
        self.bankr[b] = list(ents)

    def load_consts(self):
        d = self.dma
        d("sync", self.iden[:], self.iden_c[:, :])
        d("sync", self.neglow[:], self.neglow_c[:, :])
        d("sync", self.upin[:], self.upin_c[:, :])
        d("sync", self.estat[:], self.estat_c[:, :])
        d("sync", self.e4[:], self.e4_c[:, :])
        d("sync", self.epsc[:], self.epsp_c[:, :])
        d("sync", self.convf[:], self.convf_h[:, :])
        d("sync", self.firlf[:], self.firl_h[:, :])
        d("sync", self.firsf[:], self.firs_h[:, :])
        d("sync", self.b1s[:], self.b1_h[:, :])
        d("sync", self.b1es[:], self.b1e_h[:, :])
        d("sync", self.bs1s[:], self.bs1_h[:, :])
        d("sync", self.rmsw[:], self.rmsw_h[:, :])
        d("sync", self.pfl[:], self.pflag[:, :])
        d("gpsimd", self.w1b[:], self.w1b_h[:, :])
        d("gpsimd", self.w1bv[:], self.w1bv_h[:, :])
        d("gpsimd", self.cvec[:], self.c_h[:, :])
        d("gpsimd", self.w2s[:].rearrange("p (c h) -> p c h", c=DCH),
          self.w2_h[:, :].rearrange("(c p) h -> p c h", p=128))
        d("gpsimd", self.wbW[:].rearrange("p (c h) -> p c h", c=DCH),
          self.wb_h[:, :].rearrange("(c p) h -> p c h", p=128))
        d("gpsimd", self.gwW[:].rearrange("p (c e) -> p c e", c=DCH),
          self.gw_h[:, :].rearrange("(c p) e -> p c e", p=128))
        d("gpsimd", self.b2e_s[:], self.b2e_h[:, :])
        self.c_sync = self.qtok()
        self.c_g = self.qtok()
        self.v(lambda E: E.memset(self.onescol[:], 1.0))
        self.v(lambda E: E.memset(self.onerow[:], 1.0))
        self.v(lambda E: E.memset(self.onerow_f[:], 1.0))
        self.v(lambda E: E.memset(self.vtail[:], 0.0))
        for i in range(3):
            self.v(lambda E, i=i: E.memset(self.pret[i][:], 0.0))
        self.v(lambda E: E.memset(self.r4scr[:], 0.0))
        self.v(lambda E: E.memset(self.smallrow[:], 0.0))
        self.v(lambda E: E.tensor_scalar_mul(self.w1bv[:], self.w1bv[:], -1.0),
               waits=list(self.c_g))
        self.v(lambda E: E.memset(self.kcon[:, 0:1], 1e-5))
        self.v(lambda E: E.memset(self.kcon[:, 1:2], 1e-6))
        self.init_v = self.v(lambda E: E.memset(self.kcon[:, 2:3], 1.0), inc=True)

    def phase_mod(self):
        sT, sS, sV = self.sT, self.sS, self.sV
        MODV = 8192
        sil_done = self.s(
            lambda E: E.activation(self.silc[:], self.cvec[:], AF.Silu),
            waits=self.c_g + [(sV, self.init_v)], inc=True)
        evacs = []
        wr_d = []
        wr_all = []
        for it in range(12):
            slot = self.wslot[it % 2][:, 0:DCH * 512]
            w = [(sT, evacs[it - 2][0])] if it >= 2 else []
            lv = self.dma("gpsimd", slot.rearrange("p (c t) -> p c t", c=DCH),
                          self.wmod_h[:, it * 512:(it + 1) * 512]
                          .rearrange("(c p) t -> p c t", p=128), waits=w)
            lt = 0
            for k in range(DCH):
                stat = self.silc[:, k:k + 1]
                mov = slot[:, k * 512:(k + 1) * 512]
                dstp = self.pw(it % 2)[0:1, :]
                lt = self.t(lambda E, s=stat, m=mov, d=dstp, k=k: E.matmul(
                    d, s, m, start=(k == 0), stop=(k == DCH - 1)),
                    waits=lv + [(sS, sil_done)] +
                          ([(sS, evacs[it - 2][1])] if it >= 2 else []),
                    inc=(k == DCH - 1))
            ev = self.s(lambda E, srcp=self.pw(it % 2)[0:1, :]: E.copy(
                self.r4scr[0:1, 0:512], srcp),
                waits=[(sT, lt)] + wr_d, inc=True)
            wr_d = self.dma("scalar", self.relay_h[MODV + it * 512:
                                                   MODV + (it + 1) * 512],
                            self.r4scr[0:1, 0:512], waits=[(sS, ev)])
            wr_all += wr_d
            evacs.append((lt, ev))
        pp_d = []
        for (dst, r0) in ((self.sa_p, 0), (self.ga_p, 2), (self.sbp, 3),
                          (self.gb_p, 5)):
            pp_d = pp_d + self.dma("scalar", dst[:],
                            self.relay_h[MODV + r0 * D:MODV + (r0 + 1) * D]
                            .rearrange("(c p) -> p c", p=128),
                            waits=wr_all, slow=True)
        t_sg = self.invtemp[:, 0:DCH]
        t_mg = self.invtemp[:, 8:8 + DCH]
        gprev = 0
        for g in (self.ga_p, self.gb_p):
            self.s(lambda E, g=g: E.sign(t_sg, g[:]),
                   waits=pp_d + [(self.sV, gprev)])
            sm = self.s(lambda E, g=g: E.activation(t_mg, g[:], AF.Abs),
                        inc=True)
            mxv = self.v(lambda E: E.tensor_scalar_max(t_mg, t_mg, 0.25),
                         waits=[(sS, sm)], inc=True)
            gprev = self.v(lambda E, g=g: E.tensor_mul(g[:], t_sg, t_mg),
                           waits=[(sV, mxv)], inc=True)
        self.mod_v = self.sV.v
        self.MODV = MODV

    def phase_small_consts(self):
        sS, sV, sT = self.sS, self.sV, self.sT
        ltr = self.sbuf("ltr", [1, 4], F32)
        crlr = self.sbuf("crlr", [1, 4], F32)
        b2r = self.sbuf("b2r", [1, 4], F32)
        bbr = self.sbuf("bbr", [1, 16], F32)
        rscr = self.sbuf("rscr", [1, 1], F32)
        lv = self.dma("sync", ltr[:], self.ltrow_h[:, :])
        lv = lv + self.dma("sync", crlr[:], self.crlrow_h[:, :])
        lv = lv + self.dma("sync", b2r[:], self.b2row_h[:, :])
        lv = lv + self.dma("sync", rscr[:], self.rsc_h[:, :])
        lv = lv + self.dma("sync", bbr[:], self.bbrow_h[:, :])
        sr = self.smallrow
        t_ab, t_ex, t_ln, t_re = (sr[0:1, 0:4], sr[0:1, 4:8], sr[0:1, 8:12],
                                  sr[0:1, 12:16])
        t_sp, t_it, t_cs, t_it4 = (sr[0:1, 16:20], sr[0:1, 20:24], sr[0:1, 24:28],
                                   sr[0:1, 28:44])
        e1 = self.s(lambda E: E.activation(t_ab, ltr[:], AF.Abs),
                    waits=lv + [(sV, self.init_v)], inc=True)
        e2 = self.s(lambda E: E.activation(t_ex, t_ab, AF.Exp, scale=-1.0),
                    waits=[(sS, e1)], inc=True)
        e3 = self.s(lambda E: E.activation(t_ln, t_ex, AF.Ln,
                                           bias=self.kcon[0:1, 2:3]),
                    waits=[(sS, e2)], inc=True)
        self.s(lambda E: E.activation(t_re, ltr[:], AF.Relu), inc=True)
        e4 = self.v(lambda E: E.tensor_add(t_sp, t_re, t_ln),
                    waits=[(sS, self.sS.v)], inc=True)
        e5 = self.v(lambda E: E.tensor_scalar_add(t_sp, t_sp, 1e-4),
                    waits=[(sV, e4)], inc=True)
        e6 = self.v(lambda E: E.reciprocal(t_it, t_sp), waits=[(sV, e5)],
                    inc=True)
        self.v(lambda E: E.tensor_copy(
            t_it4.rearrange("o (h c) -> o h c", h=4),
            t_it.rearrange("o (h c) -> o h c", h=4).to_broadcast([1, 4, 4])),
            waits=[(sV, e6)], inc=True)
        e7 = self.s(lambda E: E.activation(t_cs, crlr[:], AF.Sigmoid),
                    waits=list(lv), inc=True)
        self.s(lambda E: E.mul(t_cs, t_cs, 0.5), waits=[(sS, e7)], inc=True)
        self.v(lambda E: E.tensor_add(
            bbr[:].rearrange("o (h c) -> o h c", h=4),
            bbr[:].rearrange("o (h c) -> o h c", h=4),
            b2r[:].rearrange("o (h c) -> o h c", h=1).to_broadcast([1, 4, 4])),
            waits=list(lv), inc=True)
        for (src, dstp, n) in ((t_it4, self.invtemp[:, 0:16], 16),
                               (bbr[0:1, :], self.bbrep[:, 0:16], 16),
                               (t_cs, self.crlrep[:, 0:4], 4),
                               (rscr[0:1, :], None, 1)):
            lt = self.t(lambda E, src=src, n=n: E.matmul(
                self.pw(3)[:, 0:n], self.onerow_f[:, :], src, start=True,
                stop=True),
                waits=[(sV, self.sV.v), (sS, self.sS.v)] + self.c_sync,
                inc=True)
            if dstp is None:
                self.s(lambda E: E.copy(self.rsc8[:, 0:1], self.pw(3)[0:NE, 0:1]),
                       waits=[(sT, lt)], inc=True)
            else:
                self.s(lambda E, dstp=dstp, n=n: E.copy(dstp, self.pw(3)[:, 0:n]),
                       waits=[(sT, lt)], inc=True)
        self.maybe_dbg("sc", lambda: self.invtemp[:, 0:16])
        self.maybe_dbg("sc2", lambda: self.bbrep[:, 0:16])
        self.maybe_dbg("sc3", lambda: self.smallrow[0:1, 0:44])

    def ln_build(self, xb, hf, modoff, s_pp, sqbuf, ready):
        sT, sV, sS = self.sT, self.sV, self.sS
        r4 = self.r4scr
        r_sx, r_sx2 = r4[0:1, 1024:1536], r4[0:1, 1536:2048]
        r_mu, r_mur = r4[0:1, 2048:2560], r4[0:1, 2560:3072]
        r_r = r4[0:1, 1536:2048]
        r_t = r4[0:1, 1024:1536]
        cv = self.dma("sync", self.carow[0:1, :],
                      self.relay_h[self.MODV + modoff:self.MODV + modoff + D],
                      waits=list(ready) + [(sT, self.sT.v)])
        ca1v = self.v(lambda E: E.tensor_scalar_add(
            self.carow[0:1, :], self.carow[0:1, :], 1.0),
            waits=cv, inc=True)
        sq_done = 0
        for c in range(DCH):
            sq_done = self.s(lambda E, d=sqbuf(c), s=xb[:, c * SEG:(c + 1) * SEG]:
                             E.activation(d, s, AF.Square),
                             waits=list(ready) + self.scratch_busy, inc=True)
        mm_done = 0
        for (movf, bank) in ((lambda c: xb[:, c * SEG:(c + 1) * SEG], 0),
                             (sqbuf, 1)):
            for c in range(DCH):
                mm_done = self.t(lambda E, m=movf(c), bank=bank, c=c: E.matmul(
                    self.pw(bank)[0:1, 0:SEG], self.onescol[:, :], m,
                    start=(c == 0), stop=(c == DCH - 1)),
                    waits=[(sS, sq_done), (sV, self.init_v)] + list(ready) +
                          self.brw(0, 1), inc=(c == DCH - 1))
        self.s(lambda E: E.copy(r_sx, self.pw(0)[0:1, 0:SEG]),
               waits=[(sT, mm_done)] + self.r4_busy)
        ln_sums = self.s(lambda E: E.copy(r_sx2, self.pw(1)[0:1, 0:SEG]), inc=True)
        self.brs(0, (sS, ln_sums))
        self.brs(1, (sS, ln_sums))
        self.v(lambda E: E.tensor_scalar_mul(r_mu, r_sx, 1.0 / D),
               waits=[(sS, ln_sums)])
        self.v(lambda E: E.tensor_mul(r_t, r_mu, r_mu), waits=self.beta_busy)
        vv = self.v(lambda E: E.scalar_tensor_tensor(
            r_t, r_sx2, 1.0 / D, r_t, ALU.mult, ALU.subtract), inc=True)
        sv = self.s(lambda E: E.activation(r_t, r_t, AF.Sqrt, bias=self.kcon[0:1, 0:1]),
                    waits=[(sV, vv)], inc=True)
        self.v(lambda E: E.reciprocal(r_r, r_t), waits=[(sS, sv)])
        ln_rows = self.v(lambda E: E.tensor_mul(r_mur, r_mu, r_r), inc=True)
        h_done = 0
        for c in range(DCH):
            ca1 = self.carow[0:1, c * 128:(c + 1) * 128]
            hc = hf(c)
            xc = xb[:, c * SEG:(c + 1) * SEG]
            spp = s_pp[:, c:c + 1]
            self.t(lambda E, ca1=ca1: E.matmul(
                self.pw(2)[:, 0:SEG], ca1, r_r, start=True, stop=True),
                waits=[(sV, ln_rows), (sV, ca1v), (sV, self.mod_v)] +
                      self.brw(2, 3))
            bc_t = self.t(lambda E, ca1=ca1: E.matmul(
                self.pw(3)[:, 0:SEG], ca1, r_mur, start=True, stop=True), inc=True)
            self.v(lambda E, hc=hc, xc=xc: E.tensor_mul(
                hc, xc, self.pw(2)[:, 0:SEG]), waits=[(sT, bc_t)])
            h_done = self.v(lambda E, hc=hc, spp=spp: E.scalar_tensor_tensor(
                hc, hc, spp, self.pw(3)[:, 0:SEG], ALU.add, ALU.subtract),
                inc=True)
            self.t(lambda E: None, waits=[(sV, h_done)])
        self.brs(2, (sV, h_done))
        self.brs(3, (sV, h_done))
        return h_done

    def projection(self, w_h, hsrc, h_done, dst, dw, doff, cfo, prebuf, tl,
                   slot_i, porot, prev, dstg=()):
        sT, sV, sS, sG = self.sT, self.sV, self.sS, self.sG
        slot = self.wslot[slot_i]
        wv = self.dma("gpsimd", slot[:].rearrange("p (c m) -> p c m", c=DCH),
                      w_h[:, :].rearrange("(c p) m -> p c m", p=128),
                      waits=[(sT, self.slot_free[slot_i])])
        pre_free = [(sV, prev["conv_v"])] if prev else []
        evs = []
        for m in range(DCH):
            bank = m % 2
            mm = 0
            for kk in range(DCH):
                stat = slot[:, kk * 1024 + m * 128:kk * 1024 + (m + 1) * 128]
                mov = hsrc[:, kk * SEG:(kk + 1) * SEG]
                mm = self.t(lambda E, s=stat, m2=mov, bank=bank, kk=kk: E.matmul(
                    self.pw(bank)[:, 0:SEG], s, m2,
                    start=(kk == 0), stop=(kk == DCH - 1)),
                    waits=wv + [(sV, h_done)] +
                          ([(sS, evs[m - 2])] if m >= 2
                           else self.brw(bank)),
                    inc=(kk == DCH - 1))
            dstp = prebuf[:, m * PREW + PADC:(m + 1) * PREW]
            ev = self.s(lambda E, d=dstp, bank=bank: E.copy(
                d, self.pw(bank)[:, 0:SEG]),
                waits=[(sT, mm)] + pre_free + self.scratch_busy, inc=True)
            evs.append(ev)
            self.brs(bank, (sS, ev))
        self.slot_free[slot_i] = self.sT.v
        ev_all = evs[-1]
        prer = prebuf[:].rearrange("p (c w) -> p c w", c=DCH)
        tlr = tl[:].rearrange("p (c w) -> p c w", c=DCH)
        self.v(lambda E, a=prer[:, :, 0:PADC], b=tlr[:, :, 0:PADC]:
               E.tensor_copy(a, b), waits=[(sS, ev_all)])
        tails_v = self.v(lambda E, a=tlr[:, :, 0:PADC],
                         b=prer[:, :, PREW - PADC:PREW]:
                         E.tensor_copy(a, b), inc=True)
        conv_v = 0
        comb_g = prev["comb_g"] if prev else 0
        silu_s = prev["silu_s"] if prev else 0
        for c in range(DCH):
            rc = c % 2
            f = [self.convf[:, cfo + c * 4 + j:cfo + c * 4 + j + 1]
                 for j in range(4)]
            s0 = prebuf[:, c * PREW + PADC:(c + 1) * PREW]
            s2 = prebuf[:, c * PREW + PADC - 2:(c + 1) * PREW - 2]
            s0x = prebuf[:, c * PREW + PADC - 2:(c + 1) * PREW]
            s2x = prebuf[:, c * PREW + PADC - 4:(c + 1) * PREW - 2]
            pec = dst[:, c * dw + doff:c * dw + doff + SEG]
            pox = porot[:, rc * POW + 2:rc * POW + 4 + SEG]
            posh = porot[:, rc * POW + 3:rc * POW + 3 + SEG]
            wt = [(sS, evs[c]), (sV, tails_v), (sS, silu_s),
                  (sG, comb_g)] + list(dstg)
            self.v(lambda E, pec=pec, s0=s0, f0=f[0]:
                   E.tensor_scalar_mul(pec, s0, f0), waits=wt)
            self.v(lambda E, pec=pec, s2=s2, f2=f[2]:
                   E.scalar_tensor_tensor(pec, s2, f2, pec, ALU.mult, ALU.add))
            self.v(lambda E, pox=pox, s0x=s0x, f1=f[1]:
                   E.tensor_scalar_mul(pox, s0x, f1))
            conv_v = self.v(lambda E, pox=pox, s2x=s2x, f3=f[3]:
                            E.scalar_tensor_tensor(pox, s2x, f3, pox,
                                                   ALU.mult, ALU.add), inc=True)
            comb_g = self.g(lambda E, pec=pec, posh=posh:
                            E.tensor_tensor(pec, pec, posh, ALU.add),
                            waits=[(sV, conv_v)], inc=True)
            silu_s = self.s(lambda E, d=pec: E.activation(
                d, d, AF.Silu), waits=[(sG, comb_g)], inc=True)
        return dict(conv_v=conv_v, comb_g=comb_g, silu_s=silu_s, ev_all=ev_all)

    def segment(self, tglob, own, glob_first):
        sT, sV, sS, sG = self.sT, self.sV, self.sS, self.sG
        self.relay_off = self.relay_base + (tglob // SEG % 2) * 32768
        ctx = ExitStack()
        tag = f"_{tglob}_{int(own)}"
        sb = lambda n, s, dt=BF16: self.sbuf(n + tag, s, dt, ctx)

        q = sb("q", [128, DCH * SEG]) if own else None
        k = sb("k", [128, DCH * SEG])
        vv = sb("vv", [128, DCH * VW])
        v_tm = sb("v_tm", [128, H * NCH * 256])
        d_tm = sb("d_tm", [128, H * NCH * 256]) if own else None
        Pg = sb("Pg", [128, DCH * SEG]) if own else None
        fs_tm = sb("fs_tm", [128, H * NCH * 256]) if own else None
        fl_tm = sb("fl_tm", [128, H * NCH * 256]) if own else None
        ktmb = sb("ktmb", [128, 2 * H * 256])
        beta_tm = sb("beta_tm", [128, H * NCH], F32)
        nbeta_tm = sb("nbeta_tm", [128, H * NCH], F32)
        rq_tm = sb("rq_tm", [128, H * NCH], F32) if own else None
        powb = [sb(f"powb{h}", [128, 9 * 128]) for h in range(H)]
        uwA = [sb(f"uwA{h}", [128, 512]) for h in range(H)]
        rhs = [sb(f"rhs{h}", [128, 512]) for h in range(H)]
        ut = [sb(f"ut{h}", [128, 256]) for h in range(H)]
        wfm = [sb(f"wfm{h}", [128, 256]) for h in range(H)]
        r4 = self.r4scr
        beta = r4[0:4, 0:512]
        krow = r4[0:1, 1024:3072]

        subA = ExitStack()
        pre = self.sbuf("pre" + tag, [128, DCH * PREW], BF16, subA)
        porot = self.sbuf("porot" + tag, [128, 2 * POW], BF16, subA)

        def sq0(c):
            return pre[:, c * PREW + PADC:c * PREW + PADC + SEG]

        xb = self.sbuf("xb" + tag, [128, DCH * SEG], BF16, subA)
        h_ = self.sbuf("h" + tag, [128, DCH * SEG], BF16, subA)

        xv = self.dma("gpsimd", xb[:].rearrange("p (c t) -> p c t", c=DCH),
                      self.xT[:, tglob:tglob + SEG]
                      .rearrange("(c p) t -> p c t", p=128),
                      waits=self.subB_busy + self.stats_busy)

        h_done = self.ln_build(xb, lambda c: h_[:, c * SEG:(c + 1) * SEG],
                               D if own else D, self.sa_p, sq0, xv)
        self.maybe_dbg("h", lambda: h_[:, 0:DBG_SHAPE[1]])
        if STAGE < 2:
            self.barrier(); subA.close(); ctx.close(); return

        rq_res = None
        if own:
            rq_res = self.projection(self.wq_h, h_, h_done, q, SEG, 0, 0,
                                     pre, self.pret[0], 0, porot, None,
                                     dstg=self.drv_busy)
        rk_res = self.projection(self.wk_h, h_, h_done, k, SEG, 0, DCH * 4,
                                 pre, self.pret[1], 1, porot, rq_res,
                                 dstg=[(sT, self.ktm_busy[0])])
        rv_res = self.projection(self.wv_h, h_, h_done, vv, VW, 64, 2 * DCH * 4,
                                 pre, self.pret[2], 0, porot, rk_res,
                                 dstg=self.stats_busy)
        svv = rv_res["silu_s"]
        self.maybe_dbg("v", lambda: vv[:, 64:64 + DBG_SHAPE[1]])

        if (not own) and tglob == TOWN - SEG:
            slot = self.wslot[0]
            wv = self.dma("gpsimd", slot[:].rearrange("p (c m) -> p c m", c=DCH),
                          self.wq_h[:, :].rearrange("(c p) m -> p c m", p=128),
                          waits=[(sT, self.slot_free[0])])
            for m in range(DCH):
                mm = 0
                for kk in range(DCH):
                    stat = slot[:, kk * 1024 + m * 128:kk * 1024 + (m + 1) * 128]
                    mov = h_[:, kk * SEG + SEG - PADC:(kk + 1) * SEG]
                    mm = self.t(lambda E, s=stat, m2=mov, m3=m, kk=kk: E.matmul(
                        self.pw(m3 % 2)[:, 0:PADC], s, m2,
                        start=(kk == 0), stop=(kk == DCH - 1)),
                        waits=wv + [(sV, h_done), (sS, self.sS.v)],
                        inc=(kk == DCH - 1))
                self.s(lambda E, m=m: E.copy(
                    self.pret[0][:, m * PADC:(m + 1) * PADC],
                    self.pw(m % 2)[:, 0:PADC]), waits=[(sT, mm)], inc=True)
            self.slot_free[0] = self.sT.v

        bmm = 0
        for c in range(DCH):
            stat = self.wbW[:, c * H:(c + 1) * H]
            mov = h_[:, c * SEG:(c + 1) * SEG]
            bmm = self.t(lambda E, s=stat, m=mov, c=c: E.matmul(
                self.pw(2)[0:4, 0:SEG], s, m,
                start=(c == 0), stop=(c == DCH - 1)),
                waits=[(sV, h_done)] + self.c_g + self.brw(2),
                inc=(c == DCH - 1))
        bs = self.s(lambda E, b=beta: E.activation(b, self.pw(2)[0:4, 0:SEG],
                                                   AF.Sigmoid),
                    waits=[(sT, bmm)] + self.r4_busy, inc=True)
        self.brs(2, (sS, bs))
        bv = 0
        if not own:
            bv = self.v(lambda E, b=beta: E.tensor_scalar_mul(
                b, b, self.pfl[0:4, 0:1]),
                waits=[(sS, bs)] + self.c_sync, inc=True)
        bt_d = []
        for hh in range(H):
            bt_d = bt_d + self.relay(beta[hh:hh + 1, :],
                              beta_tm[:, hh * NCH:(hh + 1) * NCH], SEG,
                              waits=[(sS, bs), (sV, bv)] + self.btm_busy)
        beta_v = self.v(lambda E: E.tensor_scalar_mul(
            nbeta_tm[:], beta_tm[:], -1.0), waits=bt_d, inc=True)

        def l2rows(src_slices, dst_rows, ready):
            sqd = 0
            mmv = 0
            for cc in range(DCH):
                sqd = self.s(lambda E, d=sq0(cc), s=src_slices(cc):
                             E.activation(d, s, AF.Square),
                             waits=list(ready) + [(sT, mmv),
                                                  (sV, rv_res["conv_v"])],
                             inc=True)
                hh = cc // 2
                mmv = self.t(lambda E, cc=cc, hh=hh, d=sq0(cc): E.matmul(
                    self.pw(3)[0:4, 0:SEG], self.e4[:, hh * 4:(hh + 1) * 4], d,
                    start=(cc == 0), stop=(cc == DCH - 1)),
                    waits=[(sS, sqd)] + self.brw(3),
                    inc=(cc == DCH - 1))
            sv2 = self.s(lambda E, d=dst_rows: E.activation(
                d, self.pw(3)[0:4, 0:SEG], AF.Sqrt,
                bias=self.kcon[0:4, 1:2]),
                waits=[(sT, mmv)] + self.r4_busy, inc=True)
            self.brs(3, (sS, sv2))
            rdone = self.v(lambda E, d=dst_rows: E.reciprocal(d, d),
                           waits=[(sS, sv2)], inc=True)
            return rdone

        kr = r4[0:4, 0:512]
        qr = r4[0:4, 512:1024]
        kr_done = l2rows(lambda cc: k[:, cc * SEG:(cc + 1) * SEG], kr,
                         [(sS, rk_res["silu_s"]), (sV, self.sV.v)] + bt_d)
        off = self.relay_off
        self.relay_off += 2048
        wv2 = self.dma("scalar", self.relay_h[off:off + 2048], kr,
                       waits=[(sV, kr_done)])
        kr_read = wv2
        kb_d = self.dma("scalar", krow, self.relay_h[off:off + 2048],
                        waits=wv2 + self.r4_busy)
        kscaled_v = 0
        for hh in range(H):
            mov = krow[0:1, hh * SEG:(hh + 1) * SEG]
            bct = self.t(lambda E, m=mov: E.matmul(
                self.pw(2)[:, 0:SEG], self.onerow_f[:, :], m,
                start=True, stop=True),
                waits=kb_d + self.brw(2), inc=True)
            for c2 in range(2):
                cc = 2 * hh + c2
                kc = k[:, cc * SEG:(cc + 1) * SEG]
                kscaled_v = self.v(lambda E, kc=kc: E.tensor_mul(
                    kc, kc, self.pw(2)[:, 0:SEG]), waits=[(sT, bct)], inc=True)
            self.brs(2, (sV, kscaled_v))
            self.t(lambda E: None, waits=[(sV, kscaled_v)])
        ktoks = {}

        def ktm_flip(ch2, extra):
            tok = []
            for hh in range(H):
                for c2 in range(2):
                    col = ((ch2 % 2) * H + hh) * 256 + c2 * 128
                    tok = tok + self.dmaT(
                        ktmb[:, col:col + 128],
                        k[:, (2 * hh + c2) * SEG + ch2 * 128:
                          (2 * hh + c2) * SEG + (ch2 + 1) * 128],
                        waits=[(sV, kscaled_v)] + extra, q="scalar")
            ktoks[ch2] = tok

        for ch2 in range(min(2, NCH)):
            ktm_flip(ch2, [(sT, self.ktm_busy[0])])
        rq_d = []
        if own:
            qr_done = l2rows(lambda cc: q[:, cc * SEG:(cc + 1) * SEG], qr,
                             [(sS, rq_res["silu_s"]), (sV, kscaled_v)])
            for hh in range(H):
                rq_d = rq_d + self.relay(qr[hh:hh + 1, :],
                                  rq_tm[:, hh * NCH:(hh + 1) * NCH], SEG,
                                  waits=[(sV, qr_done)])
        self.maybe_dbg("kn", lambda: k[:, 0:DBG_SHAPE[1]])

        tm_d = []
        for hh in range(H):
            for c2 in range(2):
                cc = 2 * hh + c2
                for tc in range(NCH):
                    col = (hh * NCH + tc) * 256 + c2 * 128
                    tm_d = tm_d + self.dmaT(
                        v_tm[:, col:col + 128],
                        vv[:, cc * VW + 64 + tc * 128:cc * VW + 64 + (tc + 1) * 128],
                        waits=[(sS, svv)] + self.vtm_busy)

        gateP_done = 0
        wo_ready = 0
        if own:
            slot = self.wslot[1]
            wv = self.dma("gpsimd", slot[:].rearrange("p (c m) -> p c m", c=DCH),
                          self.w1a_h[:, :].rearrange("(c p) m -> p c m", p=128),
                          waits=[(sT, self.slot_free[1])])
            evs = []
            for m in range(DCH):
                mm = 0
                for kk in range(DCH):
                    stat = slot[:, kk * 1024 + m * 128:kk * 1024 + (m + 1) * 128]
                    mov = h_[:, kk * SEG:(kk + 1) * SEG]
                    mm = self.t(lambda E, s=stat, m2=mov, m3=m, kk=kk: E.matmul(
                        self.pw(m3 % 2)[:, 0:SEG], s, m2,
                        start=(kk == 0), stop=(kk == DCH - 1)),
                        waits=wv +
                              ([(sS, evs[m - 2])] if m >= 2
                               else self.brw(m % 2)),
                        inc=(kk == DCH - 1))
                ev = self.s(lambda E, m=m, d=Pg[:, m * SEG:(m + 1) * SEG]:
                            E.copy(d, self.pw(m % 2)[:, 0:SEG]),
                            waits=[(sT, mm)] + self.drv_busy, inc=True)
                evs.append(ev)
                self.brs(m % 2, (sS, ev))
            self.slot_free[1] = self.sT.v
            gateP_done = evs[-1]
            wov = self.dma("gpsimd",
                           self.wslot[0][:].rearrange("p (c m) -> p c m", c=DCH),
                           self.wo_h[:, :].rearrange("(c p) m -> p c m", p=128),
                           waits=[(sT, self.slot_free[0])])
            wsr = self.wslot[0][:].rearrange("p (c m) -> p c m", c=DCH)
            self.v(lambda E, a=wsr[:, 0::2, :]: E.tensor_scalar_mul(
                a, a, self.rmsw[:, 0:1]), waits=wov + self.c_sync)
            wo_ready = self.v(lambda E, a=wsr[:, 1::2, :]: E.tensor_scalar_mul(
                a, a, self.rmsw[:, 1:2]), inc=True)

        if STAGE < 3:
            self.barrier(); subA.close(); ctx.close(); return
        self.subB_busy = [(sT, self.sT.v), (sS, self.sS.v)]

        if STAGE < 4:
            self.barrier(); subA.close(); ctx.close(); return
        for ch in range(NCH):
            gfirst = glob_first and ch == 0
            kf = [[k[:, (2 * hh + c2) * SEG + ch * 128:
                     (2 * hh + c2) * SEG + (ch + 1) * 128]
                   for c2 in range(2)] for hh in range(H)]
            if own:
                qf = [[q[:, (2 * hh + c2) * SEG + ch * 128:
                         (2 * hh + c2) * SEG + (ch + 1) * 128]
                       for c2 in range(2)] for hh in range(H)]
            vtm = [v_tm[:, (hh * NCH + ch) * 256:(hh * NCH + ch + 1) * 256]
                   for hh in range(H)]
            ktm = [ktmb[:, ((ch % 2) * H + hh) * 256:
                        ((ch % 2) * H + hh + 1) * 256] for hh in range(H)]
            bpp = [beta_tm[:, hh * NCH + ch:hh * NCH + ch + 1] for hh in range(H)]

            def ce(hh):
                return (self.s, sS) if hh % 2 == 0 else (self.v, sV)

            np_v = []
            for hh in range(H):
                bA = hh
                w0 = [(sV, kscaled_v), (sV, beta_v)] + self.brw(bA)
                self.t(lambda E, a=kf[hh][0], bA=bA: E.matmul(
                    self.pw(bA, 128)[:, :], a, a, start=True, stop=False),
                    waits=w0)
                gr = self.t(lambda E, a=kf[hh][1], bA=bA: E.matmul(
                    self.pw(bA, 128)[:, :], a, a, start=False, stop=True),
                    inc=True)
                npp0 = powb[hh][:, 7 * 128:8 * 128]
                np_v.append(self.v(
                    lambda E, d=npp0, bA=bA, b=bpp[hh]:
                    E.scalar_tensor_tensor(d, self.pw(bA, 128)[:, :], b,
                                           self.neglow[:], ALU.mult, ALU.mult),
                    waits=[(sT, gr)], inc=True))
                self.brs(bA, (sV, np_v[hh]))
            if DSTAGE < 2:
                continue
            mp_s = []
            for hh in range(H):
                bB = 4 + hh
                eng, esem = ce(hh)
                pb = self.pw(bB, 128).bitcast(BF16)[:, 0:128]
                tt = self.t(lambda E, s=powb[hh][:, 7 * 128:8 * 128], pb=pb:
                            E.transpose(pb, s, self.iden[:]),
                            waits=[(sV, np_v[hh])] + self.brw(bB),
                            inc=True)
                mp_s.append(eng(lambda E, d=powb[hh][:, 0:128], pb=pb:
                                _cp(E, d, pb), waits=[(sT, tt)],
                                inc=True))
                self.brs(bB, (esem, mp_s[hh]))
            if DSTAGE < 3:
                continue
            ch_s = [[(ce(h)[1], mp_s[h])] for h in range(H)]
            for j in range(1, 7):
                for hh in range(H):
                    bA = hh
                    eng, esem = ce(hh)
                    n0 = powb[hh][:, (7 + (j - 1) % 2) * 128:
                                  (8 + (j - 1) % 2) * 128]
                    m0 = powb[hh][:, (j - 1) * 128:j * 128]
                    w0 = ch_s[hh] + ([(sV, np_v[hh])] if j == 1 else [])
                    self.t(lambda E, n0=n0, m0=m0, bA=bA: E.matmul(
                        self.pw(bA, 128)[:, :], m0, n0, start=True,
                        stop=False), waits=w0)
                    tt = self.t(lambda E, n0=n0, m0=m0, bA=bA: E.matmul(
                        self.pw(bA, 128, off=128)[:, :], n0, m0,
                        start=False, stop=True), inc=True)
                    eng(lambda E, d=powb[hh][:, (7 + j % 2) * 128:
                                             (8 + j % 2) * 128], bA=bA:
                        _cp(E, d, self.pw(bA, 128)[:, :]),
                        waits=[(sT, tt)])
                    cs2 = eng(
                        lambda E, d=powb[hh][:, j * 128:(j + 1) * 128],
                        bA=bA: _cp(E, d, self.pw(bA, 128, off=128)[:, :]),
                        inc=True)
                    ch_s[hh] = [(esem, cs2)]
            if DSTAGE < 4:
                continue
            sol_s = []
            for hh in range(H):
                bB = 4 + hh
                eng, esem = ce(hh)
                self.v(lambda E, d=rhs[hh][:, 0:256], s=vtm[hh], b=bpp[hh]:
                       E.tensor_scalar_mul(d, s, b),
                       waits=tm_d + ktoks[ch] + [(sV, beta_v),
                              (sT, self.rhs_busy[hh])])
                rhs_v = self.v(lambda E, d=rhs[hh][:, 256:512], s=ktm[hh],
                               b=bpp[hh]: E.tensor_scalar_mul(d, s, b),
                               inc=True)
                self.t(lambda E, bB=bB, r=rhs[hh]: E.matmul(
                    self.pw(bB)[:, :], self.iden[:], r[:], start=True,
                    stop=False, skip_group_check=True),
                    waits=[(sV, rhs_v), (esem, mp_s[hh])] + ch_s[hh])
                cur = rhs[hh][:]
                ev = []
                tt = 0
                for fj in range(6, -1, -1):
                    stat = powb[hh][:, fj * 128:(fj + 1) * 128]
                    tt = self.t(lambda E, s=stat, cur=cur, bB=bB, fj=fj:
                                E.matmul(self.pw(bB)[:, :], s, cur,
                                         start=False, stop=(fj == 0),
                                         skip_group_check=True),
                                waits=ev, inc=True)
                    if fj > 0:
                        nxt = (uwA[hh] if fj % 2 == 0 else rhs[hh])[:]
                        ev = [(esem, eng(lambda E, d=nxt, bB=bB: _cp(
                            E, d, self.pw(bB)[:, :]),
                            waits=[(sT, tt)], inc=True))]
                        cur = nxt
                self.rhs_busy[hh] = tt
                eng(lambda E, d=uwA[hh][:, 0:256], bB=bB: _cp(
                    E, d, self.pw(bB, 256)[:, :]), waits=[(sT, tt)])
                if hh % 2 == 0:
                    sv = self.s(lambda E, d=uwA[hh][:, 256:512], bB=bB:
                                E.activation(d, self.pw(bB, 256, off=256)[:, :],
                                             AF.Identity, scale=-1.0),
                                inc=True)
                else:
                    sv = self.v(lambda E, d=uwA[hh][:, 256:512], bB=bB:
                                E.tensor_scalar_mul(
                                    d, self.pw(bB, 256, off=256)[:, :], -1.0),
                                inc=True)
                sol_s.append((esem, sv))
            if DSTAGE < 5:
                continue
            wfm_d = []
            for hh in range(H):
                dd = []
                for c2 in range(2):
                    dd = dd + self.dmaT(
                        wfm[hh][:, c2 * 128:(c2 + 1) * 128],
                        uwA[hh][:, 256 + c2 * 128:256 + (c2 + 1) * 128],
                        waits=[sol_s[hh]], q="scalar")
                wfm_d.append(dd)
            for hh in range(H):
                bA = hh
                bB = 4 + hh
                eng, esem = ce(hh)
                if not gfirst:
                    self.t(lambda E, bA=bA, w0=wfm[hh][:, 0:128],
                           s0=self.S_sb[hh][:, 0:256]: E.matmul(
                        self.pw(bA, 256)[:, :], w0, s0, start=True, stop=False),
                        waits=wfm_d[hh] + self.scp_tok[hh] + ch_s[hh])
                    self.t(lambda E, bA=bA, w1=wfm[hh][:, 128:256],
                           s1=self.S_sb[hh][:, 256:512]: E.matmul(
                        self.pw(bA, 256)[:, :], w1, s1, start=False,
                        stop=False))
                    utt = self.t(lambda E, bA=bA, u=uwA[hh][:, 0:256]:
                                 E.matmul(self.pw(bA, 256)[:, :],
                                          self.iden[:], u,
                                          start=False, stop=True),
                                 waits=[sol_s[hh]], inc=True)
                else:
                    utt = self.t(lambda E, bA=bA, u=uwA[hh][:, 0:256]:
                                 E.matmul(self.pw(bA, 256)[:, :],
                                          self.iden[:], u,
                                          start=True, stop=True),
                                 waits=[sol_s[hh]] + ch_s[hh], inc=True)
                ut_s = eng(lambda E, d=ut[hh][:], bA=bA: _cp(
                    E, d, self.pw(bA, 256)[:, :]), waits=[(sT, utt)], inc=True)
                self.brs(bA, (esem, ut_s))
                d_s = None
                if own and DSTAGE >= 6:
                    alt_h = powb[hh][:, 8 * 128:9 * 128]
                    self.t(lambda E, a=kf[hh][0], b=qf[hh][0], bB=bB:
                           E.matmul(self.pw(bB, 128)[:, :], a, b,
                                    start=True, stop=False),
                           waits=[sol_s[hh], (sS, rq_res["silu_s"])])
                    at = self.t(lambda E, a=kf[hh][1], b=qf[hh][1], bB=bB:
                                E.matmul(self.pw(bB, 128)[:, :], a, b,
                                         start=False, stop=True), inc=True)
                    alt_v = self.v(lambda E, d=alt_h, bB=bB:
                                   E.tensor_mul(d, self.pw(bB, 128)[:, :],
                                                self.upin[:]),
                                   waits=[(sT, at)], inc=True)
                    if not gfirst:
                        self.t(lambda E, a=qf[hh][0], s0=self.S_sb[hh][:, 0:256],
                               bB=bB: E.matmul(
                            self.pw(bB, 256)[:, :], a, s0,
                            start=True, stop=False),
                            waits=[(sV, alt_v)] + self.scp_tok[hh])
                        self.t(lambda E, a=qf[hh][1], s1=self.S_sb[hh][:, 256:512],
                               bB=bB: E.matmul(
                            self.pw(bB, 256)[:, :], a, s1,
                            start=False, stop=False))
                        ot = self.t(lambda E, a=alt_h, u=ut[hh][:],
                                    bB=bB: E.matmul(
                            self.pw(bB, 256)[:, :], a, u,
                            start=False, stop=True),
                            waits=[(esem, ut_s)], inc=True)
                    else:
                        ot = self.t(lambda E, a=alt_h, u=ut[hh][:],
                                    bB=bB: E.matmul(
                            self.pw(bB, 256)[:, :], a, u,
                            start=True, stop=True),
                            waits=[(sV, alt_v), (esem, ut_s)], inc=True)
                    dcol = d_tm[:, (hh * NCH + ch) * 256:(hh * NCH + ch + 1) * 256]
                    rqs = rq_tm[:, hh * NCH + ch:hh * NCH + ch + 1]
                    if hh % 2 == 0:
                        d_s = (sS, self.s(lambda E, d=dcol, rqs=rqs, bB=bB:
                                          E.activation(d, self.pw(bB, 256)[:, :],
                                                       AF.Identity, scale=rqs),
                                          waits=[(sT, ot)] + rq_d +
                                                self.dtm_busy, inc=True))
                    else:
                        d_s = (sV, self.v(lambda E, d=dcol, rqs=rqs, bB=bB:
                                          E.tensor_scalar_mul(
                                              d, self.pw(bB, 256)[:, :], rqs),
                                          waits=[(sT, ot)] + rq_d +
                                                self.dtm_busy, inc=True))
                if DSTAGE < 7:
                    continue
                sw = [(esem, ut_s)] + ktoks[ch] + \
                     ([d_s] if d_s is not None else [(sV, alt_v)]
                      if (own and DSTAGE >= 6) else [sol_s[hh]])
                if not gfirst:
                    self.t(lambda E, bB=bB, s0=self.S_sb[hh][:, 0:256]:
                           E.matmul(self.pw(bB, 256)[:, :], self.iden[:], s0,
                                    start=True, stop=False),
                           waits=sw + self.scp_tok[hh])
                    self.t(lambda E, s=ktm[hh][:, 0:128], u=ut[hh][:], bB=bB:
                           E.matmul(self.pw(bB, 256)[:, :], s, u,
                                    start=False, stop=False))
                    self.t(lambda E, bB=bB, s1=self.S_sb[hh][:, 256:512]:
                           E.matmul(self.pw(bB, 256, off=256)[:, :],
                                    self.iden[:], s1, start=False, stop=False))
                    st2 = self.t(lambda E, s=ktm[hh][:, 128:256], u=ut[hh][:],
                                 bB=bB: E.matmul(
                        self.pw(bB, 256, off=256)[:, :], s, u,
                        start=False, stop=True), inc=True)
                else:
                    self.t(lambda E, s=ktm[hh][:, 0:128], u=ut[hh][:], bB=bB:
                           E.matmul(self.pw(bB, 256)[:, :], s, u,
                                    start=True, stop=False), waits=sw)
                    st2 = self.t(lambda E, s=ktm[hh][:, 128:256], u=ut[hh][:],
                                 bB=bB: E.matmul(
                        self.pw(bB, 256, off=256)[:, :], s, u,
                        start=False, stop=True), inc=True)
                sc2 = eng(lambda E, hh=hh, bB=bB: _cp(
                    E, self.S_sb[hh][:], self.pw(bB)[:, :]),
                    waits=[(sT, st2)], inc=True)
                self.scp_tok[hh] = [(esem, sc2)]
                self.brs(bB, (esem, sc2))
            if ch + 2 < NCH:
                ktm_flip(ch + 2, [(sT, self.sT.v)])
        self.ktm_busy[0] = self.sT.v
        self.btm_busy = [(sV, self.sV.v)]
        if HBAR & 1:
            self.barrier()

        if own:
            self.maybe_dbg("dtm", lambda: d_tm[:, 0:DBG_SHAPE[1]])
        if STAGE < 5 and own:
            self.barrier(); subA.close(); ctx.close(); return
        if not own:
            self.v(lambda E,
                   a=self.vtail[:].rearrange("p (c w) -> p c w", c=DCH),
                   b=vv[:].rearrange("p (c w) -> p c w", c=DCH)[:, :, SEG:VW]:
                   E.tensor_copy(a, b), waits=[(sS, svv)], inc=True)
            self.r4_busy = [(sT, self.sT.v)] + bt_d + kr_read
            self.beta_busy = bt_d
            self.scratch_busy = [(sT, self.sT.v)]
            self.btm_busy = [(sV, self.sV.v)]
            subA.close()
            ctx.close()
            return

        subC = ExitStack()
        fl = self.sbuf("fl" + tag, [128, DCH * SEG], BF16, subC)
        fs = self.sbuf("fs" + tag, [128, DCH * SEG], BF16, subC)
        d_fm = xb
        drv = q[0:36, 0:2048]
        mu2b = q[0:4, 2048:4096]
        stsb = drv

        halo_v = 0
        for c in range(DCH):
            halo_v = self.v(lambda E, a=vv[:, c * VW:c * VW + 64],
                            b=self.vtail[:, c * 64:(c + 1) * 64]:
                            E.tensor_copy(a, b), waits=[(sS, svv)], inc=True)
        fir_g = rv_res["comb_g"]
        fir_v = 0
        for c in range(DCH):
            rc = c % 2
            pox = porot[:, rc * POW + 2:rc * POW + 4 + SEG]
            posh = porot[:, rc * POW + 3:rc * POW + 3 + SEG]
            flc = fl[:, c * SEG:(c + 1) * SEG]
            fsc = fs[:, c * SEG:(c + 1) * SEG]
            wt = [(sS, svv), (sG, fir_g), (sV, halo_v)] + tm_d + \
                 self.fir_guard
            first = True
            for j in range(0, 64, 2):
                fj = self.firlf[:, c * 64 + j:c * 64 + j + 1]
                s = vv[:, c * VW + 64 - j:c * VW + 64 - j + SEG]
                if first:
                    self.v(lambda E, d=flc, s=s, fj=fj:
                           E.tensor_scalar_mul(d, s, fj), waits=wt)
                    first = False
                else:
                    self.v(lambda E, d=flc, s=s, fj=fj:
                           E.scalar_tensor_tensor(d, s, fj, d,
                                                  ALU.mult, ALU.add))
            first = True
            for j in range(1, 64, 2):
                fj = self.firlf[:, c * 64 + j:c * 64 + j + 1]
                s = vv[:, c * VW + 62 - (j - 1):c * VW + 64 - (j - 1) + SEG]
                if first:
                    self.v(lambda E, d=pox, s=s, fj=fj:
                           E.tensor_scalar_mul(d, s, fj), waits=wt)
                    first = False
                else:
                    self.v(lambda E, d=pox, s=s, fj=fj:
                           E.scalar_tensor_tensor(d, s, fj, d,
                                                  ALU.mult, ALU.add))
            flv = self.v(lambda E: None, inc=True)
            flg = self.g(lambda E, d=flc, posh=posh:
                         E.tensor_tensor(d, d, posh, ALU.add),
                         waits=[(sV, flv)], inc=True)
            self.v(lambda E, d=fsc, s=vv[:, c * VW + 64:c * VW + 64 + SEG],
                   fj=self.firsf[:, c * 5:c * 5 + 1]:
                   E.tensor_scalar_mul(d, s, fj), waits=[(sG, flg)])
            for j in (2, 4):
                self.v(lambda E, d=fsc,
                       s=vv[:, c * VW + 64 - j:c * VW + 64 - j + SEG],
                       fj=self.firsf[:, c * 5 + j:c * 5 + j + 1]:
                       E.scalar_tensor_tensor(d, s, fj, d, ALU.mult, ALU.add))
            self.v(lambda E, d=pox, s=vv[:, c * VW + 62:c * VW + 64 + SEG],
                   fj=self.firsf[:, c * 5 + 1:c * 5 + 2]:
                   E.tensor_scalar_mul(d, s, fj))
            fsv = self.v(lambda E, d=pox,
                         s=vv[:, c * VW + 60:c * VW + 62 + SEG],
                         fj=self.firsf[:, c * 5 + 3:c * 5 + 4]:
                         E.scalar_tensor_tensor(d, s, fj, d,
                                                ALU.mult, ALU.add), inc=True)
            fir_g = self.g(lambda E, d=fsc, posh=posh:
                           E.tensor_tensor(d, d, posh, ALU.add),
                           waits=[(sV, fsv)], inc=True)
            fir_v = fsv
        vtl = self.v(lambda E,
                     a=self.vtail[:].rearrange("p (c w) -> p c w", c=DCH),
                     b=vv[:].rearrange("p (c w) -> p c w", c=DCH)[:, :, SEG:VW]:
                     E.tensor_copy(a, b),
                     waits=[(sS, svv), (sG, fir_g)], inc=True)
        if HBAR & 2:
            self.barrier()
        self.maybe_dbg("fl", lambda: fl[:, 0:DBG_SHAPE[1]])

        if OSTAGE < 2:
            self.barrier(); subC.close(); subA.close(); ctx.close(); return
        post_tm = []
        for hh in range(H):
            for c2 in range(2):
                cc = 2 * hh + c2
                for tc in range(NCH):
                    col = (hh * NCH + tc) * 256 + c2 * 128
                    fcol = cc * SEG + tc * 128
                    post_tm = post_tm + self.dmaT(
                        fs_tm[:, col:col + 128], fs[:, fcol:fcol + 128],
                        waits=[(sG, fir_g)])
                    post_tm = post_tm + self.dmaT(
                        fl_tm[:, col:col + 128], fl[:, fcol:fcol + 128])
                    post_tm = post_tm + self.dmaT(
                        d_fm[:, fcol:fcol + 128], d_tm[:, col:col + 128],
                        waits=[(sS, self.sS.v), (sV, self.sV.v),
                               (sT, self.sT.v)])

        if OSTAGE < 3:
            self.barrier(); subC.close(); subA.close(); ctx.close(); return
        streams = [lambda cc: fs[:, cc * SEG:(cc + 1) * SEG],
                   lambda cc: fl[:, cc * SEG:(cc + 1) * SEG],
                   lambda cc: d_fm[:, cc * SEG:(cc + 1) * SEG],
                   lambda cc: vv[:, cc * VW + 64:cc * VW + 64 + SEG]]
        stat_w = [(sG, fir_g), (sV, fir_v), (sS, svv)] + post_tm
        stP = self.PW[0:36, :]
        first_in_bank = [True] * H
        for st_i in range(4):
            stf = streams[st_i]
            for hh in range(H):
                for c2 in range(2):
                    cc = 2 * hh + c2
                    fib = first_in_bank[hh]
                    first_in_bank[hh] = False
                    self.t(lambda E, st_i=st_i, m=stf(cc), hh=hh, fib=fib:
                           E.matmul(stP[:, hh * SEG:(hh + 1) * SEG],
                                    self.estat[:, st_i * 36:st_i * 36 + 36], m,
                                    start=fib, stop=False),
                           waits=stat_w + (self.brw(hh) if fib else []))
        sq_t = 0
        ab_t = 0
        for st_i in range(4):
            stf = streams[st_i]
            for hh in range(H):
                for c2 in range(2):
                    cc = 2 * hh + c2
                    rc = cc % 2
                    sqd = self.s(lambda E, d=sq0(cc), m=stf(cc):
                                 E.activation(d, m, AF.Square),
                                 waits=stat_w + [(sT, sq_t)], inc=True)
                    sq_t = self.t(lambda E, st_i=st_i, hh=hh, d=sq0(cc):
                                  E.matmul(stP[:, hh * SEG:(hh + 1) * SEG],
                                           self.estat[:, (8 + st_i) * 36:
                                                      (8 + st_i) * 36 + 36],
                                           d, start=False, stop=False),
                                  waits=[(sS, sqd)], inc=True)
                    ab = porot[:, rc * POW + 2:rc * POW + 2 + SEG]
                    abd = self.v(lambda E, d=ab, m=stf(cc):
                                 E.scalar_tensor_tensor(d, m, -1.0, m,
                                                        ALU.mult, ALU.max),
                                 waits=stat_w + [(sT, ab_t)], inc=True)
                    last = (st_i == 3 and c2 == 1)
                    ab_t = self.t(lambda E, st_i=st_i, hh=hh, d=ab, last=last:
                                  E.matmul(stP[:, hh * SEG:(hh + 1) * SEG],
                                           self.estat[:, (4 + st_i) * 36:
                                                      (4 + st_i) * 36 + 36],
                                           d, start=False, stop=last),
                                  waits=[(sV, abd)], inc=True)
        stat_t = self.sT.v
        self.scratch_busy = [(sT, stat_t)]
        self.stats_busy = [(sT, stat_t), (sS, self.sS.v), (sV, self.sV.v)]
        stv = self.s(lambda E: E.copy(stsb, stP[0:36, 0:4 * SEG]),
                     waits=[(sT, stat_t)], inc=True)
        for b in range(4):
            self.brs(b, (sS, stv))
        drv_v = self.v(lambda E: E.tensor_scalar_mul(
            drv[0:12, 0:2048], drv[0:12, 0:2048], 1.0 / DV),
            waits=[(sS, stv)], inc=True)
        self.s(lambda E: E.activation(mu2b, drv[0:4, 0:2048], AF.Square),
               waits=[(sV, drv_v)])
        drv_s = self.s(lambda E: E.activation(drv[32:36, 0:2048],
                                              drv[32:36, 0:2048], AF.Sqrt),
                       waits=[(sS, stv)], inc=True)
        off_l2 = self.relay_off
        self.relay_off += 8192
        l2w = self.dma("gpsimd", self.relay_h[off_l2:off_l2 + 8192],
                       drv[32:36, 0:2048], waits=[(sS, drv_s)])
        l2_d = self.dma("gpsimd", drv[12:16, 0:2048],
                        self.relay_h[off_l2:off_l2 + 8192]
                        .rearrange("(p t) -> p t", p=4),
                        waits=l2w)
        if HBAR & 4:
            self.barrier()
        self.maybe_dbg("drv", lambda: drv[0:16, 0:DBG_SHAPE[1]])
        if OSTAGE < 4:
            self.barrier(); subC.close(); subA.close(); ctx.close(); return
        subC.close()

        subD = ExitStack()
        h1 = self.sbuf("h1" + tag, [128, DCH * SEG], BF16, subD)
        p_tm = self.sbuf("p_tm" + tag, [128, NCH * 16], F32, subD)
        mx = self.sbuf("mx" + tag, [128, NCH * 4], F32, subD)
        osq = self.sbuf("osq" + tag, [128, 256], BF16, subD)
        osc = self.sbuf("osc" + tag, [128, 256], BF16, subD)
        osum = self.sbuf("osum" + tag, [128, 4], F32, subD)
        lbuf = r4[0:4, 0:2048]

        lg_s = []
        for hh in range(H):
            evs = []
            for m in range(DCH):
                self.t(lambda E, m=m, hh=hh,
                       s=self.w1b[0:16, m * 128:(m + 1) * 128],
                       mo=drv[0:16, hh * SEG:(hh + 1) * SEG]: E.matmul(
                    self.pw(m % 2)[:, 0:SEG], s, mo, start=True, stop=False),
                    waits=[(sV, drv_v), (sS, drv_s), (sS, gateP_done)] +
                          l2_d +
                          ([(sS, evs[m - 2])] if m >= 2
                           else self.brw(m % 2)))
                if not (GSKIP & 2):
                    self.t(lambda E, m=m, hh=hh,
                           s=self.w1bv[0:4, m * 128:(m + 1) * 128],
                           mo=mu2b[0:4, hh * SEG:(hh + 1) * SEG]: E.matmul(
                        self.pw(m % 2)[:, 0:SEG], s, mo, start=False,
                        stop=False))
                mm = self.t(lambda E, m=m, mo=Pg[:, m * SEG:(m + 1) * SEG]:
                            E.matmul(self.pw(m % 2)[:, 0:SEG], self.iden[:],
                                     mo, start=False, stop=True), inc=True)
                ev = self.s(lambda E, m=m, d=h1[:, m * SEG:(m + 1) * SEG]:
                            E.activation(d, self.pw(m % 2)[:, 0:SEG],
                                         AF.Gelu,
                                         bias=self.b1s[:, m:m + 1]),
                            waits=[(sT, mm)] + self.h1_busy, inc=True)
                evs.append(ev)
                self.brs(m % 2, (sS, ev))
            lmm = 0
            for c in range(DCH):
                lmm = self.t(lambda E, c=c, hh=hh,
                             s=self.w2s[:, c * H:(c + 1) * H],
                             mo=h1[:, c * SEG:(c + 1) * SEG]: E.matmul(
                    self.pw(2 + hh % 2)[0:4, 0:SEG], s, mo,
                    start=(c == 0), stop=(c == DCH - 1)),
                    waits=[(sS, evs[-1])] + self.brw(2 + hh % 2),
                    inc=(c == DCH - 1))
            lg = self.s(lambda E, hh=hh,
                        d=lbuf[0:4, hh * SEG:(hh + 1) * SEG]:
                        E.copy(d, self.pw(2 + hh % 2)[0:4, 0:SEG]),
                        waits=[(sT, lmm)] + self.r4_busy + rq_d + kr_read,
                        inc=True)
            lg_s.append(lg)
            self.brs(2 + hh % 2, (sS, lg))
        logits_s = self.sS.v
        self.maybe_dbg("lg", lambda: lbuf[0:4, 0:DBG_SHAPE[1]])
        if OSTAGE < 5:
            self.barrier(); subD.close(); subA.close(); ctx.close(); return
        p_d = []
        for hh in range(H):
            for comp in range(4):
                dst = p_tm[:, :].rearrange("p (t g) -> p t g", g=16)[:, :,
                                                                    hh * 4 + comp]
                p_d = p_d + self.relay(
                    lbuf[comp:comp + 1, hh * SEG:(hh + 1) * SEG],
                    dst, SEG, waits=[(sS, logits_s)] + self.dtm_busy)
        self.maybe_dbg("ptm0", lambda: p_tm[:, 0:DBG_SHAPE[1]])
        v44 = p_tm[:, :].rearrange("p (t h c) -> p t h c", h=4, c=4)
        v16 = p_tm[:, :].rearrange("p (t g) -> p t g", g=16)
        mxv = mx[:, :].rearrange("p (t h) -> p t h", h=4)
        mxb = mx[:, :].rearrange("p (t h o) -> p t h o", h=4, o=1) \
            .to_broadcast([128, NCH, 4, 4])
        self.v(lambda E: E.tensor_add(
            v16, v16, self.bbrep[:, :].rearrange("p (o g) -> p o g", o=1)
            .to_broadcast([128, NCH, 16])), waits=p_d)
        self.v(lambda E: E.tensor_mul(
            v16, v16, self.invtemp[:, :].rearrange("p (o g) -> p o g", o=1)
            .to_broadcast([128, NCH, 16])))
        f1 = self.v(lambda E: E.reduce_max(mxv, v44, axis=AX.X), inc=True)
        sv3 = self.v(lambda E: E.tensor_sub(v44, v44, mxb),
                     waits=[(sV, f1)], inc=True)
        se = self.s(lambda E: E.activation(p_tm[:], p_tm[:], AF.Exp),
                    waits=[(sV, sv3)], inc=True)
        f2 = self.v(lambda E: E.reduce_sum(mxv, v44, axis=AX.X),
                    waits=[(sS, se)], inc=True)
        f3 = self.v(lambda E: E.reciprocal(mx[:], mx[:]),
                    waits=[(sV, f2)], inc=True)
        self.v(lambda E: E.tensor_mul(v44, v44, mxb), waits=[(sV, f3)])
        self.v(lambda E: E.tensor_max(
            v16, v16, self.epsc[:, :].rearrange("p (o g) -> p o g", o=1)
            .to_broadcast([128, NCH, 16])))
        f4 = self.v(lambda E: E.reduce_sum(mxv, v44, axis=AX.X), inc=True)
        f5 = self.v(lambda E: E.reciprocal(mx[:], mx[:]),
                    waits=[(sV, f4)], inc=True)
        self.v(lambda E: E.tensor_mul(v44, v44, mxb), waits=[(sV, f5)])
        p_done = self.v(lambda E: E.tensor_add(
            v44[:, :, :, 0], v44[:, :, :, 0],
            self.crlrep[:, :].rearrange("p (o h) -> p o h", o=1)
            .to_broadcast([128, NCH, 4])), inc=True)
        if HBAR & 8:
            self.barrier()
        self.maybe_dbg("p", lambda: p_tm[:, 0:DBG_SHAPE[1]])

        if OSTAGE < 6:
            self.barrier(); subD.close(); subA.close(); ctx.close(); return
        omn_s = []
        for hh in range(H):
            for tc in range(NCH):
                col = (hh * NCH + tc) * 256
                pj = [p_tm[:, tc * 16 + hh * 4 + c:tc * 16 + hh * 4 + c + 1]
                      for c in range(4)]
                vtile = v_tm[:, col:col + 256]
                self.v(lambda E, s=fs_tm[:, col:col + 256], p=pj[0]:
                       E.tensor_scalar_mul(osc[:], s, p),
                       waits=[(sV, p_done),
                              (sS, self.sS.v)] + post_tm)
                self.v(lambda E, s=fl_tm[:, col:col + 256], p=pj[1]:
                       E.scalar_tensor_tensor(osc[:], s, p, osc[:],
                                              ALU.mult, ALU.add))
                self.v(lambda E, s=d_tm[:, col:col + 256], p=pj[2]:
                       E.scalar_tensor_tensor(osc[:], s, p, osc[:],
                                              ALU.mult, ALU.add))
                ov = self.v(lambda E, vt=vtile, p=pj[3]:
                            E.scalar_tensor_tensor(vt, vt, p, osc[:],
                                                   ALU.mult, ALU.add),
                            inc=True)
                sq1 = self.s(lambda E, vt=vtile: E.activation(
                    osq[:], vt, AF.Square, accum_out=osum[:, 0:1]),
                    waits=[(sV, ov)], inc=True)
                ss2 = self.s(lambda E: E.activation(
                    osum[:, 1:2], osum[:, 0:1], AF.Sqrt, scale=1.0 / DV,
                    bias=self.kcon[:, 0:1]), waits=[(sS, sq1)], inc=True)
                rv2 = self.v(lambda E: E.reciprocal(osum[:, 2:3], osum[:, 1:2]),
                             waits=[(sS, ss2)], inc=True)
                omn_s.append(self.s(lambda E, vt=vtile: E.activation(
                    vt, vt, AF.Identity, scale=osum[:, 2:3]),
                    waits=[(sV, rv2)], inc=True))
        if HBAR & 16:
            self.barrier()
        om_fm = h1
        omfm_d = []
        for hh in range(H):
            for c2 in range(2):
                cc = 2 * hh + c2
                for tc in range(NCH):
                    col = (hh * NCH + tc) * 256 + c2 * 128
                    fcol = cc * SEG + tc * 128
                    omfm_d = omfm_d + self.dmaT(om_fm[:, fcol:fcol + 128],
                                       v_tm[:, col:col + 128],
                                       waits=[(sS, omn_s[hh * NCH + tc])])
        self.maybe_dbg("omfm", lambda: om_fm[:, 0:DBG_SHAPE[1]])

        if OSTAGE < 7:
            self.barrier(); subD.close(); subA.close(); ctx.close(); return
        oo = tglob % TOWN
        for m in range(DCH):
            xv2 = self.dma("sync", self.xf32[m % 2][:],
                           self.xT[m * 128:(m + 1) * 128, tglob:tglob + SEG],
                           waits=[(sV, self.xf_free[m % 2])])
            mm = 0
            for kk in range(DCH):
                stat = self.wslot[0][:, kk * 1024 + m * 128:
                                     kk * 1024 + (m + 1) * 128]
                mov = om_fm[:, kk * SEG:(kk + 1) * SEG]
                mm = self.t(lambda E, s=stat, mo=mov, m3=m, kk=kk: E.matmul(
                    self.pw(m3 % 2)[:, 0:SEG], s, mo,
                    start=(kk == 0), stop=(kk == DCH - 1)),
                    waits=omfm_d + [(sV, wo_ready)] + self.brw(m % 2),
                    inc=(kk == DCH - 1))
            sv4 = self.v(lambda E, m=m: E.scalar_tensor_tensor(
                self.xmo[m % 2][:], self.pw(m % 2)[:, 0:SEG],
                self.ga_p[:, m:m + 1], self.xf32[m % 2][:], ALU.mult, ALU.add),
                waits=[(sT, mm)] + xv2 + self.xmo_free[m % 2],
                inc=True)
            self.xf_free[m % 2] = sv4
            self.brs(m % 2, (sV, sv4))
            self.xmo_free[m % 2] = self.dma(
                "sync", self.xmid_h[m * 128:(m + 1) * 128, oo:oo + SEG],
                self.xmo[m % 2][:], waits=[(sV, sv4)])
        self.slot_free[0] = self.sT.v
        self.drv_busy = [(sT, self.sT.v)]
        self.h1_busy = [(sT, self.sT.v)]
        self.r4_busy = [(sT, self.sT.v)] + p_d + rq_d + kr_read + bt_d
        self.beta_busy = bt_d + [(sV, bv)] if not own else bt_d
        self.dtm_busy = [(sV, self.sV.v)] + post_tm
        self.vtm_busy = omfm_d
        self.fir_guard = self.stats_busy + post_tm
        subD.close()
        subA.close()
        ctx.close()

    def ffn(self):
        sT, sV, sS, sG = self.sT, self.sV, self.sS, self.sG
        self.relay_off = self.relay_base + 2 * 32768
        ctx = ExitStack()
        sb = lambda n, s, dt=BF16: self.sbuf(n + "_f", s, dt, ctx)
        h2 = sb("h2", [128, DCH * TOWN])
        gsc = sb("gsc", [NE, TOWN], F32)
        gfb = sb("gfb", [NE, TOWN])
        growb = sb("growb", [1, SEG], F32)
        g_tm = sb("g_tm", [128, 16 * NE], F32)
        hs = sb("hs", [128, (HF // 128) * SEG])
        h1e = sb("h1e", [128, NE * (HF // 128) * SEG])
        wsl = [sb(f"wsl{i}", [128, (HF // 128) * 1024]) for i in range(2)]
        gb = sb("gbc", [128, SEG])

        self.gro_free = 0
        sub = ExitStack()
        xb2 = self.sbuf("xb2_f", [128, DCH * SEG], BF16, sub)
        sq2 = self.sbuf("sq2_f", [128, DCH * SEG], BF16, sub)
        for ti in range(4):
            xv = self.dma("gpsimd", xb2[:].rearrange("p (c t) -> p c t", c=DCH),
                          self.xmid_h[:, ti * SEG:(ti + 1) * SEG]
                          .rearrange("(c p) t -> p c t", p=128),
                          waits=self.snap())
            self.ln_build(
                xb2,
                lambda c, ti=ti: h2[:, c * TOWN + ti * SEG:
                                    c * TOWN + (ti + 1) * SEG],
                4 * D, self.sbp, lambda c: sq2[:, c * SEG:(c + 1) * SEG],
                xv)
            self.barrier()
        sub.close()
        h2c = lambda c, ti: h2[:, c * TOWN + ti * SEG:c * TOWN + (ti + 1) * SEG]
        self.maybe_dbg("h2", lambda: h2[:, 0:DBG_SHAPE[1]])

        gmm = 0
        for ti in range(4):
            for c in range(DCH):
                gmm = self.t(lambda E, c=c, ti=ti,
                             s=self.gwW[:, c * NE:(c + 1) * NE],
                             m=h2c(c, ti): E.matmul(
                    self.pw(2)[0:NE, 0:SEG], s, m,
                    start=(c == 0), stop=(c == DCH - 1)),
                    waits=[(sG, self.sG.v), (sV, self.sV.v),
                           (sS, self.sS.v)],
                    inc=(c == DCH - 1))
            gse = self.s(lambda E, ti=ti, d=gsc[0:NE, ti * SEG:(ti + 1) * SEG]:
                         E.activation(d, self.pw(2)[0:NE, 0:SEG], AF.Sigmoid),
                         waits=[(sT, gmm)], inc=True)
        gsv = self.v(lambda E: E.tensor_scalar_mul(gsc[:], gsc[:],
                                                   self.rsc8[:, 0:1]),
                     waits=[(sS, self.sS.v)], inc=True)
        gt_d = []
        for e in range(NE):
            dst = g_tm[:, :].rearrange("p (t e) -> p t e", e=NE)[:, :, e]
            gt_d = gt_d + self.relay(gsc[e:e + 1, :], dst, TOWN,
                                     waits=[(sV, gsv)])
        gv = g_tm[:, :].rearrange("p (t e) -> p t e", e=NE)
        m1b = sb("m1b", [128, 16], F32)
        m2b = sb("m2b", [128, 16], F32)
        isx = sb("isx", [128, 16 * NE], F32)
        iv = isx[:, :].rearrange("p (t e) -> p t e", e=NE)
        m1v = m1b[:, :].rearrange("p (t o) -> p t o", o=1)
        m2v = m2b[:, :].rearrange("p (t o) -> p t o", o=1)
        g1 = self.v(lambda E: E.reduce_max(m1v, gv, axis=AX.X),
                    waits=gt_d, inc=True)
        self.v(lambda E: E.tensor_tensor(iv, gv, m1v.to_broadcast([128, 16, NE]),
                                         ALU.is_ge), waits=[(sV, g1)])
        g2 = self.v(lambda E: E.scalar_tensor_tensor(iv, iv, -1e30, gv,
                                                     ALU.mult, ALU.add),
                    inc=True)
        g3 = self.v(lambda E: E.reduce_max(m2v, iv, axis=AX.X),
                    waits=[(sV, g2)], inc=True)
        self.v(lambda E: E.tensor_tensor(iv, gv, m2v.to_broadcast([128, 16, NE]),
                                         ALU.is_ge), waits=[(sV, g3)])
        self.v(lambda E: E.tensor_mul(iv, iv, gv))
        g4 = self.v(lambda E: E.tensor_add(m1b[:], m1b[:], m2b[:]), inc=True)
        g5 = self.v(lambda E: E.reciprocal(m1b[:], m1b[:]),
                    waits=[(sV, g4)], inc=True)
        gtv = self.v(lambda E: E.tensor_mul(
            iv, iv, m1v.to_broadcast([128, 16, NE])), waits=[(sV, g5)],
            inc=True)
        gf_d = []
        gscr = []
        for e in range(NE):
            off = self.relay_off
            self.relay_off += 2048
            scr = self.relay_h[off:off + 2048]
            wv2 = self.dma("scalar", scr,
                           isx[:, :].rearrange("p (t e) -> p t e", e=NE)[:, :, e],
                           waits=[(sV, gtv)], slow=True)
            gd = self.dma("gpsimd", gfb[e:e + 1, :],
                          scr.rearrange("(p t) -> t p", t=16),
                          waits=wv2, slow=True)
            gf_d = gf_d + wv2 + gd
            gscr.append(scr)
        self.maybe_dbg("gf", lambda: gsc[0:NE, 0:DBG_SHAPE[1]])

        HB = HF // 128
        wv_pend = {}
        for ti in range(4):
            for e in range(NE):
                si = e % 2
                if (ti, e) not in wv_pend:
                    pass
                wv = self.dma("gpsimd",
                              wsl[si][:].rearrange("p (c m) -> p c m", c=DCH),
                              self.w1e_h[e, :, :]
                              .rearrange("(c p) m -> p c m", p=128),
                              waits=[(sT, self.wsl_free.get(si, 0))])
                for m in range(HB):
                    mm = 0
                    for kk in range(DCH):
                        stat = wsl[si][:, kk * HF + m * 128:kk * HF + (m + 1) * 128]
                        mov = h2c(kk, ti)
                        mm = self.t(lambda E, s=stat, mo=mov, m3=m, kk=kk:
                                    E.matmul(self.pw(m3 % 2)[:, 0:SEG], s, mo,
                                             start=(kk == 0),
                                             stop=(kk == DCH - 1)),
                                    waits=wv + [(sS, self.sS.v),
                                                (sV, self.sV.v)],
                                    inc=(kk == DCH - 1))
                    dcol = h1e[:, (e * HB + m) * SEG:(e * HB + m + 1) * SEG]
                    self.s(lambda E, d=dcol, m3=m, b=self.b1es[:, e * HB + m:
                                                              e * HB + m + 1]:
                           E.activation(d, self.pw(m3 % 2)[:, 0:SEG], AF.Gelu,
                                        bias=b), waits=[(sT, mm)], inc=True)
                self.wsl_free[si] = self.sT.v
                grv = self.dma(
                    "sync", growb[0:1, :],
                    gscr[e].rearrange("(p t) -> t p", t=16)[4 * ti:4 * ti + 4, :],
                    waits=gf_d + [(sT, self.gro_free)], slow=True)
                bct = self.t(lambda E, m=growb[0:1, :]:
                             E.matmul(self.pw(2)[:, 0:SEG], self.onerow_f[:, :],
                                      m, start=True, stop=True),
                             waits=grv + [(sS, self.sS.v)], inc=True)
                self.gro_free = bct
                gbe = self.s(lambda E: E.copy(gb[:], self.pw(2)[:, 0:SEG]),
                             waits=[(sT, bct)], inc=True)
                gw_v = 0
                for m in range(HB):
                    dcol = h1e[:, (e * HB + m) * SEG:(e * HB + m + 1) * SEG]
                    gw_v = self.v(lambda E, d=dcol: E.tensor_mul(d, d, gb[:]),
                                  waits=[(sS, gbe)], inc=True)
                self.t(lambda E: None, waits=[(sV, gw_v)])
            for (w_h, dst, bias, nb) in ((self.ws1_h, hs, self.bs1s, HB),):
                si = 0
                wv = self.dma("gpsimd",
                              wsl[si][:, 0:DCH * HF]
                              .rearrange("p (c m) -> p c m", c=DCH),
                              w_h[:, :].rearrange("(c p) m -> p c m", p=128),
                              waits=[(sT, self.wsl_free.get(si, 0))])
                for m in range(nb):
                    mm = 0
                    for kk in range(DCH):
                        stat = wsl[si][:, kk * HF + m * 128:kk * HF + (m + 1) * 128]
                        mov = h2c(kk, ti)
                        mm = self.t(lambda E, s=stat, mo=mov, m3=m, kk=kk:
                                    E.matmul(self.pw(m3 % 2)[:, 0:SEG], s, mo,
                                             start=(kk == 0),
                                             stop=(kk == DCH - 1)),
                                    waits=wv + [(sS, self.sS.v)],
                                    inc=(kk == DCH - 1))
                    dcol = dst[:, m * SEG:(m + 1) * SEG]
                    self.s(lambda E, d=dcol, m3=m, b=bias[:, m:m + 1]:
                           E.activation(d, self.pw(m3 % 2)[:, 0:SEG], AF.Gelu,
                                        bias=b), waits=[(sT, mm)], inc=True)
                self.wsl_free[si] = self.sT.v
            h1_done = self.sS.v
            gwv_done = self.sV.v
            for mg in range(2):
                ymm = [0] * 4
                for src_i in range(NE + 1):
                    si = src_i % 2
                    if src_i < NE:
                        wv = self.dma("gpsimd",
                                      wsl[si][:, 0:2048]
                                      .rearrange("p (c m) -> p c m", c=HB),
                                      self.w2e_h[src_i, :,
                                                 mg * 512:(mg + 1) * 512]
                                      .rearrange("(c p) m -> p c m", p=128),
                                      waits=[(sT, self.wsl_free.get(si, 0))])
                        hsrc = lambda kk, src_i=src_i: h1e[
                            :, (src_i * HB + kk) * SEG:(src_i * HB + kk + 1) * SEG]
                    else:
                        wv = self.dma("gpsimd",
                                      wsl[si][:, 0:2048]
                                      .rearrange("p (c m) -> p c m", c=HB),
                                      self.ws2_h[:, mg * 512:(mg + 1) * 512]
                                      .rearrange("(c p) m -> p c m", p=128),
                                      waits=[(sT, self.wsl_free.get(si, 0))])
                        hsrc = lambda kk: hs[:, kk * SEG:(kk + 1) * SEG]
                    for m in range(4):
                        mm = 0
                        for kk in range(HB):
                            stat = wsl[si][:, kk * 512 + m * 128:
                                           kk * 512 + (m + 1) * 128]
                            mov = hsrc(kk)
                            mm = self.t(lambda E, s=stat, mo=mov, m3=m,
                                        first=(src_i == 0 and kk == 0):
                                        E.matmul(self.pw(m3)[:, 0:SEG], s, mo,
                                                 start=first, stop=False),
                                        waits=wv + [(sS, h1_done),
                                               (sV, gwv_done),
                                               (sS, self.sS.v),
                                               (sV, self.sV.v)],
                                        inc=(kk == HB - 1))
                        ymm[m] = mm
                    self.wsl_free[si] = self.sT.v
                for m in range(4):
                    mc = mg * 4 + m
                    bmm = self.t(lambda E, m=m, mc=mc,
                                 s=self.b2e_s[0:NE, mc * 128:(mc + 1) * 128],
                                 mo=gfb[0:NE, ti * SEG:(ti + 1) * SEG]:
                                 E.matmul(self.pw(m)[:, 0:SEG], s, mo,
                                          start=False, stop=True),
                                 waits=gf_d + self.c_g, inc=True)
                    xv2 = self.dma("sync", self.xf32[m % 2][:],
                                   self.xmid_h[mc * 128:(mc + 1) * 128,
                                               ti * SEG:(ti + 1) * SEG],
                                   waits=[(sV, self.xf_free[m % 2])])
                    sv5 = self.s(lambda E, m=m, mc=mc: E.activation(
                        self.pw(m, SEG)[:, :], self.pw(m, SEG)[:, :],
                        AF.Identity,
                        bias=self.bs2p[:, mc:mc + 1]),
                        waits=[(sT, bmm)] + self.bs2_tok, inc=True)
                    sv4 = self.v(lambda E, m=m, mc=mc: E.scalar_tensor_tensor(
                        self.xmo[m % 2][:], self.pw(m, SEG)[:, :],
                        self.gb_p[:, mc:mc + 1], self.xf32[m % 2][:],
                        ALU.mult, ALU.add),
                        waits=[(sS, sv5)] + xv2 +
                              self.xmo_free[m % 2], inc=True)
                    self.xf_free[m % 2] = sv4
                    self.xmo_free[m % 2] = self.dma(
                        "sync", self.out_h[mc * 128:(mc + 1) * 128,
                                           ti * SEG:(ti + 1) * SEG],
                        self.xmo[m % 2][:], waits=[(sV, sv4)])
            self.barrier()
        ctx.close()

    def build(self):
        self.declare_io()
        self.alloc_persistent()
        self.load_consts()
        self.bs2_tok = self.dma(
            "sync", self.bs2p[:],
            self.bs2row_h[:].rearrange("(c p) -> p c", p=128), slow=True)
        self.relay_off = 8192
        self.relay_base = 8192 + 32768
        self.phase_mod()
        self.phase_small_consts()
        self.scp_tok = [[] for _ in range(H)]
        self.bankr = {b: [] for b in range(8)}
        self.r4_busy = []
        self.beta_busy = []
        self.drv_busy = []
        self.scratch_busy = []
        self.subB_busy = []
        self.dtm_busy = []
        self.vtm_busy = []
        self.stats_busy = []
        self.h1_busy = []
        self.btm_busy = []
        self.fir_guard = []
        self.ktm_busy = [0, 0]
        self.rhs_busy = [0] * H
        self.xf_free = {0: 0, 1: 0}
        self.xmo_free = {0: [], 1: []}
        self.slot_free = [0, 0]
        self.wsl_free = {}
        for i in range(N_PRE):
            self.segment(i * SEG, own=False, glob_first=(i == 0))
        gt = 0
        for tl in [self.vtail] + self.pret:
            gt = self.v(lambda E, tl=tl: E.tensor_scalar_mul(
                tl[:], tl[:], self.pfl[:, 0:1]), waits=self.snap(), inc=True)
        self.barrier()
        for i in range(N_OWN):
            self.segment(TOWN + i * SEG, own=True,
                         glob_first=(N_PRE == 0 and i == 0))
        if RUN_FFN:
            self.ffn()
        fin = self.snap()
        for eng in Prog.ENGINES:
            self.P.op(eng, lambda E: None, waits=fin)
        self.emit()
        return self.nc

    def emit(self):
        P, nc = self.P, self.nc
        with nc.Block() as blk:
            @blk.tensor
            def _(t):
                for th in P.q["tensor"]:
                    th(t)

            @blk.vector
            def _(v):
                for th in P.q["vector"]:
                    th(v)

            @blk.scalar
            def _(s):
                for th in P.q["scalar"]:
                    th(s)

            @blk.gpsimd
            def _(g):
                for th in P.q["gpsimd"]:
                    th(g)

            @blk.sync
            def _(sy):
                for th in P.q["sync"]:
                    th(sy)


def _host_inputs(inputs, b, half):
    i = inputs
    xb_ = np.asarray(i["L0_attn_x"], dtype=np.float32)[b]
    if half == 0:
        xT = np.ascontiguousarray(
            np.concatenate([xb_[0:TOWN], xb_[0:TOWN]], 0).T)
    else:
        xT = np.ascontiguousarray(xb_.T)

    def fm(a, w):
        return np.ascontiguousarray(
            a.reshape(DCH, 128, w).transpose(1, 0, 2).reshape(128, DCH * w))

    refk = [0, 2, 1, 3]
    w1 = np.asarray(i["L0_attn_W1"], dtype=np.float32)
    w1b = np.stack([w1[1024 + (r % 4) * 4 + refk[r // 4]] for r in range(16)])
    w1bv = np.stack([w1[1024 + st * 4 + 1] for st in range(4)])
    d = {
        "xT": xT,
        "pflag": np.full((128, 1), float(half), np.float32),
        "wq": i["L0_attn_Wq"], "wk": i["L0_attn_Wk"], "wv": i["L0_attn_Wv"],
        "wb": i["L0_attn_Wb"],
        "convf": np.concatenate(
            [fm(np.asarray(i[f"L0_attn_conv_{p}"],
                           dtype=np.float32)[:, ::-1], 4)
             for p in "qkv"], axis=1),
        "firl": fm(np.asarray(i["L0_attn_fir_long"],
                              dtype=np.float32).reshape(1024, 64)[:, ::-1], 64),
        "firs": fm(np.asarray(i["L0_attn_fir_short"],
                              dtype=np.float32).reshape(1024, 5)[:, ::-1], 5),
        "w1a": w1[:1024], "w1b": w1b, "w1bv": w1bv,
        "b1": np.asarray(i["L0_attn_b1"], dtype=np.float32).reshape(DCH, 128).T,
        "w2": i["L0_attn_W2"],
        "b2row": np.asarray(i["L0_attn_b2"], dtype=np.float32).reshape(1, 4),
        "bbrow": np.asarray(i["L0_attn_base_bias"],
                            dtype=np.float32).reshape(1, 16),
        "ltrow": np.asarray(i["L0_attn_log_temp"],
                            dtype=np.float32).reshape(1, 4),
        "crlrow": np.asarray(i["L0_attn_conv_residual_logit"],
                             dtype=np.float32).reshape(1, 4),
        "rmsw": np.asarray(i["L0_attn_rms_w"],
                           dtype=np.float32).reshape(2, 128).T,
        "wo": i["L0_attn_Wo"],
        "gw": i["L0_ffn_gate_W"],
        "w1e": i["L0_ffn_W1"],
        "b1e": np.asarray(i["L0_ffn_b1"], dtype=np.float32)
        .reshape(NE, 4, 128).transpose(2, 0, 1).reshape(128, NE * 4),
        "w2e": i["L0_ffn_W2"],
        "b2e": i["L0_ffn_b2"],
        "ws1": i["L0_ffn_Ws1"],
        "bs1": np.asarray(i["L0_ffn_bs1"], dtype=np.float32).reshape(4, 128).T,
        "ws2": i["L0_ffn_Ws2"],
        "bs2row": np.asarray(i["L0_ffn_bs2"], dtype=np.float32).reshape(1024),
        "rsc": np.asarray(i["L0_ffn_route_scale"],
                          dtype=np.float32).reshape(1, 1),
        "cvec": np.asarray(i["L0_c"], dtype=np.float32)[b].reshape(DCH, 128).T,
        "wmod": i["L0_wmod"],
    }
    return {k: np.ascontiguousarray(np.asarray(v, dtype=np.float32))
            for k, v in d.items()}


def kernel(**inputs):
    bk = BK()
    nc = bk.build()
    in_maps = [_host_inputs(inputs, c // 2, c % 2) for c in range(8)]
    res = run_bass_kernel_spmd(nc, in_maps, core_ids=list(range(8)))
    out = np.zeros((B, L, D), np.float32)
    for c in range(8):
        b, half = c // 2, c % 2
        out[b, half * TOWN:(half + 1) * TOWN, :] = res.results[c]["out"].T
    return out


# version 09
# speedup vs baseline: 1.0097x; 1.0097x over previous
import numpy as np
import ml_dtypes
from contextlib import ExitStack

import concourse.bass as bass
import concourse.mybir as mybir
from concourse.bass_utils import run_bass_kernel_spmd

F32 = mybir.dt.float32
BF16 = mybir.dt.bfloat16
AF = mybir.ActivationFunctionType
ALU = mybir.AluOpType
AX = mybir.AxisListType
BF = ml_dtypes.bfloat16

B, L, D = 4, 4096, 1024
H, DV = 4, 256
NE, HF = 8, 512
CH = 128
SEG = 512
TOWN = 2048
NSEG = TOWN // SEG
NCH = SEG // CH
DCH = D // 128
VW = SEG + 64
PADC = 4
PREW = SEG + PADC
POW = SEG + 4

DBG_POINT = None
DBG_SHAPE = None
STAGE = 9
DSTAGE = 9
OSTAGE = 9
GSKIP = 0
HBAR = 0
N_OWN = NSEG
N_PRE = NSEG
RUN_FFN = True
RACE_CHECK = False


def _cp(E, d, s):
    if hasattr(E, "copy"):
        return E.copy(d, s)
    return E.tensor_copy(d, s)


class Ctr:
    def __init__(self, ctx, nc, name, is_dma=False):
        self.h = ctx.enter_context(nc.semaphore(name))
        self.v = 0
        self.name = name
        self.is_dma = is_dma


class Prog:
    ENGINES = ("tensor", "vector", "scalar", "gpsimd", "sync")

    def __init__(self):
        self.q = {e: [] for e in self.ENGINES}
        self.lastwait = {e: {} for e in self.ENGINES}

    def op(self, eng, fn, waits=(), inc=None, dn=1):
        ws = []
        for (c, v) in waits:
            if v <= 0:
                continue
            if self.lastwait[eng].get(id(c), 0) < v:
                ws.append((c.h, v))
                self.lastwait[eng][id(c)] = v
        tgt = None
        if inc is not None:
            inc.v += dn
            tgt = (inc.h, dn)

        def thunk(E, ws=tuple(ws), tgt=tgt, fn=fn):
            for (h, v) in ws:
                E.wait_ge(h, v)
            ins = fn(E)
            if tgt is not None:
                if ins is None:
                    ins = E.nop()
                ins.then_inc(tgt[0], tgt[1])

        self.q[eng].append(thunk)


class BK:
    def __init__(self):
        self.nc = bass.Bass("TRN2", target_bir_lowering=False,
                            detect_race_conditions=RACE_CHECK)
        self.P = Prog()
        self.ctx = ExitStack()
        nc = self.nc
        self.sT = Ctr(self.ctx, nc, "sT")
        self.sV = Ctr(self.ctx, nc, "sV")
        self.sS = Ctr(self.ctx, nc, "sS")
        self.sG = Ctr(self.ctx, nc, "sG")
        NQ = 6
        self.qsems = {q: [Ctr(self.ctx, nc, f"d{q}{i}", is_dma=True)
                          for i in range(NQ)] for q in ("sync", "scalar",
                                                        "gpsimd")}
        self.qrr = {q: 0 for q in self.qsems}
        self.dbg_done = False

    def t(self, fn, waits=(), inc=False):
        self.P.op("tensor", fn, waits, self.sT if inc else None)
        return self.sT.v

    def v(self, fn, waits=(), inc=False):
        self.P.op("vector", fn, waits, self.sV if inc else None)
        return self.sV.v

    def s(self, fn, waits=(), inc=False):
        self.P.op("scalar", fn, waits, self.sS if inc else None)
        return self.sS.v

    def g(self, fn, waits=(), inc=False):
        self.P.op("gpsimd", fn, waits, self.sG if inc else None)
        return self.sG.v

    def dma(self, q, dst, src, waits=(), slow=False):
        ring = self.qsems[q]
        ctr = ring[self.qrr[q]]
        self.qrr[q] = (self.qrr[q] + 1) % len(ring)
        chain = [(ctr, ctr.v)]
        self.P.op(q, lambda E, d=dst, s=src, slow=slow: E.dma_start(
            out=d, in_=s, allow_slow_non_contiguous=slow),
            list(waits) + chain, ctr, dn=16)
        return [(ctr, ctr.v)]

    def qtok(self, q=None):
        qs = [q] if q else list(self.qsems)
        return [(c, c.v) for qq in qs for c in self.qsems[qq]]

    def dmaT(self, dst, src, waits=(), q="sync"):
        ring = self.qsems[q]
        ctr = ring[self.qrr[q]]
        self.qrr[q] = (self.qrr[q] + 1) % len(ring)
        chain = [(ctr, ctr.v)]
        self.P.op(q,
                  lambda E, d=dst, s=src: E.dma_start(out=d, in_=s,
                                                      transpose=True),
                  list(waits) + chain, ctr, dn=16)
        return [(ctr, ctr.v)]

    def relay(self, src_ap, dst_ap, N, waits=()):
        off = self.relay_off
        self.relay_off += (N + 127) & ~127
        scr = self.relay_h[off:off + N]
        wv = self.dma("gpsimd", scr, src_ap, waits=waits)
        return self.dma("gpsimd", dst_ap,
                        scr.rearrange("(c p) -> p c", p=128),
                        waits=wv, slow=True)

    def sbuf(self, name, shape, dt=BF16, ctx=None):
        return (ctx or self.ctx).enter_context(
            self.nc.sbuf_tensor(name, list(shape), dt))

    def snap(self):
        return [(self.sT, self.sT.v), (self.sV, self.sV.v), (self.sS, self.sS.v),
                (self.sG, self.sG.v)] + self.qtok()

    def barrier(self):
        snap = self.snap()
        for eng in Prog.ENGINES:
            self.P.op(eng, lambda E: None, waits=snap)

    def maybe_dbg(self, point, ap_fn):
        if DBG_POINT != point or self.dbg_done:
            return
        self.dbg_done = True
        ap = ap_fn()
        n0, n1 = DBG_SHAPE
        vv2 = self.v(lambda E, ap=ap: E.tensor_copy(
            self.dbgscr[0:n0, 0:n1], ap), waits=self.snap(), inc=True)
        self.dma("sync", self.dbg_h[:, :], self.dbgscr[0:n0, 0:n1],
                 waits=[(self.sV, vv2)])
        self.barrier()

    def declare_io(self):
        nc = self.nc

        def din(name, shape, dtype=F32):
            return nc.dram_tensor(name, list(shape), dtype, kind="ExternalInput").ap()

        self.xT = din("xT", [D, L])
        self.pflag = din("pflag", [128, 1])
        self.wq_h = din("wq", [D, D]); self.wk_h = din("wk", [D, D])
        self.wv_h = din("wv", [D, D]); self.wb_h = din("wb", [D, H])
        self.convf_h = din("convf", [128, 3 * DCH * 4])
        self.firl_h = din("firl", [128, DCH * 64])
        self.firs_h = din("firs", [128, DCH * 5])
        self.w1a_h = din("w1a", [D, D]); self.w1b_h = din("w1b", [16, D])
        self.w1bv_h = din("w1bv", [4, D])
        self.b1_h = din("b1", [128, DCH]); self.w2_h = din("w2", [D, H])
        self.b2row_h = din("b2row", [1, 4]); self.bbrow_h = din("bbrow", [1, 16])
        self.ltrow_h = din("ltrow", [1, 4]); self.crlrow_h = din("crlrow", [1, 4])
        self.rmsw_h = din("rmsw", [128, 2]); self.wo_h = din("wo", [D, D])
        self.gw_h = din("gw", [D, NE])
        self.w1e_h = din("w1e", [NE, D, HF])
        self.b1e_h = din("b1e", [128, NE * (HF // 128)])
        self.w2e_h = din("w2e", [NE, HF, D])
        self.b2e_h = din("b2e", [NE, D])
        self.ws1_h = din("ws1", [D, HF]); self.bs1_h = din("bs1", [128, HF // 128])
        self.ws2_h = din("ws2", [HF, D]); self.bs2row_h = din("bs2row", [D])
        self.rsc_h = din("rsc", [1, 1])
        self.c_h = din("cvec", [128, DCH])
        self.wmod_h = din("wmod", [D, 6 * D])
        self.out_h = nc.dram_tensor("out", [D, TOWN], F32, kind="ExternalOutput").ap()
        self.relay_h = nc.dram_tensor("relayscr", [1 << 18], F32,
                                      kind="Internal").ap()
        self.relay_off = 0
        self.xmid_h = nc.dram_tensor("xmid", [D, TOWN], F32, kind="Internal").ap()
        self.dbg_h = None
        if DBG_POINT is not None:
            self.dbg_h = nc.dram_tensor("dbg", list(DBG_SHAPE), F32,
                                        kind="ExternalOutput").ap()

        self.iden_c = nc.inline_tensor(np.eye(128, dtype=BF), "iden_c").ap()
        self.neglow_c = nc.inline_tensor(
            (-np.tril(np.ones((128, 128), np.float32), -1)).astype(BF),
            "neglow_c").ap()
        self.upin_c = nc.inline_tensor(
            np.triu(np.ones((128, 128), np.float32)).astype(BF), "upin_c").ap()
        epsp = np.zeros((128, 16), np.float32)
        epsp[:, 0::4] = 0.05
        epsp[:, 1::4] = 0.05
        self.epsp_c = nc.inline_tensor(epsp, "epsp_c").ap()
        est = np.zeros((128, 12 * 36), np.float32)
        for r in range(12):
            est[:, r * 36 + r] = 1.0
            if 8 <= r < 12:
                est[:, r * 36 + 24 + r] = 1.0
        self.estat_c = nc.inline_tensor(est.astype(BF), "estat_c").ap()
        e4 = np.zeros((128, 16), np.float32)
        for r in range(4):
            e4[:, r * 4 + r] = 1.0
        self.e4_c = nc.inline_tensor(e4.astype(BF), "e4_c").ap()

    def alloc_persistent(self):
        sb = self.sbuf
        self.iden = sb("iden", [128, 128])
        self.neglow = sb("neglow", [128, 128])
        self.upin = sb("upin", [128, 128])
        self.estat = sb("estat", [128, 12 * 36])
        self.e4 = sb("e4", [128, 16])
        self.epsc = sb("epsc", [128, 16], F32)
        self.onescol = sb("onescol", [128, 1])
        self.onerow = sb("onerow", [1, 128])
        self.onerow_f = sb("onerow_f", [1, 128], F32)
        self.convf = sb("convf_s", [128, 3 * DCH * 4], F32)
        self.firlf = sb("firlf", [128, DCH * 64], F32)
        self.firsf = sb("firsf", [128, DCH * 5], F32)
        self.w1b = sb("w1b_s", [16, D])
        self.w1bv = sb("w1bv_s", [4, D])
        self.wbW = sb("wbW", [128, DCH * H])
        self.gwW = sb("gwW", [128, DCH * NE])
        self.w2s = sb("w2s", [128, DCH * H])
        self.b1s = sb("b1s", [128, DCH], F32)
        self.b1es = sb("b1es", [128, NE * (HF // 128)], F32)
        self.bs1s = sb("bs1s", [128, HF // 128], F32)
        self.rmsw = sb("rmsw_s", [128, 2], F32)
        self.cvec = sb("cvec_s", [128, DCH])
        self.pfl = sb("pfl", [128, 1], F32)
        self.carow = sb("carow", [1, D], F32)
        self.sa_p = sb("sa_p", [128, DCH], F32)
        self.ga_p = sb("ga_p", [128, DCH], F32)
        self.sbp = sb("sbp", [128, DCH], F32)
        self.gb_p = sb("gb_p", [128, DCH], F32)
        self.S_sb = [sb(f"S_sb{h}", [128, 512]) for h in range(H)]
        self.vtail = sb("vtail", [128, DCH * 64])
        self.pret = [sb(f"pret{i}", [128, DCH * PADC]) for i in range(3)]
        self.wslot = [sb(f"wslot{i}", [128, DCH * 1024]) for i in range(2)]
        self.smallrow = sb("smallrow", [1, 64], F32)
        self.invtemp = sb("invtemp", [128, 16], F32)
        self.bbrep = sb("bbrep", [128, 16], F32)
        self.crlrep = sb("crlrep", [128, 4], F32)
        self.b2e_s = sb("b2e_s", [NE, D])
        self.r4scr = sb("r4scr", [4, 6 * SEG], F32)
        self.silc = sb("silc", [128, DCH])
        self.xf32 = [sb(f"xf32_{i}", [128, SEG], F32) for i in range(2)]
        self.xmo = [sb(f"xmo_{i}", [128, SEG], F32) for i in range(2)]
        self.rsc8 = sb("rsc8", [NE, 1], F32)
        self.kcon = sb("kcon", [128, 4], F32)
        self.bs2p = self.sbuf("bs2p", [128, DCH], F32)

        if DBG_POINT is not None:
            self.dbgscr = sb("dbgscr", [128, 2048], F32)
        self.PW = self.ctx.enter_context(self.nc.psum_tensor("PW", [128, 4096], F32))

    def pw(self, i, n=512, off=0):
        return self.PW[:, i * 512 + off:i * 512 + off + n]

    def brw(self, *bs):
        return [e for b in bs for e in self.bankr[b]]

    def brs(self, b, *ents):
        self.bankr[b] = list(ents)

    def load_consts(self):
        d = self.dma
        d("sync", self.iden[:], self.iden_c[:, :])
        d("sync", self.neglow[:], self.neglow_c[:, :])
        d("sync", self.upin[:], self.upin_c[:, :])
        d("sync", self.estat[:], self.estat_c[:, :])
        d("sync", self.e4[:], self.e4_c[:, :])
        d("sync", self.epsc[:], self.epsp_c[:, :])
        d("sync", self.convf[:], self.convf_h[:, :])
        d("sync", self.firlf[:], self.firl_h[:, :])
        d("sync", self.firsf[:], self.firs_h[:, :])
        d("sync", self.b1s[:], self.b1_h[:, :])
        d("sync", self.b1es[:], self.b1e_h[:, :])
        d("sync", self.bs1s[:], self.bs1_h[:, :])
        d("sync", self.rmsw[:], self.rmsw_h[:, :])
        d("sync", self.pfl[:], self.pflag[:, :])
        d("gpsimd", self.w1b[:], self.w1b_h[:, :])
        d("gpsimd", self.w1bv[:], self.w1bv_h[:, :])
        d("gpsimd", self.cvec[:], self.c_h[:, :])
        d("gpsimd", self.w2s[:].rearrange("p (c h) -> p c h", c=DCH),
          self.w2_h[:, :].rearrange("(c p) h -> p c h", p=128))
        d("gpsimd", self.wbW[:].rearrange("p (c h) -> p c h", c=DCH),
          self.wb_h[:, :].rearrange("(c p) h -> p c h", p=128))
        d("gpsimd", self.gwW[:].rearrange("p (c e) -> p c e", c=DCH),
          self.gw_h[:, :].rearrange("(c p) e -> p c e", p=128))
        d("gpsimd", self.b2e_s[:], self.b2e_h[:, :])
        self.c_sync = self.qtok()
        self.c_g = self.qtok()
        self.v(lambda E: E.memset(self.onescol[:], 1.0))
        self.v(lambda E: E.memset(self.onerow[:], 1.0))
        self.v(lambda E: E.memset(self.onerow_f[:], 1.0))
        self.v(lambda E: E.memset(self.vtail[:], 0.0))
        for i in range(3):
            self.v(lambda E, i=i: E.memset(self.pret[i][:], 0.0))
        self.v(lambda E: E.memset(self.r4scr[:], 0.0))
        self.v(lambda E: E.memset(self.smallrow[:], 0.0))
        self.v(lambda E: E.tensor_scalar_mul(self.w1bv[:], self.w1bv[:], -1.0),
               waits=list(self.c_g))
        self.v(lambda E: E.memset(self.kcon[:, 0:1], 1e-5))
        self.v(lambda E: E.memset(self.kcon[:, 1:2], 1e-6))
        self.init_v = self.v(lambda E: E.memset(self.kcon[:, 2:3], 1.0), inc=True)

    def phase_mod(self):
        sT, sS, sV = self.sT, self.sS, self.sV
        MODV = 8192
        sil_done = self.s(
            lambda E: E.activation(self.silc[:], self.cvec[:], AF.Silu),
            waits=self.c_g + [(sV, self.init_v)], inc=True)
        evacs = []
        wr_d = []
        wr_all = []
        for it in range(12):
            slot = self.wslot[it % 2][:, 0:DCH * 512]
            w = [(sT, evacs[it - 2][0])] if it >= 2 else []
            lv = self.dma("gpsimd", slot.rearrange("p (c t) -> p c t", c=DCH),
                          self.wmod_h[:, it * 512:(it + 1) * 512]
                          .rearrange("(c p) t -> p c t", p=128), waits=w)
            lt = 0
            for k in range(DCH):
                stat = self.silc[:, k:k + 1]
                mov = slot[:, k * 512:(k + 1) * 512]
                dstp = self.pw(it % 2)[0:1, :]
                lt = self.t(lambda E, s=stat, m=mov, d=dstp, k=k: E.matmul(
                    d, s, m, start=(k == 0), stop=(k == DCH - 1)),
                    waits=lv + [(sS, sil_done)] +
                          ([(sS, evacs[it - 2][1])] if it >= 2 else []),
                    inc=(k == DCH - 1))
            ev = self.s(lambda E, srcp=self.pw(it % 2)[0:1, :]: E.copy(
                self.r4scr[0:1, 0:512], srcp),
                waits=[(sT, lt)] + wr_d, inc=True)
            wr_d = self.dma("scalar", self.relay_h[MODV + it * 512:
                                                   MODV + (it + 1) * 512],
                            self.r4scr[0:1, 0:512], waits=[(sS, ev)])
            wr_all += wr_d
            evacs.append((lt, ev))
        pp_d = []
        for (dst, r0) in ((self.sa_p, 0), (self.ga_p, 2), (self.sbp, 3),
                          (self.gb_p, 5)):
            pp_d = pp_d + self.dma("scalar", dst[:],
                            self.relay_h[MODV + r0 * D:MODV + (r0 + 1) * D]
                            .rearrange("(c p) -> p c", p=128),
                            waits=wr_all, slow=True)
        t_sg = self.invtemp[:, 0:DCH]
        t_mg = self.invtemp[:, 8:8 + DCH]
        gprev = 0
        for g in (self.ga_p, self.gb_p):
            self.s(lambda E, g=g: E.sign(t_sg, g[:]),
                   waits=pp_d + [(self.sV, gprev)])
            sm = self.s(lambda E, g=g: E.activation(t_mg, g[:], AF.Abs),
                        inc=True)
            mxv = self.v(lambda E: E.tensor_scalar_max(t_mg, t_mg, 0.25),
                         waits=[(sS, sm)], inc=True)
            gprev = self.v(lambda E, g=g: E.tensor_mul(g[:], t_sg, t_mg),
                           waits=[(sV, mxv)], inc=True)
        self.mod_v = self.sV.v
        self.MODV = MODV

    def phase_small_consts(self):
        sS, sV, sT = self.sS, self.sV, self.sT
        ltr = self.sbuf("ltr", [1, 4], F32)
        crlr = self.sbuf("crlr", [1, 4], F32)
        b2r = self.sbuf("b2r", [1, 4], F32)
        bbr = self.sbuf("bbr", [1, 16], F32)
        rscr = self.sbuf("rscr", [1, 1], F32)
        lv = self.dma("sync", ltr[:], self.ltrow_h[:, :])
        lv = lv + self.dma("sync", crlr[:], self.crlrow_h[:, :])
        lv = lv + self.dma("sync", b2r[:], self.b2row_h[:, :])
        lv = lv + self.dma("sync", rscr[:], self.rsc_h[:, :])
        lv = lv + self.dma("sync", bbr[:], self.bbrow_h[:, :])
        sr = self.smallrow
        t_ab, t_ex, t_ln, t_re = (sr[0:1, 0:4], sr[0:1, 4:8], sr[0:1, 8:12],
                                  sr[0:1, 12:16])
        t_sp, t_it, t_cs, t_it4 = (sr[0:1, 16:20], sr[0:1, 20:24], sr[0:1, 24:28],
                                   sr[0:1, 28:44])
        e1 = self.s(lambda E: E.activation(t_ab, ltr[:], AF.Abs),
                    waits=lv + [(sV, self.init_v)], inc=True)
        e2 = self.s(lambda E: E.activation(t_ex, t_ab, AF.Exp, scale=-1.0),
                    waits=[(sS, e1)], inc=True)
        e3 = self.s(lambda E: E.activation(t_ln, t_ex, AF.Ln,
                                           bias=self.kcon[0:1, 2:3]),
                    waits=[(sS, e2)], inc=True)
        self.s(lambda E: E.activation(t_re, ltr[:], AF.Relu), inc=True)
        e4 = self.v(lambda E: E.tensor_add(t_sp, t_re, t_ln),
                    waits=[(sS, self.sS.v)], inc=True)
        e5 = self.v(lambda E: E.tensor_scalar_add(t_sp, t_sp, 1e-4),
                    waits=[(sV, e4)], inc=True)
        e6 = self.v(lambda E: E.reciprocal(t_it, t_sp), waits=[(sV, e5)],
                    inc=True)
        self.v(lambda E: E.tensor_copy(
            t_it4.rearrange("o (h c) -> o h c", h=4),
            t_it.rearrange("o (h c) -> o h c", h=4).to_broadcast([1, 4, 4])),
            waits=[(sV, e6)], inc=True)
        e7 = self.s(lambda E: E.activation(t_cs, crlr[:], AF.Sigmoid),
                    waits=list(lv), inc=True)
        self.s(lambda E: E.mul(t_cs, t_cs, 0.5), waits=[(sS, e7)], inc=True)
        self.v(lambda E: E.tensor_add(
            bbr[:].rearrange("o (h c) -> o h c", h=4),
            bbr[:].rearrange("o (h c) -> o h c", h=4),
            b2r[:].rearrange("o (h c) -> o h c", h=1).to_broadcast([1, 4, 4])),
            waits=list(lv), inc=True)
        for (src, dstp, n) in ((t_it4, self.invtemp[:, 0:16], 16),
                               (bbr[0:1, :], self.bbrep[:, 0:16], 16),
                               (t_cs, self.crlrep[:, 0:4], 4),
                               (rscr[0:1, :], None, 1)):
            lt = self.t(lambda E, src=src, n=n: E.matmul(
                self.pw(3)[:, 0:n], self.onerow_f[:, :], src, start=True,
                stop=True),
                waits=[(sV, self.sV.v), (sS, self.sS.v)] + self.c_sync,
                inc=True)
            if dstp is None:
                self.s(lambda E: E.copy(self.rsc8[:, 0:1], self.pw(3)[0:NE, 0:1]),
                       waits=[(sT, lt)], inc=True)
            else:
                self.s(lambda E, dstp=dstp, n=n: E.copy(dstp, self.pw(3)[:, 0:n]),
                       waits=[(sT, lt)], inc=True)
        self.maybe_dbg("sc", lambda: self.invtemp[:, 0:16])
        self.maybe_dbg("sc2", lambda: self.bbrep[:, 0:16])
        self.maybe_dbg("sc3", lambda: self.smallrow[0:1, 0:44])

    def ln_build(self, xb, hf, modoff, s_pp, sqbuf, ready):
        sT, sV, sS = self.sT, self.sV, self.sS
        r4 = self.r4scr
        r_sx, r_sx2 = r4[0:1, 1024:1536], r4[0:1, 1536:2048]
        r_mu, r_mur = r4[0:1, 2048:2560], r4[0:1, 2560:3072]
        r_r = r4[0:1, 1536:2048]
        r_t = r4[0:1, 1024:1536]
        cv = self.dma("sync", self.carow[0:1, :],
                      self.relay_h[self.MODV + modoff:self.MODV + modoff + D],
                      waits=list(ready) + [(sT, self.sT.v)])
        ca1v = self.v(lambda E: E.tensor_scalar_add(
            self.carow[0:1, :], self.carow[0:1, :], 1.0),
            waits=cv, inc=True)
        sq_done = 0
        for c in range(DCH):
            sq_done = self.s(lambda E, d=sqbuf(c), s=xb[:, c * SEG:(c + 1) * SEG]:
                             E.activation(d, s, AF.Square),
                             waits=list(ready) + self.scratch_busy, inc=True)
        mm_done = 0
        for (movf, bank) in ((lambda c: xb[:, c * SEG:(c + 1) * SEG], 0),
                             (sqbuf, 1)):
            for c in range(DCH):
                mm_done = self.t(lambda E, m=movf(c), bank=bank, c=c: E.matmul(
                    self.pw(bank)[0:1, 0:SEG], self.onescol[:, :], m,
                    start=(c == 0), stop=(c == DCH - 1)),
                    waits=[(sS, sq_done), (sV, self.init_v)] + list(ready) +
                          self.brw(0, 1), inc=(c == DCH - 1))
        self.s(lambda E: E.copy(r_sx, self.pw(0)[0:1, 0:SEG]),
               waits=[(sT, mm_done)] + self.r4_busy)
        ln_sums = self.s(lambda E: E.copy(r_sx2, self.pw(1)[0:1, 0:SEG]), inc=True)
        self.brs(0, (sS, ln_sums))
        self.brs(1, (sS, ln_sums))
        self.v(lambda E: E.tensor_scalar_mul(r_mu, r_sx, 1.0 / D),
               waits=[(sS, ln_sums)])
        self.v(lambda E: E.tensor_mul(r_t, r_mu, r_mu), waits=self.beta_busy)
        vv = self.v(lambda E: E.scalar_tensor_tensor(
            r_t, r_sx2, 1.0 / D, r_t, ALU.mult, ALU.subtract), inc=True)
        sv = self.s(lambda E: E.activation(r_t, r_t, AF.Sqrt, bias=self.kcon[0:1, 0:1]),
                    waits=[(sV, vv)], inc=True)
        self.v(lambda E: E.reciprocal(r_r, r_t), waits=[(sS, sv)])
        ln_rows = self.v(lambda E: E.tensor_mul(r_mur, r_mu, r_r), inc=True)
        h_done = 0
        for c in range(DCH):
            ca1 = self.carow[0:1, c * 128:(c + 1) * 128]
            hc = hf(c)
            xc = xb[:, c * SEG:(c + 1) * SEG]
            spp = s_pp[:, c:c + 1]
            self.t(lambda E, ca1=ca1: E.matmul(
                self.pw(2)[:, 0:SEG], ca1, r_r, start=True, stop=True),
                waits=[(sV, ln_rows), (sV, ca1v), (sV, self.mod_v)] +
                      self.brw(2, 3))
            bc_t = self.t(lambda E, ca1=ca1: E.matmul(
                self.pw(3)[:, 0:SEG], ca1, r_mur, start=True, stop=True), inc=True)
            self.v(lambda E, hc=hc, xc=xc: E.tensor_mul(
                hc, xc, self.pw(2)[:, 0:SEG]), waits=[(sT, bc_t)])
            h_done = self.v(lambda E, hc=hc, spp=spp: E.scalar_tensor_tensor(
                hc, hc, spp, self.pw(3)[:, 0:SEG], ALU.add, ALU.subtract),
                inc=True)
            self.t(lambda E: None, waits=[(sV, h_done)])
        self.brs(2, (sV, h_done))
        self.brs(3, (sV, h_done))
        return h_done

    def projection(self, w_h, hsrc, h_done, dst, dw, doff, cfo, prebuf, tl,
                   slot_i, porot, prev, dstg=()):
        sT, sV, sS, sG = self.sT, self.sV, self.sS, self.sG
        slot = self.wslot[slot_i]
        wv = self.dma("gpsimd", slot[:].rearrange("p (c m) -> p c m", c=DCH),
                      w_h[:, :].rearrange("(c p) m -> p c m", p=128),
                      waits=[(sT, self.slot_free[slot_i])])
        pre_free = [(sV, prev["conv_v"])] if prev else []
        evs = []
        for m in range(DCH):
            bank = m % 2
            mm = 0
            for kk in range(DCH):
                stat = slot[:, kk * 1024 + m * 128:kk * 1024 + (m + 1) * 128]
                mov = hsrc[:, kk * SEG:(kk + 1) * SEG]
                mm = self.t(lambda E, s=stat, m2=mov, bank=bank, kk=kk: E.matmul(
                    self.pw(bank)[:, 0:SEG], s, m2,
                    start=(kk == 0), stop=(kk == DCH - 1)),
                    waits=wv + [(sV, h_done)] +
                          ([(sS, evs[m - 2])] if m >= 2
                           else self.brw(bank)),
                    inc=(kk == DCH - 1))
            dstp = prebuf[:, m * PREW + PADC:(m + 1) * PREW]
            ev = self.s(lambda E, d=dstp, bank=bank: E.copy(
                d, self.pw(bank)[:, 0:SEG]),
                waits=[(sT, mm)] + pre_free + self.scratch_busy, inc=True)
            evs.append(ev)
            self.brs(bank, (sS, ev))
        self.slot_free[slot_i] = self.sT.v
        ev_all = evs[-1]
        prer = prebuf[:].rearrange("p (c w) -> p c w", c=DCH)
        tlr = tl[:].rearrange("p (c w) -> p c w", c=DCH)
        self.v(lambda E, a=prer[:, :, 0:PADC], b=tlr[:, :, 0:PADC]:
               E.tensor_copy(a, b), waits=[(sS, ev_all)])
        tails_v = self.v(lambda E, a=tlr[:, :, 0:PADC],
                         b=prer[:, :, PREW - PADC:PREW]:
                         E.tensor_copy(a, b), inc=True)
        conv_v = 0
        comb_g = prev["comb_g"] if prev else 0
        silu_s = prev["silu_s"] if prev else 0
        for c in range(DCH):
            rc = c % 2
            f = [self.convf[:, cfo + c * 4 + j:cfo + c * 4 + j + 1]
                 for j in range(4)]
            s0 = prebuf[:, c * PREW + PADC:(c + 1) * PREW]
            s2 = prebuf[:, c * PREW + PADC - 2:(c + 1) * PREW - 2]
            s0x = prebuf[:, c * PREW + PADC - 2:(c + 1) * PREW]
            s2x = prebuf[:, c * PREW + PADC - 4:(c + 1) * PREW - 2]
            pec = dst[:, c * dw + doff:c * dw + doff + SEG]
            pox = porot[:, rc * POW + 2:rc * POW + 4 + SEG]
            posh = porot[:, rc * POW + 3:rc * POW + 3 + SEG]
            wt = [(sS, evs[c]), (sV, tails_v), (sS, silu_s),
                  (sG, comb_g)] + list(dstg)
            self.v(lambda E, pec=pec, s0=s0, f0=f[0]:
                   E.tensor_scalar_mul(pec, s0, f0), waits=wt)
            self.v(lambda E, pec=pec, s2=s2, f2=f[2]:
                   E.scalar_tensor_tensor(pec, s2, f2, pec, ALU.mult, ALU.add))
            self.v(lambda E, pox=pox, s0x=s0x, f1=f[1]:
                   E.tensor_scalar_mul(pox, s0x, f1))
            conv_v = self.v(lambda E, pox=pox, s2x=s2x, f3=f[3]:
                            E.scalar_tensor_tensor(pox, s2x, f3, pox,
                                                   ALU.mult, ALU.add), inc=True)
            comb_g = self.g(lambda E, pec=pec, posh=posh:
                            E.tensor_tensor(pec, pec, posh, ALU.add),
                            waits=[(sV, conv_v)], inc=True)
            silu_s = self.s(lambda E, d=pec: E.activation(
                d, d, AF.Silu), waits=[(sG, comb_g)], inc=True)
        return dict(conv_v=conv_v, comb_g=comb_g, silu_s=silu_s, ev_all=ev_all)

    def segment(self, tglob, own, glob_first):
        sT, sV, sS, sG = self.sT, self.sV, self.sS, self.sG
        self.relay_off = self.relay_base + (tglob // SEG % 2) * 32768
        ctx = ExitStack()
        tag = f"_{tglob}_{int(own)}"
        sb = lambda n, s, dt=BF16: self.sbuf(n + tag, s, dt, ctx)

        q = sb("q", [128, DCH * SEG]) if own else None
        k = sb("k", [128, DCH * SEG])
        vv = sb("vv", [128, DCH * VW])
        v_tm = sb("v_tm", [128, H * NCH * 256])
        d_tm = sb("d_tm", [128, H * NCH * 256]) if own else None
        Pg = sb("Pg", [128, DCH * SEG]) if own else None
        fs_tm = sb("fs_tm", [128, H * NCH * 256]) if own else None
        fl_tm = sb("fl_tm", [128, H * NCH * 256]) if own else None
        ktmb = sb("ktmb", [128, 2 * H * 256])
        beta_tm = sb("beta_tm", [128, H * NCH], F32)
        nbeta_tm = sb("nbeta_tm", [128, H * NCH], F32)
        rq_tm = sb("rq_tm", [128, H * NCH], F32) if own else None
        powb = [sb(f"powb{h}", [128, 9 * 128]) for h in range(H)]
        uwA = [sb(f"uwA{h}", [128, 512]) for h in range(H)]
        rhs = [sb(f"rhs{h}", [128, 512]) for h in range(H)]
        ut = [sb(f"ut{h}", [128, 256]) for h in range(H)]
        wfm = [sb(f"wfm{h}", [128, 256]) for h in range(H)]
        r4 = self.r4scr
        beta = r4[0:4, 0:512]
        krow = r4[0:1, 1024:3072]

        subA = ExitStack()
        pre = self.sbuf("pre" + tag, [128, DCH * PREW], BF16, subA)
        porot = self.sbuf("porot" + tag, [128, 2 * POW], BF16, subA)

        def sq0(c):
            return pre[:, c * PREW + PADC:c * PREW + PADC + SEG]

        xb = self.sbuf("xb" + tag, [128, DCH * SEG], BF16, subA)
        h_ = self.sbuf("h" + tag, [128, DCH * SEG], BF16, subA)

        xv = self.dma("gpsimd", xb[:].rearrange("p (c t) -> p c t", c=DCH),
                      self.xT[:, tglob:tglob + SEG]
                      .rearrange("(c p) t -> p c t", p=128),
                      waits=self.subB_busy + self.stats_busy)

        h_done = self.ln_build(xb, lambda c: h_[:, c * SEG:(c + 1) * SEG],
                               D if own else D, self.sa_p, sq0, xv)
        self.maybe_dbg("h", lambda: h_[:, 0:DBG_SHAPE[1]])
        if STAGE < 2:
            self.barrier(); subA.close(); ctx.close(); return

        rq_res = None
        if own:
            rq_res = self.projection(self.wq_h, h_, h_done, q, SEG, 0, 0,
                                     pre, self.pret[0], 0, porot, None,
                                     dstg=self.drv_busy)
        rk_res = self.projection(self.wk_h, h_, h_done, k, SEG, 0, DCH * 4,
                                 pre, self.pret[1], 1, porot, rq_res,
                                 dstg=[(sT, self.ktm_busy[0])])
        rv_res = self.projection(self.wv_h, h_, h_done, vv, VW, 64, 2 * DCH * 4,
                                 pre, self.pret[2], 0, porot, rk_res,
                                 dstg=self.stats_busy)
        svv = rv_res["silu_s"]
        self.maybe_dbg("v", lambda: vv[:, 64:64 + DBG_SHAPE[1]])

        if (not own) and tglob == TOWN - SEG:
            slot = self.wslot[0]
            wv = self.dma("gpsimd", slot[:].rearrange("p (c m) -> p c m", c=DCH),
                          self.wq_h[:, :].rearrange("(c p) m -> p c m", p=128),
                          waits=[(sT, self.slot_free[0])])
            for m in range(DCH):
                mm = 0
                for kk in range(DCH):
                    stat = slot[:, kk * 1024 + m * 128:kk * 1024 + (m + 1) * 128]
                    mov = h_[:, kk * SEG + SEG - PADC:(kk + 1) * SEG]
                    mm = self.t(lambda E, s=stat, m2=mov, m3=m, kk=kk: E.matmul(
                        self.pw(m3 % 2)[:, 0:PADC], s, m2,
                        start=(kk == 0), stop=(kk == DCH - 1)),
                        waits=wv + [(sV, h_done), (sS, self.sS.v)],
                        inc=(kk == DCH - 1))
                self.s(lambda E, m=m: E.copy(
                    self.pret[0][:, m * PADC:(m + 1) * PADC],
                    self.pw(m % 2)[:, 0:PADC]), waits=[(sT, mm)], inc=True)
            self.slot_free[0] = self.sT.v

        bmm = 0
        for c in range(DCH):
            stat = self.wbW[:, c * H:(c + 1) * H]
            mov = h_[:, c * SEG:(c + 1) * SEG]
            bmm = self.t(lambda E, s=stat, m=mov, c=c: E.matmul(
                self.pw(2)[0:4, 0:SEG], s, m,
                start=(c == 0), stop=(c == DCH - 1)),
                waits=[(sV, h_done)] + self.c_g + self.brw(2),
                inc=(c == DCH - 1))
        bs = self.s(lambda E, b=beta: E.activation(b, self.pw(2)[0:4, 0:SEG],
                                                   AF.Sigmoid),
                    waits=[(sT, bmm)] + self.r4_busy, inc=True)
        self.brs(2, (sS, bs))
        bv = 0
        if not own:
            bv = self.v(lambda E, b=beta: E.tensor_scalar_mul(
                b, b, self.pfl[0:4, 0:1]),
                waits=[(sS, bs)] + self.c_sync, inc=True)
        bt_d = []
        for hh in range(H):
            bt_d = bt_d + self.relay(beta[hh:hh + 1, :],
                              beta_tm[:, hh * NCH:(hh + 1) * NCH], SEG,
                              waits=[(sS, bs), (sV, bv)] + self.btm_busy)
        beta_v = self.v(lambda E: E.tensor_scalar_mul(
            nbeta_tm[:], beta_tm[:], -1.0), waits=bt_d, inc=True)

        def l2rows(src_slices, dst_rows, ready):
            sqd = 0
            mmv = 0
            for cc in range(DCH):
                sqd = self.s(lambda E, d=sq0(cc), s=src_slices(cc):
                             E.activation(d, s, AF.Square),
                             waits=list(ready) + [(sT, mmv),
                                                  (sV, rv_res["conv_v"])],
                             inc=True)
                hh = cc // 2
                mmv = self.t(lambda E, cc=cc, hh=hh, d=sq0(cc): E.matmul(
                    self.pw(3)[0:4, 0:SEG], self.e4[:, hh * 4:(hh + 1) * 4], d,
                    start=(cc == 0), stop=(cc == DCH - 1)),
                    waits=[(sS, sqd)] + self.brw(3),
                    inc=(cc == DCH - 1))
            sv2 = self.s(lambda E, d=dst_rows: E.activation(
                d, self.pw(3)[0:4, 0:SEG], AF.Sqrt,
                bias=self.kcon[0:4, 1:2]),
                waits=[(sT, mmv)] + self.r4_busy, inc=True)
            self.brs(3, (sS, sv2))
            rdone = self.v(lambda E, d=dst_rows: E.reciprocal(d, d),
                           waits=[(sS, sv2)], inc=True)
            return rdone

        kr = r4[0:4, 0:512]
        qr = r4[0:4, 512:1024]
        kr_done = l2rows(lambda cc: k[:, cc * SEG:(cc + 1) * SEG], kr,
                         [(sS, rk_res["silu_s"]), (sV, self.sV.v)] + bt_d)
        off = self.relay_off
        self.relay_off += 2048
        wv2 = self.dma("scalar", self.relay_h[off:off + 2048], kr,
                       waits=[(sV, kr_done)])
        kr_read = wv2
        kb_d = self.dma("scalar", krow, self.relay_h[off:off + 2048],
                        waits=wv2 + self.r4_busy)
        kscaled_v = 0
        for hh in range(H):
            mov = krow[0:1, hh * SEG:(hh + 1) * SEG]
            bct = self.t(lambda E, m=mov: E.matmul(
                self.pw(2)[:, 0:SEG], self.onerow_f[:, :], m,
                start=True, stop=True),
                waits=kb_d + self.brw(2), inc=True)
            for c2 in range(2):
                cc = 2 * hh + c2
                kc = k[:, cc * SEG:(cc + 1) * SEG]
                kscaled_v = self.v(lambda E, kc=kc: E.tensor_mul(
                    kc, kc, self.pw(2)[:, 0:SEG]), waits=[(sT, bct)], inc=True)
            self.brs(2, (sV, kscaled_v))
            self.t(lambda E: None, waits=[(sV, kscaled_v)])
        ktoks = {}

        def ktm_flip(ch2, extra):
            tok = []
            for hh in range(H):
                for c2 in range(2):
                    col = ((ch2 % 2) * H + hh) * 256 + c2 * 128
                    tok = tok + self.dmaT(
                        ktmb[:, col:col + 128],
                        k[:, (2 * hh + c2) * SEG + ch2 * 128:
                          (2 * hh + c2) * SEG + (ch2 + 1) * 128],
                        waits=[(sV, kscaled_v)] + extra)
            ktoks[ch2] = tok

        for ch2 in range(min(2, NCH)):
            ktm_flip(ch2, [(sT, self.ktm_busy[0])])
        rq_d = []
        if own:
            qr_done = l2rows(lambda cc: q[:, cc * SEG:(cc + 1) * SEG], qr,
                             [(sS, rq_res["silu_s"]), (sV, kscaled_v)])
            for hh in range(H):
                rq_d = rq_d + self.relay(qr[hh:hh + 1, :],
                                  rq_tm[:, hh * NCH:(hh + 1) * NCH], SEG,
                                  waits=[(sV, qr_done)])
        self.maybe_dbg("kn", lambda: k[:, 0:DBG_SHAPE[1]])

        tm_d = []
        for hh in range(H):
            for c2 in range(2):
                cc = 2 * hh + c2
                for tc in range(NCH):
                    col = (hh * NCH + tc) * 256 + c2 * 128
                    tm_d = tm_d + self.dmaT(
                        v_tm[:, col:col + 128],
                        vv[:, cc * VW + 64 + tc * 128:cc * VW + 64 + (tc + 1) * 128],
                        waits=[(sS, svv)] + self.vtm_busy)

        gateP_done = 0
        wo_ready = 0
        if own:
            slot = self.wslot[1]
            wv = self.dma("gpsimd", slot[:].rearrange("p (c m) -> p c m", c=DCH),
                          self.w1a_h[:, :].rearrange("(c p) m -> p c m", p=128),
                          waits=[(sT, self.slot_free[1])])
            evs = []
            for m in range(DCH):
                mm = 0
                for kk in range(DCH):
                    stat = slot[:, kk * 1024 + m * 128:kk * 1024 + (m + 1) * 128]
                    mov = h_[:, kk * SEG:(kk + 1) * SEG]
                    mm = self.t(lambda E, s=stat, m2=mov, m3=m, kk=kk: E.matmul(
                        self.pw(m3 % 2)[:, 0:SEG], s, m2,
                        start=(kk == 0), stop=(kk == DCH - 1)),
                        waits=wv +
                              ([(sS, evs[m - 2])] if m >= 2
                               else self.brw(m % 2)),
                        inc=(kk == DCH - 1))
                ev = self.s(lambda E, m=m, d=Pg[:, m * SEG:(m + 1) * SEG]:
                            E.copy(d, self.pw(m % 2)[:, 0:SEG]),
                            waits=[(sT, mm)] + self.drv_busy, inc=True)
                evs.append(ev)
                self.brs(m % 2, (sS, ev))
            self.slot_free[1] = self.sT.v
            gateP_done = evs[-1]
            wov = self.dma("gpsimd",
                           self.wslot[0][:].rearrange("p (c m) -> p c m", c=DCH),
                           self.wo_h[:, :].rearrange("(c p) m -> p c m", p=128),
                           waits=[(sT, self.slot_free[0])])
            wsr = self.wslot[0][:].rearrange("p (c m) -> p c m", c=DCH)
            self.v(lambda E, a=wsr[:, 0::2, :]: E.tensor_scalar_mul(
                a, a, self.rmsw[:, 0:1]), waits=wov + self.c_sync)
            wo_ready = self.v(lambda E, a=wsr[:, 1::2, :]: E.tensor_scalar_mul(
                a, a, self.rmsw[:, 1:2]), inc=True)

        if STAGE < 3:
            self.barrier(); subA.close(); ctx.close(); return
        self.subB_busy = [(sT, self.sT.v), (sS, self.sS.v)]

        if STAGE < 4:
            self.barrier(); subA.close(); ctx.close(); return
        for ch in range(NCH):
            gfirst = glob_first and ch == 0
            kf = [[k[:, (2 * hh + c2) * SEG + ch * 128:
                     (2 * hh + c2) * SEG + (ch + 1) * 128]
                   for c2 in range(2)] for hh in range(H)]
            if own:
                qf = [[q[:, (2 * hh + c2) * SEG + ch * 128:
                         (2 * hh + c2) * SEG + (ch + 1) * 128]
                       for c2 in range(2)] for hh in range(H)]
            vtm = [v_tm[:, (hh * NCH + ch) * 256:(hh * NCH + ch + 1) * 256]
                   for hh in range(H)]
            ktm = [ktmb[:, ((ch % 2) * H + hh) * 256:
                        ((ch % 2) * H + hh + 1) * 256] for hh in range(H)]
            bpp = [beta_tm[:, hh * NCH + ch:hh * NCH + ch + 1] for hh in range(H)]

            def ce(hh):
                return (self.s, sS) if hh % 2 == 0 else (self.v, sV)

            np_v = []
            for hh in range(H):
                bA = hh
                w0 = [(sV, kscaled_v), (sV, beta_v)] + self.brw(bA)
                self.t(lambda E, a=kf[hh][0], bA=bA: E.matmul(
                    self.pw(bA, 128)[:, :], a, a, start=True, stop=False),
                    waits=w0)
                gr = self.t(lambda E, a=kf[hh][1], bA=bA: E.matmul(
                    self.pw(bA, 128)[:, :], a, a, start=False, stop=True),
                    inc=True)
                npp0 = powb[hh][:, 7 * 128:8 * 128]
                np_v.append(self.v(
                    lambda E, d=npp0, bA=bA, b=bpp[hh]:
                    E.scalar_tensor_tensor(d, self.pw(bA, 128)[:, :], b,
                                           self.neglow[:], ALU.mult, ALU.mult),
                    waits=[(sT, gr)], inc=True))
                self.brs(bA, (sV, np_v[hh]))
            if DSTAGE < 2:
                continue
            mp_s = []
            for hh in range(H):
                bB = 4 + hh
                eng, esem = ce(hh)
                pb = self.pw(bB, 128).bitcast(BF16)[:, 0:128]
                tt = self.t(lambda E, s=powb[hh][:, 7 * 128:8 * 128], pb=pb:
                            E.transpose(pb, s, self.iden[:]),
                            waits=[(sV, np_v[hh])] + self.brw(bB),
                            inc=True)
                mp_s.append(eng(lambda E, d=powb[hh][:, 0:128], pb=pb:
                                _cp(E, d, pb), waits=[(sT, tt)],
                                inc=True))
                self.brs(bB, (esem, mp_s[hh]))
            if DSTAGE < 3:
                continue
            ch_s = [[(ce(h)[1], mp_s[h])] for h in range(H)]
            for j in range(1, 7):
                for hh in range(H):
                    bA = hh
                    eng, esem = ce(hh)
                    n0 = powb[hh][:, (7 + (j - 1) % 2) * 128:
                                  (8 + (j - 1) % 2) * 128]
                    m0 = powb[hh][:, (j - 1) * 128:j * 128]
                    w0 = ch_s[hh] + ([(sV, np_v[hh])] if j == 1 else [])
                    self.t(lambda E, n0=n0, m0=m0, bA=bA: E.matmul(
                        self.pw(bA, 128)[:, :], m0, n0, start=True,
                        stop=False), waits=w0)
                    tt = self.t(lambda E, n0=n0, m0=m0, bA=bA: E.matmul(
                        self.pw(bA, 128, off=128)[:, :], n0, m0,
                        start=False, stop=True), inc=True)
                    eng(lambda E, d=powb[hh][:, (7 + j % 2) * 128:
                                             (8 + j % 2) * 128], bA=bA:
                        _cp(E, d, self.pw(bA, 128)[:, :]),
                        waits=[(sT, tt)])
                    cs2 = eng(
                        lambda E, d=powb[hh][:, j * 128:(j + 1) * 128],
                        bA=bA: _cp(E, d, self.pw(bA, 128, off=128)[:, :]),
                        inc=True)
                    ch_s[hh] = [(esem, cs2)]
            if DSTAGE < 4:
                continue
            sol_s = []
            for hh in range(H):
                bB = 4 + hh
                eng, esem = ce(hh)
                self.v(lambda E, d=rhs[hh][:, 0:256], s=vtm[hh], b=bpp[hh]:
                       E.tensor_scalar_mul(d, s, b),
                       waits=tm_d + ktoks[ch] + [(sV, beta_v),
                              (sT, self.rhs_busy[hh])])
                rhs_v = self.v(lambda E, d=rhs[hh][:, 256:512], s=ktm[hh],
                               b=bpp[hh]: E.tensor_scalar_mul(d, s, b),
                               inc=True)
                self.t(lambda E, bB=bB, r=rhs[hh]: E.matmul(
                    self.pw(bB)[:, :], self.iden[:], r[:], start=True,
                    stop=False, skip_group_check=True),
                    waits=[(sV, rhs_v), (esem, mp_s[hh])] + ch_s[hh])
                cur = rhs[hh][:]
                ev = []
                tt = 0
                for fj in range(6, -1, -1):
                    stat = powb[hh][:, fj * 128:(fj + 1) * 128]
                    tt = self.t(lambda E, s=stat, cur=cur, bB=bB, fj=fj:
                                E.matmul(self.pw(bB)[:, :], s, cur,
                                         start=False, stop=(fj == 0),
                                         skip_group_check=True),
                                waits=ev, inc=True)
                    if fj > 0:
                        nxt = (uwA[hh] if fj % 2 == 0 else rhs[hh])[:]
                        ev = [(esem, eng(lambda E, d=nxt, bB=bB: _cp(
                            E, d, self.pw(bB)[:, :]),
                            waits=[(sT, tt)], inc=True))]
                        cur = nxt
                self.rhs_busy[hh] = tt
                eng(lambda E, d=uwA[hh][:, 0:256], bB=bB: _cp(
                    E, d, self.pw(bB, 256)[:, :]), waits=[(sT, tt)])
                if hh % 2 == 0:
                    sv = self.s(lambda E, d=uwA[hh][:, 256:512], bB=bB:
                                E.activation(d, self.pw(bB, 256, off=256)[:, :],
                                             AF.Identity, scale=-1.0),
                                inc=True)
                else:
                    sv = self.v(lambda E, d=uwA[hh][:, 256:512], bB=bB:
                                E.tensor_scalar_mul(
                                    d, self.pw(bB, 256, off=256)[:, :], -1.0),
                                inc=True)
                sol_s.append((esem, sv))
            if DSTAGE < 5:
                continue
            wfm_d = []
            for hh in range(H):
                dd = []
                for c2 in range(2):
                    dd = dd + self.dmaT(
                        wfm[hh][:, c2 * 128:(c2 + 1) * 128],
                        uwA[hh][:, 256 + c2 * 128:256 + (c2 + 1) * 128],
                        waits=[sol_s[hh]])
                wfm_d.append(dd)
            for hh in range(H):
                bA = hh
                bB = 4 + hh
                eng, esem = ce(hh)
                if not gfirst:
                    self.t(lambda E, bA=bA, w0=wfm[hh][:, 0:128],
                           s0=self.S_sb[hh][:, 0:256]: E.matmul(
                        self.pw(bA, 256)[:, :], w0, s0, start=True, stop=False),
                        waits=wfm_d[hh] + self.scp_tok[hh] + ch_s[hh])
                    self.t(lambda E, bA=bA, w1=wfm[hh][:, 128:256],
                           s1=self.S_sb[hh][:, 256:512]: E.matmul(
                        self.pw(bA, 256)[:, :], w1, s1, start=False,
                        stop=False))
                    utt = self.t(lambda E, bA=bA, u=uwA[hh][:, 0:256]:
                                 E.matmul(self.pw(bA, 256)[:, :],
                                          self.iden[:], u,
                                          start=False, stop=True),
                                 waits=[sol_s[hh]], inc=True)
                else:
                    utt = self.t(lambda E, bA=bA, u=uwA[hh][:, 0:256]:
                                 E.matmul(self.pw(bA, 256)[:, :],
                                          self.iden[:], u,
                                          start=True, stop=True),
                                 waits=[sol_s[hh]] + ch_s[hh], inc=True)
                ut_s = eng(lambda E, d=ut[hh][:], bA=bA: _cp(
                    E, d, self.pw(bA, 256)[:, :]), waits=[(sT, utt)], inc=True)
                self.brs(bA, (esem, ut_s))
                d_s = None
                if own and DSTAGE >= 6:
                    alt_h = powb[hh][:, 8 * 128:9 * 128]
                    self.t(lambda E, a=kf[hh][0], b=qf[hh][0], bB=bB:
                           E.matmul(self.pw(bB, 128)[:, :], a, b,
                                    start=True, stop=False),
                           waits=[sol_s[hh], (sS, rq_res["silu_s"])])
                    at = self.t(lambda E, a=kf[hh][1], b=qf[hh][1], bB=bB:
                                E.matmul(self.pw(bB, 128)[:, :], a, b,
                                         start=False, stop=True), inc=True)
                    alt_v = self.v(lambda E, d=alt_h, bB=bB:
                                   E.tensor_mul(d, self.pw(bB, 128)[:, :],
                                                self.upin[:]),
                                   waits=[(sT, at)], inc=True)
                    if not gfirst:
                        self.t(lambda E, a=qf[hh][0], s0=self.S_sb[hh][:, 0:256],
                               bB=bB: E.matmul(
                            self.pw(bB, 256)[:, :], a, s0,
                            start=True, stop=False),
                            waits=[(sV, alt_v)] + self.scp_tok[hh])
                        self.t(lambda E, a=qf[hh][1], s1=self.S_sb[hh][:, 256:512],
                               bB=bB: E.matmul(
                            self.pw(bB, 256)[:, :], a, s1,
                            start=False, stop=False))
                        ot = self.t(lambda E, a=alt_h, u=ut[hh][:],
                                    bB=bB: E.matmul(
                            self.pw(bB, 256)[:, :], a, u,
                            start=False, stop=True),
                            waits=[(esem, ut_s)], inc=True)
                    else:
                        ot = self.t(lambda E, a=alt_h, u=ut[hh][:],
                                    bB=bB: E.matmul(
                            self.pw(bB, 256)[:, :], a, u,
                            start=True, stop=True),
                            waits=[(sV, alt_v), (esem, ut_s)], inc=True)
                    dcol = d_tm[:, (hh * NCH + ch) * 256:(hh * NCH + ch + 1) * 256]
                    rqs = rq_tm[:, hh * NCH + ch:hh * NCH + ch + 1]
                    if hh % 2 == 0:
                        d_s = (sS, self.s(lambda E, d=dcol, rqs=rqs, bB=bB:
                                          E.activation(d, self.pw(bB, 256)[:, :],
                                                       AF.Identity, scale=rqs),
                                          waits=[(sT, ot)] + rq_d +
                                                self.dtm_busy, inc=True))
                    else:
                        d_s = (sV, self.v(lambda E, d=dcol, rqs=rqs, bB=bB:
                                          E.tensor_scalar_mul(
                                              d, self.pw(bB, 256)[:, :], rqs),
                                          waits=[(sT, ot)] + rq_d +
                                                self.dtm_busy, inc=True))
                if DSTAGE < 7:
                    continue
                sw = [(esem, ut_s)] + ktoks[ch] + \
                     ([d_s] if d_s is not None else [(sV, alt_v)]
                      if (own and DSTAGE >= 6) else [sol_s[hh]])
                if not gfirst:
                    self.t(lambda E, bB=bB, s0=self.S_sb[hh][:, 0:256]:
                           E.matmul(self.pw(bB, 256)[:, :], self.iden[:], s0,
                                    start=True, stop=False),
                           waits=sw + self.scp_tok[hh])
                    self.t(lambda E, s=ktm[hh][:, 0:128], u=ut[hh][:], bB=bB:
                           E.matmul(self.pw(bB, 256)[:, :], s, u,
                                    start=False, stop=False))
                    self.t(lambda E, bB=bB, s1=self.S_sb[hh][:, 256:512]:
                           E.matmul(self.pw(bB, 256, off=256)[:, :],
                                    self.iden[:], s1, start=False, stop=False))
                    st2 = self.t(lambda E, s=ktm[hh][:, 128:256], u=ut[hh][:],
                                 bB=bB: E.matmul(
                        self.pw(bB, 256, off=256)[:, :], s, u,
                        start=False, stop=True), inc=True)
                else:
                    self.t(lambda E, s=ktm[hh][:, 0:128], u=ut[hh][:], bB=bB:
                           E.matmul(self.pw(bB, 256)[:, :], s, u,
                                    start=True, stop=False), waits=sw)
                    st2 = self.t(lambda E, s=ktm[hh][:, 128:256], u=ut[hh][:],
                                 bB=bB: E.matmul(
                        self.pw(bB, 256, off=256)[:, :], s, u,
                        start=False, stop=True), inc=True)
                sc2 = eng(lambda E, hh=hh, bB=bB: _cp(
                    E, self.S_sb[hh][:], self.pw(bB)[:, :]),
                    waits=[(sT, st2)], inc=True)
                self.scp_tok[hh] = [(esem, sc2)]
                self.brs(bB, (esem, sc2))
            if ch + 2 < NCH:
                ktm_flip(ch + 2, [(sT, self.sT.v)])
        self.ktm_busy[0] = self.sT.v
        self.btm_busy = [(sV, self.sV.v)]
        if HBAR & 1:
            self.barrier()

        if own:
            self.maybe_dbg("dtm", lambda: d_tm[:, 0:DBG_SHAPE[1]])
        if STAGE < 5 and own:
            self.barrier(); subA.close(); ctx.close(); return
        if not own:
            self.v(lambda E,
                   a=self.vtail[:].rearrange("p (c w) -> p c w", c=DCH),
                   b=vv[:].rearrange("p (c w) -> p c w", c=DCH)[:, :, SEG:VW]:
                   E.tensor_copy(a, b), waits=[(sS, svv)], inc=True)
            self.r4_busy = [(sT, self.sT.v)] + bt_d + kr_read
            self.beta_busy = bt_d
            self.scratch_busy = [(sT, self.sT.v)]
            self.btm_busy = [(sV, self.sV.v)]
            subA.close()
            ctx.close()
            return

        subC = ExitStack()
        fl = self.sbuf("fl" + tag, [128, DCH * SEG], BF16, subC)
        fs = self.sbuf("fs" + tag, [128, DCH * SEG], BF16, subC)
        d_fm = xb
        drv = q[0:36, 0:2048]
        mu2b = q[0:4, 2048:4096]
        stsb = drv

        halo_v = 0
        for c in range(DCH):
            halo_v = self.v(lambda E, a=vv[:, c * VW:c * VW + 64],
                            b=self.vtail[:, c * 64:(c + 1) * 64]:
                            E.tensor_copy(a, b), waits=[(sS, svv)], inc=True)
        fir_g = rv_res["comb_g"]
        fir_v = 0
        for c in range(DCH):
            rc = c % 2
            pox = porot[:, rc * POW + 2:rc * POW + 4 + SEG]
            posh = porot[:, rc * POW + 3:rc * POW + 3 + SEG]
            flc = fl[:, c * SEG:(c + 1) * SEG]
            fsc = fs[:, c * SEG:(c + 1) * SEG]
            wt = [(sS, svv), (sG, fir_g), (sV, halo_v)] + tm_d + \
                 self.fir_guard
            first = True
            for j in range(0, 64, 2):
                fj = self.firlf[:, c * 64 + j:c * 64 + j + 1]
                s = vv[:, c * VW + 64 - j:c * VW + 64 - j + SEG]
                if first:
                    self.v(lambda E, d=flc, s=s, fj=fj:
                           E.tensor_scalar_mul(d, s, fj), waits=wt)
                    first = False
                else:
                    self.v(lambda E, d=flc, s=s, fj=fj:
                           E.scalar_tensor_tensor(d, s, fj, d,
                                                  ALU.mult, ALU.add))
            first = True
            for j in range(1, 64, 2):
                fj = self.firlf[:, c * 64 + j:c * 64 + j + 1]
                s = vv[:, c * VW + 62 - (j - 1):c * VW + 64 - (j - 1) + SEG]
                if first:
                    self.v(lambda E, d=pox, s=s, fj=fj:
                           E.tensor_scalar_mul(d, s, fj), waits=wt)
                    first = False
                else:
                    self.v(lambda E, d=pox, s=s, fj=fj:
                           E.scalar_tensor_tensor(d, s, fj, d,
                                                  ALU.mult, ALU.add))
            flv = self.v(lambda E: None, inc=True)
            flg = self.g(lambda E, d=flc, posh=posh:
                         E.tensor_tensor(d, d, posh, ALU.add),
                         waits=[(sV, flv)], inc=True)
            self.v(lambda E, d=fsc, s=vv[:, c * VW + 64:c * VW + 64 + SEG],
                   fj=self.firsf[:, c * 5:c * 5 + 1]:
                   E.tensor_scalar_mul(d, s, fj), waits=[(sG, flg)])
            for j in (2, 4):
                self.v(lambda E, d=fsc,
                       s=vv[:, c * VW + 64 - j:c * VW + 64 - j + SEG],
                       fj=self.firsf[:, c * 5 + j:c * 5 + j + 1]:
                       E.scalar_tensor_tensor(d, s, fj, d, ALU.mult, ALU.add))
            self.v(lambda E, d=pox, s=vv[:, c * VW + 62:c * VW + 64 + SEG],
                   fj=self.firsf[:, c * 5 + 1:c * 5 + 2]:
                   E.tensor_scalar_mul(d, s, fj))
            fsv = self.v(lambda E, d=pox,
                         s=vv[:, c * VW + 60:c * VW + 62 + SEG],
                         fj=self.firsf[:, c * 5 + 3:c * 5 + 4]:
                         E.scalar_tensor_tensor(d, s, fj, d,
                                                ALU.mult, ALU.add), inc=True)
            fir_g = self.g(lambda E, d=fsc, posh=posh:
                           E.tensor_tensor(d, d, posh, ALU.add),
                           waits=[(sV, fsv)], inc=True)
            fir_v = fsv
        vtl = self.v(lambda E,
                     a=self.vtail[:].rearrange("p (c w) -> p c w", c=DCH),
                     b=vv[:].rearrange("p (c w) -> p c w", c=DCH)[:, :, SEG:VW]:
                     E.tensor_copy(a, b),
                     waits=[(sS, svv), (sG, fir_g)], inc=True)
        if HBAR & 2:
            self.barrier()
        self.maybe_dbg("fl", lambda: fl[:, 0:DBG_SHAPE[1]])

        if OSTAGE < 2:
            self.barrier(); subC.close(); subA.close(); ctx.close(); return
        post_tm = []
        for hh in range(H):
            for c2 in range(2):
                cc = 2 * hh + c2
                for tc in range(NCH):
                    col = (hh * NCH + tc) * 256 + c2 * 128
                    fcol = cc * SEG + tc * 128
                    post_tm = post_tm + self.dmaT(
                        fs_tm[:, col:col + 128], fs[:, fcol:fcol + 128],
                        waits=[(sG, fir_g)])
                    post_tm = post_tm + self.dmaT(
                        fl_tm[:, col:col + 128], fl[:, fcol:fcol + 128])
                    post_tm = post_tm + self.dmaT(
                        d_fm[:, fcol:fcol + 128], d_tm[:, col:col + 128],
                        waits=[(sS, self.sS.v), (sV, self.sV.v),
                               (sT, self.sT.v)])

        if OSTAGE < 3:
            self.barrier(); subC.close(); subA.close(); ctx.close(); return
        streams = [lambda cc: fs[:, cc * SEG:(cc + 1) * SEG],
                   lambda cc: fl[:, cc * SEG:(cc + 1) * SEG],
                   lambda cc: d_fm[:, cc * SEG:(cc + 1) * SEG],
                   lambda cc: vv[:, cc * VW + 64:cc * VW + 64 + SEG]]
        stat_w = [(sG, fir_g), (sV, fir_v), (sS, svv)] + post_tm
        stP = self.PW[0:36, :]
        first_in_bank = [True] * H
        for st_i in range(4):
            stf = streams[st_i]
            for hh in range(H):
                for c2 in range(2):
                    cc = 2 * hh + c2
                    fib = first_in_bank[hh]
                    first_in_bank[hh] = False
                    self.t(lambda E, st_i=st_i, m=stf(cc), hh=hh, fib=fib:
                           E.matmul(stP[:, hh * SEG:(hh + 1) * SEG],
                                    self.estat[:, st_i * 36:st_i * 36 + 36], m,
                                    start=fib, stop=False),
                           waits=stat_w + (self.brw(hh) if fib else []))
        sq_t = 0
        ab_t = 0
        for st_i in range(4):
            stf = streams[st_i]
            for hh in range(H):
                for c2 in range(2):
                    cc = 2 * hh + c2
                    rc = cc % 2
                    sqd = self.s(lambda E, d=sq0(cc), m=stf(cc):
                                 E.activation(d, m, AF.Square),
                                 waits=stat_w + [(sT, sq_t)], inc=True)
                    sq_t = self.t(lambda E, st_i=st_i, hh=hh, d=sq0(cc):
                                  E.matmul(stP[:, hh * SEG:(hh + 1) * SEG],
                                           self.estat[:, (8 + st_i) * 36:
                                                      (8 + st_i) * 36 + 36],
                                           d, start=False, stop=False),
                                  waits=[(sS, sqd)], inc=True)
                    ab = porot[:, rc * POW + 2:rc * POW + 2 + SEG]
                    abd = self.v(lambda E, d=ab, m=stf(cc):
                                 E.scalar_tensor_tensor(d, m, -1.0, m,
                                                        ALU.mult, ALU.max),
                                 waits=stat_w + [(sT, ab_t)], inc=True)
                    last = (st_i == 3 and c2 == 1)
                    ab_t = self.t(lambda E, st_i=st_i, hh=hh, d=ab, last=last:
                                  E.matmul(stP[:, hh * SEG:(hh + 1) * SEG],
                                           self.estat[:, (4 + st_i) * 36:
                                                      (4 + st_i) * 36 + 36],
                                           d, start=False, stop=last),
                                  waits=[(sV, abd)], inc=True)
        stat_t = self.sT.v
        self.scratch_busy = [(sT, stat_t)]
        self.stats_busy = [(sT, stat_t), (sS, self.sS.v), (sV, self.sV.v)]
        stv = self.s(lambda E: E.copy(stsb, stP[0:36, 0:4 * SEG]),
                     waits=[(sT, stat_t)], inc=True)
        for b in range(4):
            self.brs(b, (sS, stv))
        drv_v = self.v(lambda E: E.tensor_scalar_mul(
            drv[0:12, 0:2048], drv[0:12, 0:2048], 1.0 / DV),
            waits=[(sS, stv)], inc=True)
        self.s(lambda E: E.activation(mu2b, drv[0:4, 0:2048], AF.Square),
               waits=[(sV, drv_v)])
        drv_s = self.s(lambda E: E.activation(drv[32:36, 0:2048],
                                              drv[32:36, 0:2048], AF.Sqrt),
                       waits=[(sS, stv)], inc=True)
        off_l2 = self.relay_off
        self.relay_off += 8192
        l2w = self.dma("gpsimd", self.relay_h[off_l2:off_l2 + 8192],
                       drv[32:36, 0:2048], waits=[(sS, drv_s)])
        l2_d = self.dma("gpsimd", drv[12:16, 0:2048],
                        self.relay_h[off_l2:off_l2 + 8192]
                        .rearrange("(p t) -> p t", p=4),
                        waits=l2w)
        if HBAR & 4:
            self.barrier()
        self.maybe_dbg("drv", lambda: drv[0:16, 0:DBG_SHAPE[1]])
        if OSTAGE < 4:
            self.barrier(); subC.close(); subA.close(); ctx.close(); return
        subC.close()

        subD = ExitStack()
        h1 = self.sbuf("h1" + tag, [128, DCH * SEG], BF16, subD)
        p_tm = self.sbuf("p_tm" + tag, [128, NCH * 16], F32, subD)
        mx = self.sbuf("mx" + tag, [128, NCH * 4], F32, subD)
        osq = self.sbuf("osq" + tag, [128, 256], BF16, subD)
        osc = self.sbuf("osc" + tag, [128, 256], BF16, subD)
        osum = self.sbuf("osum" + tag, [128, 4], F32, subD)
        lbuf = r4[0:4, 0:2048]

        lg_s = []
        for hh in range(H):
            evs = []
            for m in range(DCH):
                self.t(lambda E, m=m, hh=hh,
                       s=self.w1b[0:16, m * 128:(m + 1) * 128],
                       mo=drv[0:16, hh * SEG:(hh + 1) * SEG]: E.matmul(
                    self.pw(m % 2)[:, 0:SEG], s, mo, start=True, stop=False),
                    waits=[(sV, drv_v), (sS, drv_s), (sS, gateP_done)] +
                          l2_d +
                          ([(sS, evs[m - 2])] if m >= 2
                           else self.brw(m % 2)))
                if not (GSKIP & 2):
                    self.t(lambda E, m=m, hh=hh,
                           s=self.w1bv[0:4, m * 128:(m + 1) * 128],
                           mo=mu2b[0:4, hh * SEG:(hh + 1) * SEG]: E.matmul(
                        self.pw(m % 2)[:, 0:SEG], s, mo, start=False,
                        stop=False))
                mm = self.t(lambda E, m=m, mo=Pg[:, m * SEG:(m + 1) * SEG]:
                            E.matmul(self.pw(m % 2)[:, 0:SEG], self.iden[:],
                                     mo, start=False, stop=True), inc=True)
                ev = self.s(lambda E, m=m, d=h1[:, m * SEG:(m + 1) * SEG]:
                            E.activation(d, self.pw(m % 2)[:, 0:SEG],
                                         AF.Gelu,
                                         bias=self.b1s[:, m:m + 1]),
                            waits=[(sT, mm)] + self.h1_busy, inc=True)
                evs.append(ev)
                self.brs(m % 2, (sS, ev))
            lmm = 0
            for c in range(DCH):
                lmm = self.t(lambda E, c=c, hh=hh,
                             s=self.w2s[:, c * H:(c + 1) * H],
                             mo=h1[:, c * SEG:(c + 1) * SEG]: E.matmul(
                    self.pw(2 + hh % 2)[0:4, 0:SEG], s, mo,
                    start=(c == 0), stop=(c == DCH - 1)),
                    waits=[(sS, evs[-1])] + self.brw(2 + hh % 2),
                    inc=(c == DCH - 1))
            lg = self.s(lambda E, hh=hh,
                        d=lbuf[0:4, hh * SEG:(hh + 1) * SEG]:
                        E.copy(d, self.pw(2 + hh % 2)[0:4, 0:SEG]),
                        waits=[(sT, lmm)] + self.r4_busy + rq_d + kr_read,
                        inc=True)
            lg_s.append(lg)
            self.brs(2 + hh % 2, (sS, lg))
        logits_s = self.sS.v
        self.maybe_dbg("lg", lambda: lbuf[0:4, 0:DBG_SHAPE[1]])
        if OSTAGE < 5:
            self.barrier(); subD.close(); subA.close(); ctx.close(); return
        p_d = []
        for hh in range(H):
            for comp in range(4):
                dst = p_tm[:, :].rearrange("p (t g) -> p t g", g=16)[:, :,
                                                                    hh * 4 + comp]
                p_d = p_d + self.relay(
                    lbuf[comp:comp + 1, hh * SEG:(hh + 1) * SEG],
                    dst, SEG, waits=[(sS, logits_s)] + self.dtm_busy)
        self.maybe_dbg("ptm0", lambda: p_tm[:, 0:DBG_SHAPE[1]])
        v44 = p_tm[:, :].rearrange("p (t h c) -> p t h c", h=4, c=4)
        v16 = p_tm[:, :].rearrange("p (t g) -> p t g", g=16)
        mxv = mx[:, :].rearrange("p (t h) -> p t h", h=4)
        mxb = mx[:, :].rearrange("p (t h o) -> p t h o", h=4, o=1) \
            .to_broadcast([128, NCH, 4, 4])
        self.v(lambda E: E.tensor_add(
            v16, v16, self.bbrep[:, :].rearrange("p (o g) -> p o g", o=1)
            .to_broadcast([128, NCH, 16])), waits=p_d)
        self.v(lambda E: E.tensor_mul(
            v16, v16, self.invtemp[:, :].rearrange("p (o g) -> p o g", o=1)
            .to_broadcast([128, NCH, 16])))
        f1 = self.v(lambda E: E.reduce_max(mxv, v44, axis=AX.X), inc=True)
        sv3 = self.v(lambda E: E.tensor_sub(v44, v44, mxb),
                     waits=[(sV, f1)], inc=True)
        se = self.s(lambda E: E.activation(p_tm[:], p_tm[:], AF.Exp),
                    waits=[(sV, sv3)], inc=True)
        f2 = self.v(lambda E: E.reduce_sum(mxv, v44, axis=AX.X),
                    waits=[(sS, se)], inc=True)
        f3 = self.v(lambda E: E.reciprocal(mx[:], mx[:]),
                    waits=[(sV, f2)], inc=True)
        self.v(lambda E: E.tensor_mul(v44, v44, mxb), waits=[(sV, f3)])
        self.v(lambda E: E.tensor_max(
            v16, v16, self.epsc[:, :].rearrange("p (o g) -> p o g", o=1)
            .to_broadcast([128, NCH, 16])))
        f4 = self.v(lambda E: E.reduce_sum(mxv, v44, axis=AX.X), inc=True)
        f5 = self.v(lambda E: E.reciprocal(mx[:], mx[:]),
                    waits=[(sV, f4)], inc=True)
        self.v(lambda E: E.tensor_mul(v44, v44, mxb), waits=[(sV, f5)])
        p_done = self.v(lambda E: E.tensor_add(
            v44[:, :, :, 0], v44[:, :, :, 0],
            self.crlrep[:, :].rearrange("p (o h) -> p o h", o=1)
            .to_broadcast([128, NCH, 4])), inc=True)
        if HBAR & 8:
            self.barrier()
        self.maybe_dbg("p", lambda: p_tm[:, 0:DBG_SHAPE[1]])

        if OSTAGE < 6:
            self.barrier(); subD.close(); subA.close(); ctx.close(); return
        omn_s = []
        for hh in range(H):
            for tc in range(NCH):
                col = (hh * NCH + tc) * 256
                pj = [p_tm[:, tc * 16 + hh * 4 + c:tc * 16 + hh * 4 + c + 1]
                      for c in range(4)]
                vtile = v_tm[:, col:col + 256]
                self.v(lambda E, s=fs_tm[:, col:col + 256], p=pj[0]:
                       E.tensor_scalar_mul(osc[:], s, p),
                       waits=[(sV, p_done),
                              (sS, self.sS.v)] + post_tm)
                self.v(lambda E, s=fl_tm[:, col:col + 256], p=pj[1]:
                       E.scalar_tensor_tensor(osc[:], s, p, osc[:],
                                              ALU.mult, ALU.add))
                self.v(lambda E, s=d_tm[:, col:col + 256], p=pj[2]:
                       E.scalar_tensor_tensor(osc[:], s, p, osc[:],
                                              ALU.mult, ALU.add))
                ov = self.v(lambda E, vt=vtile, p=pj[3]:
                            E.scalar_tensor_tensor(vt, vt, p, osc[:],
                                                   ALU.mult, ALU.add),
                            inc=True)
                sq1 = self.s(lambda E, vt=vtile: E.activation(
                    osq[:], vt, AF.Square, accum_out=osum[:, 0:1]),
                    waits=[(sV, ov)], inc=True)
                ss2 = self.s(lambda E: E.activation(
                    osum[:, 1:2], osum[:, 0:1], AF.Sqrt, scale=1.0 / DV,
                    bias=self.kcon[:, 0:1]), waits=[(sS, sq1)], inc=True)
                rv2 = self.v(lambda E: E.reciprocal(osum[:, 2:3], osum[:, 1:2]),
                             waits=[(sS, ss2)], inc=True)
                omn_s.append(self.s(lambda E, vt=vtile: E.activation(
                    vt, vt, AF.Identity, scale=osum[:, 2:3]),
                    waits=[(sV, rv2)], inc=True))
        if HBAR & 16:
            self.barrier()
        om_fm = h1
        omfm_d = []
        for hh in range(H):
            for c2 in range(2):
                cc = 2 * hh + c2
                for tc in range(NCH):
                    col = (hh * NCH + tc) * 256 + c2 * 128
                    fcol = cc * SEG + tc * 128
                    omfm_d = omfm_d + self.dmaT(om_fm[:, fcol:fcol + 128],
                                       v_tm[:, col:col + 128],
                                       waits=[(sS, omn_s[hh * NCH + tc])])
        self.maybe_dbg("omfm", lambda: om_fm[:, 0:DBG_SHAPE[1]])

        if OSTAGE < 7:
            self.barrier(); subD.close(); subA.close(); ctx.close(); return
        oo = tglob % TOWN
        for m in range(DCH):
            xv2 = self.dma("sync", self.xf32[m % 2][:],
                           self.xT[m * 128:(m + 1) * 128, tglob:tglob + SEG],
                           waits=[(sV, self.xf_free[m % 2])])
            mm = 0
            for kk in range(DCH):
                stat = self.wslot[0][:, kk * 1024 + m * 128:
                                     kk * 1024 + (m + 1) * 128]
                mov = om_fm[:, kk * SEG:(kk + 1) * SEG]
                mm = self.t(lambda E, s=stat, mo=mov, m3=m, kk=kk: E.matmul(
                    self.pw(m3 % 2)[:, 0:SEG], s, mo,
                    start=(kk == 0), stop=(kk == DCH - 1)),
                    waits=omfm_d + [(sV, wo_ready)] + self.brw(m % 2),
                    inc=(kk == DCH - 1))
            sv4 = self.v(lambda E, m=m: E.scalar_tensor_tensor(
                self.xmo[m % 2][:], self.pw(m % 2)[:, 0:SEG],
                self.ga_p[:, m:m + 1], self.xf32[m % 2][:], ALU.mult, ALU.add),
                waits=[(sT, mm)] + xv2 + self.xmo_free[m % 2],
                inc=True)
            self.xf_free[m % 2] = sv4
            self.brs(m % 2, (sV, sv4))
            self.xmo_free[m % 2] = self.dma(
                "sync", self.xmid_h[m * 128:(m + 1) * 128, oo:oo + SEG],
                self.xmo[m % 2][:], waits=[(sV, sv4)])
        self.slot_free[0] = self.sT.v
        self.drv_busy = [(sT, self.sT.v)]
        self.h1_busy = [(sT, self.sT.v)]
        self.r4_busy = [(sT, self.sT.v)] + p_d + rq_d + kr_read + bt_d
        self.beta_busy = bt_d + [(sV, bv)] if not own else bt_d
        self.dtm_busy = [(sV, self.sV.v)] + post_tm
        self.vtm_busy = omfm_d
        self.fir_guard = self.stats_busy + post_tm
        subD.close()
        subA.close()
        ctx.close()

    def ffn(self):
        sT, sV, sS, sG = self.sT, self.sV, self.sS, self.sG
        self.relay_off = self.relay_base + 2 * 32768
        ctx = ExitStack()
        sb = lambda n, s, dt=BF16: self.sbuf(n + "_f", s, dt, ctx)
        h2 = sb("h2", [128, DCH * TOWN])
        gsc = sb("gsc", [NE, TOWN], F32)
        gfb = sb("gfb", [NE, TOWN])
        growb = sb("growb", [1, SEG], F32)
        g_tm = sb("g_tm", [128, 16 * NE], F32)
        hs = sb("hs", [128, (HF // 128) * SEG])
        h1e = sb("h1e", [128, NE * (HF // 128) * SEG])
        wsl = [sb(f"wsl{i}", [128, (HF // 128) * 1024]) for i in range(2)]
        gb = sb("gbc", [128, SEG])

        self.gro_free = 0
        sub = ExitStack()
        xb2 = self.sbuf("xb2_f", [128, DCH * SEG], BF16, sub)
        sq2 = self.sbuf("sq2_f", [128, DCH * SEG], BF16, sub)
        for ti in range(4):
            xv = self.dma("gpsimd", xb2[:].rearrange("p (c t) -> p c t", c=DCH),
                          self.xmid_h[:, ti * SEG:(ti + 1) * SEG]
                          .rearrange("(c p) t -> p c t", p=128),
                          waits=self.snap() if ti == 0 else
                          [(sT, self.sT.v), (sV, self.sV.v)])
            self.r4_busy = [(sT, self.sT.v), (sV, self.sV.v)]
            self.ln_build(
                xb2,
                lambda c, ti=ti: h2[:, c * TOWN + ti * SEG:
                                    c * TOWN + (ti + 1) * SEG],
                4 * D, self.sbp, lambda c: sq2[:, c * SEG:(c + 1) * SEG],
                xv)
        sub.close()
        h2c = lambda c, ti: h2[:, c * TOWN + ti * SEG:c * TOWN + (ti + 1) * SEG]
        self.maybe_dbg("h2", lambda: h2[:, 0:DBG_SHAPE[1]])

        gmm = 0
        for ti in range(4):
            for c in range(DCH):
                gmm = self.t(lambda E, c=c, ti=ti,
                             s=self.gwW[:, c * NE:(c + 1) * NE],
                             m=h2c(c, ti): E.matmul(
                    self.pw(2)[0:NE, 0:SEG], s, m,
                    start=(c == 0), stop=(c == DCH - 1)),
                    waits=[(sG, self.sG.v), (sV, self.sV.v),
                           (sS, self.sS.v)],
                    inc=(c == DCH - 1))
            gse = self.s(lambda E, ti=ti, d=gsc[0:NE, ti * SEG:(ti + 1) * SEG]:
                         E.activation(d, self.pw(2)[0:NE, 0:SEG], AF.Sigmoid),
                         waits=[(sT, gmm)], inc=True)
        gsv = self.v(lambda E: E.tensor_scalar_mul(gsc[:], gsc[:],
                                                   self.rsc8[:, 0:1]),
                     waits=[(sS, self.sS.v)], inc=True)
        gt_d = []
        for e in range(NE):
            dst = g_tm[:, :].rearrange("p (t e) -> p t e", e=NE)[:, :, e]
            gt_d = gt_d + self.relay(gsc[e:e + 1, :], dst, TOWN,
                                     waits=[(sV, gsv)])
        gv = g_tm[:, :].rearrange("p (t e) -> p t e", e=NE)
        m1b = sb("m1b", [128, 16], F32)
        m2b = sb("m2b", [128, 16], F32)
        isx = sb("isx", [128, 16 * NE], F32)
        iv = isx[:, :].rearrange("p (t e) -> p t e", e=NE)
        m1v = m1b[:, :].rearrange("p (t o) -> p t o", o=1)
        m2v = m2b[:, :].rearrange("p (t o) -> p t o", o=1)
        g1 = self.v(lambda E: E.reduce_max(m1v, gv, axis=AX.X),
                    waits=gt_d, inc=True)
        self.v(lambda E: E.tensor_tensor(iv, gv, m1v.to_broadcast([128, 16, NE]),
                                         ALU.is_ge), waits=[(sV, g1)])
        g2 = self.v(lambda E: E.scalar_tensor_tensor(iv, iv, -1e30, gv,
                                                     ALU.mult, ALU.add),
                    inc=True)
        g3 = self.v(lambda E: E.reduce_max(m2v, iv, axis=AX.X),
                    waits=[(sV, g2)], inc=True)
        self.v(lambda E: E.tensor_tensor(iv, gv, m2v.to_broadcast([128, 16, NE]),
                                         ALU.is_ge), waits=[(sV, g3)])
        self.v(lambda E: E.tensor_mul(iv, iv, gv))
        g4 = self.v(lambda E: E.tensor_add(m1b[:], m1b[:], m2b[:]), inc=True)
        g5 = self.v(lambda E: E.reciprocal(m1b[:], m1b[:]),
                    waits=[(sV, g4)], inc=True)
        gtv = self.v(lambda E: E.tensor_mul(
            iv, iv, m1v.to_broadcast([128, 16, NE])), waits=[(sV, g5)],
            inc=True)
        gf_d = []
        gscr = []
        for e in range(NE):
            off = self.relay_off
            self.relay_off += 2048
            scr = self.relay_h[off:off + 2048]
            wv2 = self.dma("scalar", scr,
                           isx[:, :].rearrange("p (t e) -> p t e", e=NE)[:, :, e],
                           waits=[(sV, gtv)], slow=True)
            gd = self.dma("gpsimd", gfb[e:e + 1, :],
                          scr.rearrange("(p t) -> t p", t=16),
                          waits=wv2, slow=True)
            gf_d = gf_d + wv2 + gd
            gscr.append(scr)
        self.maybe_dbg("gf", lambda: gsc[0:NE, 0:DBG_SHAPE[1]])

        HB = HF // 128
        wv_pend = {}
        for ti in range(4):
            for e in range(NE):
                si = e % 2
                if (ti, e) not in wv_pend:
                    pass
                wv = self.dma("gpsimd",
                              wsl[si][:].rearrange("p (c m) -> p c m", c=DCH),
                              self.w1e_h[e, :, :]
                              .rearrange("(c p) m -> p c m", p=128),
                              waits=[(sT, self.wsl_free.get(si, 0))])
                for m in range(HB):
                    mm = 0
                    for kk in range(DCH):
                        stat = wsl[si][:, kk * HF + m * 128:kk * HF + (m + 1) * 128]
                        mov = h2c(kk, ti)
                        mm = self.t(lambda E, s=stat, mo=mov, m3=m, kk=kk:
                                    E.matmul(self.pw(m3 % 2)[:, 0:SEG], s, mo,
                                             start=(kk == 0),
                                             stop=(kk == DCH - 1)),
                                    waits=wv + [(sS, self.sS.v),
                                                (sV, self.sV.v)],
                                    inc=(kk == DCH - 1))
                    dcol = h1e[:, (e * HB + m) * SEG:(e * HB + m + 1) * SEG]
                    self.s(lambda E, d=dcol, m3=m, b=self.b1es[:, e * HB + m:
                                                              e * HB + m + 1]:
                           E.activation(d, self.pw(m3 % 2)[:, 0:SEG], AF.Gelu,
                                        bias=b), waits=[(sT, mm)], inc=True)
                self.wsl_free[si] = self.sT.v
                grv = self.dma(
                    "sync", growb[0:1, :],
                    gscr[e].rearrange("(p t) -> t p", t=16)[4 * ti:4 * ti + 4, :],
                    waits=gf_d + [(sT, self.gro_free)], slow=True)
                bct = self.t(lambda E, m=growb[0:1, :]:
                             E.matmul(self.pw(2)[:, 0:SEG], self.onerow_f[:, :],
                                      m, start=True, stop=True),
                             waits=grv + [(sS, self.sS.v)], inc=True)
                self.gro_free = bct
                gbe = self.s(lambda E: E.copy(gb[:], self.pw(2)[:, 0:SEG]),
                             waits=[(sT, bct)], inc=True)
                gw_v = 0
                for m in range(HB):
                    dcol = h1e[:, (e * HB + m) * SEG:(e * HB + m + 1) * SEG]
                    gw_v = self.v(lambda E, d=dcol: E.tensor_mul(d, d, gb[:]),
                                  waits=[(sS, gbe)], inc=True)
                self.t(lambda E: None, waits=[(sV, gw_v)])
            for (w_h, dst, bias, nb) in ((self.ws1_h, hs, self.bs1s, HB),):
                si = 0
                wv = self.dma("gpsimd",
                              wsl[si][:, 0:DCH * HF]
                              .rearrange("p (c m) -> p c m", c=DCH),
                              w_h[:, :].rearrange("(c p) m -> p c m", p=128),
                              waits=[(sT, self.wsl_free.get(si, 0))])
                for m in range(nb):
                    mm = 0
                    for kk in range(DCH):
                        stat = wsl[si][:, kk * HF + m * 128:kk * HF + (m + 1) * 128]
                        mov = h2c(kk, ti)
                        mm = self.t(lambda E, s=stat, mo=mov, m3=m, kk=kk:
                                    E.matmul(self.pw(m3 % 2)[:, 0:SEG], s, mo,
                                             start=(kk == 0),
                                             stop=(kk == DCH - 1)),
                                    waits=wv + [(sS, self.sS.v)],
                                    inc=(kk == DCH - 1))
                    dcol = dst[:, m * SEG:(m + 1) * SEG]
                    self.s(lambda E, d=dcol, m3=m, b=bias[:, m:m + 1]:
                           E.activation(d, self.pw(m3 % 2)[:, 0:SEG], AF.Gelu,
                                        bias=b), waits=[(sT, mm)], inc=True)
                self.wsl_free[si] = self.sT.v
            h1_done = self.sS.v
            gwv_done = self.sV.v
            for mg in range(2):
                ymm = [0] * 4
                for src_i in range(NE + 1):
                    si = src_i % 2
                    if src_i < NE:
                        wv = self.dma("gpsimd",
                                      wsl[si][:, 0:2048]
                                      .rearrange("p (c m) -> p c m", c=HB),
                                      self.w2e_h[src_i, :,
                                                 mg * 512:(mg + 1) * 512]
                                      .rearrange("(c p) m -> p c m", p=128),
                                      waits=[(sT, self.wsl_free.get(si, 0))])
                        hsrc = lambda kk, src_i=src_i: h1e[
                            :, (src_i * HB + kk) * SEG:(src_i * HB + kk + 1) * SEG]
                    else:
                        wv = self.dma("gpsimd",
                                      wsl[si][:, 0:2048]
                                      .rearrange("p (c m) -> p c m", c=HB),
                                      self.ws2_h[:, mg * 512:(mg + 1) * 512]
                                      .rearrange("(c p) m -> p c m", p=128),
                                      waits=[(sT, self.wsl_free.get(si, 0))])
                        hsrc = lambda kk: hs[:, kk * SEG:(kk + 1) * SEG]
                    for m in range(4):
                        mm = 0
                        for kk in range(HB):
                            stat = wsl[si][:, kk * 512 + m * 128:
                                           kk * 512 + (m + 1) * 128]
                            mov = hsrc(kk)
                            mm = self.t(lambda E, s=stat, mo=mov, m3=m,
                                        first=(src_i == 0 and kk == 0):
                                        E.matmul(self.pw(m3)[:, 0:SEG], s, mo,
                                                 start=first, stop=False),
                                        waits=wv + [(sS, h1_done),
                                               (sV, gwv_done),
                                               (sS, self.sS.v),
                                               (sV, self.sV.v)],
                                        inc=(kk == HB - 1))
                        ymm[m] = mm
                    self.wsl_free[si] = self.sT.v
                for m in range(4):
                    mc = mg * 4 + m
                    bmm = self.t(lambda E, m=m, mc=mc,
                                 s=self.b2e_s[0:NE, mc * 128:(mc + 1) * 128],
                                 mo=gfb[0:NE, ti * SEG:(ti + 1) * SEG]:
                                 E.matmul(self.pw(m)[:, 0:SEG], s, mo,
                                          start=False, stop=True),
                                 waits=gf_d + self.c_g, inc=True)
                    xv2 = self.dma("sync", self.xf32[m % 2][:],
                                   self.xmid_h[mc * 128:(mc + 1) * 128,
                                               ti * SEG:(ti + 1) * SEG],
                                   waits=[(sV, self.xf_free[m % 2])])
                    sv5 = self.s(lambda E, m=m, mc=mc: E.activation(
                        self.pw(m, SEG)[:, :], self.pw(m, SEG)[:, :],
                        AF.Identity,
                        bias=self.bs2p[:, mc:mc + 1]),
                        waits=[(sT, bmm)] + self.bs2_tok, inc=True)
                    sv4 = self.v(lambda E, m=m, mc=mc: E.scalar_tensor_tensor(
                        self.xmo[m % 2][:], self.pw(m, SEG)[:, :],
                        self.gb_p[:, mc:mc + 1], self.xf32[m % 2][:],
                        ALU.mult, ALU.add),
                        waits=[(sS, sv5)] + xv2 +
                              self.xmo_free[m % 2], inc=True)
                    self.xf_free[m % 2] = sv4
                    self.xmo_free[m % 2] = self.dma(
                        "sync", self.out_h[mc * 128:(mc + 1) * 128,
                                           ti * SEG:(ti + 1) * SEG],
                        self.xmo[m % 2][:], waits=[(sV, sv4)])
            self.barrier()
        ctx.close()

    def build(self):
        self.declare_io()
        self.alloc_persistent()
        self.load_consts()
        self.bs2_tok = self.dma(
            "sync", self.bs2p[:],
            self.bs2row_h[:].rearrange("(c p) -> p c", p=128), slow=True)
        self.relay_off = 8192
        self.relay_base = 8192 + 32768
        self.phase_mod()
        self.phase_small_consts()
        self.scp_tok = [[] for _ in range(H)]
        self.bankr = {b: [] for b in range(8)}
        self.r4_busy = []
        self.beta_busy = []
        self.drv_busy = []
        self.scratch_busy = []
        self.subB_busy = []
        self.dtm_busy = []
        self.vtm_busy = []
        self.stats_busy = []
        self.h1_busy = []
        self.btm_busy = []
        self.fir_guard = []
        self.ktm_busy = [0, 0]
        self.rhs_busy = [0] * H
        self.xf_free = {0: 0, 1: 0}
        self.xmo_free = {0: [], 1: []}
        self.slot_free = [0, 0]
        self.wsl_free = {}
        for i in range(N_PRE):
            self.segment(i * SEG, own=False, glob_first=(i == 0))
        gt = 0
        for tl in [self.vtail] + self.pret:
            gt = self.v(lambda E, tl=tl: E.tensor_scalar_mul(
                tl[:], tl[:], self.pfl[:, 0:1]), waits=self.snap(), inc=True)
        self.barrier()
        for i in range(N_OWN):
            self.segment(TOWN + i * SEG, own=True,
                         glob_first=(N_PRE == 0 and i == 0))
        if RUN_FFN:
            self.ffn()
        fin = self.snap()
        for eng in Prog.ENGINES:
            self.P.op(eng, lambda E: None, waits=fin)
        self.emit()
        return self.nc

    def emit(self):
        P, nc = self.P, self.nc
        with nc.Block() as blk:
            @blk.tensor
            def _(t):
                for th in P.q["tensor"]:
                    th(t)

            @blk.vector
            def _(v):
                for th in P.q["vector"]:
                    th(v)

            @blk.scalar
            def _(s):
                for th in P.q["scalar"]:
                    th(s)

            @blk.gpsimd
            def _(g):
                for th in P.q["gpsimd"]:
                    th(g)

            @blk.sync
            def _(sy):
                for th in P.q["sync"]:
                    th(sy)


def _host_inputs(inputs, b, half):
    i = inputs
    xb_ = np.asarray(i["L0_attn_x"], dtype=np.float32)[b]
    if half == 0:
        xT = np.ascontiguousarray(
            np.concatenate([xb_[0:TOWN], xb_[0:TOWN]], 0).T)
    else:
        xT = np.ascontiguousarray(xb_.T)

    def fm(a, w):
        return np.ascontiguousarray(
            a.reshape(DCH, 128, w).transpose(1, 0, 2).reshape(128, DCH * w))

    refk = [0, 2, 1, 3]
    w1 = np.asarray(i["L0_attn_W1"], dtype=np.float32)
    w1b = np.stack([w1[1024 + (r % 4) * 4 + refk[r // 4]] for r in range(16)])
    w1bv = np.stack([w1[1024 + st * 4 + 1] for st in range(4)])
    d = {
        "xT": xT,
        "pflag": np.full((128, 1), float(half), np.float32),
        "wq": i["L0_attn_Wq"], "wk": i["L0_attn_Wk"], "wv": i["L0_attn_Wv"],
        "wb": i["L0_attn_Wb"],
        "convf": np.concatenate(
            [fm(np.asarray(i[f"L0_attn_conv_{p}"],
                           dtype=np.float32)[:, ::-1], 4)
             for p in "qkv"], axis=1),
        "firl": fm(np.asarray(i["L0_attn_fir_long"],
                              dtype=np.float32).reshape(1024, 64)[:, ::-1], 64),
        "firs": fm(np.asarray(i["L0_attn_fir_short"],
                              dtype=np.float32).reshape(1024, 5)[:, ::-1], 5),
        "w1a": w1[:1024], "w1b": w1b, "w1bv": w1bv,
        "b1": np.asarray(i["L0_attn_b1"], dtype=np.float32).reshape(DCH, 128).T,
        "w2": i["L0_attn_W2"],
        "b2row": np.asarray(i["L0_attn_b2"], dtype=np.float32).reshape(1, 4),
        "bbrow": np.asarray(i["L0_attn_base_bias"],
                            dtype=np.float32).reshape(1, 16),
        "ltrow": np.asarray(i["L0_attn_log_temp"],
                            dtype=np.float32).reshape(1, 4),
        "crlrow": np.asarray(i["L0_attn_conv_residual_logit"],
                             dtype=np.float32).reshape(1, 4),
        "rmsw": np.asarray(i["L0_attn_rms_w"],
                           dtype=np.float32).reshape(2, 128).T,
        "wo": i["L0_attn_Wo"],
        "gw": i["L0_ffn_gate_W"],
        "w1e": i["L0_ffn_W1"],
        "b1e": np.asarray(i["L0_ffn_b1"], dtype=np.float32)
        .reshape(NE, 4, 128).transpose(2, 0, 1).reshape(128, NE * 4),
        "w2e": i["L0_ffn_W2"],
        "b2e": i["L0_ffn_b2"],
        "ws1": i["L0_ffn_Ws1"],
        "bs1": np.asarray(i["L0_ffn_bs1"], dtype=np.float32).reshape(4, 128).T,
        "ws2": i["L0_ffn_Ws2"],
        "bs2row": np.asarray(i["L0_ffn_bs2"], dtype=np.float32).reshape(1024),
        "rsc": np.asarray(i["L0_ffn_route_scale"],
                          dtype=np.float32).reshape(1, 1),
        "cvec": np.asarray(i["L0_c"], dtype=np.float32)[b].reshape(DCH, 128).T,
        "wmod": i["L0_wmod"],
    }
    return {k: np.ascontiguousarray(np.asarray(v, dtype=np.float32))
            for k, v in d.items()}


def kernel(**inputs):
    bk = BK()
    nc = bk.build()
    in_maps = [_host_inputs(inputs, c // 2, c % 2) for c in range(8)]
    res = run_bass_kernel_spmd(nc, in_maps, core_ids=list(range(8)))
    out = np.zeros((B, L, D), np.float32)
    for c in range(8):
        b, half = c // 2, c % 2
        out[b, half * TOWN:(half + 1) * TOWN, :] = res.results[c]["out"].T
    return out


# version 10
# speedup vs baseline: 1.0167x; 1.0069x over previous
import numpy as np
import ml_dtypes
from contextlib import ExitStack

import concourse.bass as bass
import concourse.mybir as mybir
from concourse.bass_utils import run_bass_kernel_spmd

F32 = mybir.dt.float32
BF16 = mybir.dt.bfloat16
AF = mybir.ActivationFunctionType
ALU = mybir.AluOpType
AX = mybir.AxisListType
BF = ml_dtypes.bfloat16

B, L, D = 4, 4096, 1024
H, DV = 4, 256
NE, HF = 8, 512
CH = 128
SEG = 512
TOWN = 2048
NSEG = TOWN // SEG
NCH = SEG // CH
DCH = D // 128
VW = SEG + 64
PADC = 4
PREW = SEG + PADC
POW = SEG + 4

DBG_POINT = None
DBG_SHAPE = None
STAGE = 9
DSTAGE = 9
OSTAGE = 9
GSKIP = 0
HBAR = 0
N_OWN = NSEG
N_PRE = NSEG
RUN_FFN = True
RACE_CHECK = False


def _cp(E, d, s):
    if hasattr(E, "copy"):
        return E.copy(d, s)
    return E.tensor_copy(d, s)


class Ctr:
    def __init__(self, ctx, nc, name, is_dma=False):
        self.h = ctx.enter_context(nc.semaphore(name))
        self.v = 0
        self.name = name
        self.is_dma = is_dma


class Prog:
    ENGINES = ("tensor", "vector", "scalar", "gpsimd", "sync")

    def __init__(self):
        self.q = {e: [] for e in self.ENGINES}
        self.lastwait = {e: {} for e in self.ENGINES}

    def op(self, eng, fn, waits=(), inc=None, dn=1):
        ws = []
        for (c, v) in waits:
            if v <= 0:
                continue
            if self.lastwait[eng].get(id(c), 0) < v:
                ws.append((c.h, v))
                self.lastwait[eng][id(c)] = v
        tgt = None
        if inc is not None:
            inc.v += dn
            tgt = (inc.h, dn)

        def thunk(E, ws=tuple(ws), tgt=tgt, fn=fn):
            for (h, v) in ws:
                E.wait_ge(h, v)
            ins = fn(E)
            if tgt is not None:
                if ins is None:
                    ins = E.nop()
                ins.then_inc(tgt[0], tgt[1])

        self.q[eng].append(thunk)


class BK:
    def __init__(self):
        self.nc = bass.Bass("TRN2", target_bir_lowering=False,
                            detect_race_conditions=RACE_CHECK)
        self.P = Prog()
        self.ctx = ExitStack()
        nc = self.nc
        self.sT = Ctr(self.ctx, nc, "sT")
        self.sV = Ctr(self.ctx, nc, "sV")
        self.sS = Ctr(self.ctx, nc, "sS")
        self.sG = Ctr(self.ctx, nc, "sG")
        NQ = 6
        self.qsems = {q: [Ctr(self.ctx, nc, f"d{q}{i}", is_dma=True)
                          for i in range(NQ)] for q in ("sync", "scalar",
                                                        "gpsimd")}
        self.qrr = {q: 0 for q in self.qsems}
        self.dbg_done = False

    def t(self, fn, waits=(), inc=False):
        self.P.op("tensor", fn, waits, self.sT if inc else None)
        return self.sT.v

    def v(self, fn, waits=(), inc=False):
        self.P.op("vector", fn, waits, self.sV if inc else None)
        return self.sV.v

    def s(self, fn, waits=(), inc=False):
        self.P.op("scalar", fn, waits, self.sS if inc else None)
        return self.sS.v

    def g(self, fn, waits=(), inc=False):
        self.P.op("gpsimd", fn, waits, self.sG if inc else None)
        return self.sG.v

    def dma(self, q, dst, src, waits=(), slow=False):
        ring = self.qsems[q]
        ctr = ring[self.qrr[q]]
        self.qrr[q] = (self.qrr[q] + 1) % len(ring)
        chain = [(ctr, ctr.v)]
        self.P.op(q, lambda E, d=dst, s=src, slow=slow: E.dma_start(
            out=d, in_=s, allow_slow_non_contiguous=slow),
            list(waits) + chain, ctr, dn=16)
        return [(ctr, ctr.v)]

    def qtok(self, q=None):
        qs = [q] if q else list(self.qsems)
        return [(c, c.v) for qq in qs for c in self.qsems[qq]]

    def dmaT(self, dst, src, waits=(), q="sync"):
        ring = self.qsems[q]
        ctr = ring[self.qrr[q]]
        self.qrr[q] = (self.qrr[q] + 1) % len(ring)
        chain = [(ctr, ctr.v)]
        self.P.op(q,
                  lambda E, d=dst, s=src: E.dma_start(out=d, in_=s,
                                                      transpose=True),
                  list(waits) + chain, ctr, dn=16)
        return [(ctr, ctr.v)]

    def relay(self, src_ap, dst_ap, N, waits=()):
        off = self.relay_off
        self.relay_off += (N + 127) & ~127
        scr = self.relay_h[off:off + N]
        wv = self.dma("gpsimd", scr, src_ap, waits=waits)
        return self.dma("gpsimd", dst_ap,
                        scr.rearrange("(c p) -> p c", p=128),
                        waits=wv, slow=True)

    def sbuf(self, name, shape, dt=BF16, ctx=None):
        return (ctx or self.ctx).enter_context(
            self.nc.sbuf_tensor(name, list(shape), dt))

    def snap(self):
        return [(self.sT, self.sT.v), (self.sV, self.sV.v), (self.sS, self.sS.v),
                (self.sG, self.sG.v)] + self.qtok()

    def barrier(self):
        snap = self.snap()
        for eng in Prog.ENGINES:
            self.P.op(eng, lambda E: None, waits=snap)

    def maybe_dbg(self, point, ap_fn):
        if DBG_POINT != point or self.dbg_done:
            return
        self.dbg_done = True
        ap = ap_fn()
        n0, n1 = DBG_SHAPE
        vv2 = self.v(lambda E, ap=ap: E.tensor_copy(
            self.dbgscr[0:n0, 0:n1], ap), waits=self.snap(), inc=True)
        self.dma("sync", self.dbg_h[:, :], self.dbgscr[0:n0, 0:n1],
                 waits=[(self.sV, vv2)])
        self.barrier()

    def declare_io(self):
        nc = self.nc

        def din(name, shape, dtype=F32):
            return nc.dram_tensor(name, list(shape), dtype, kind="ExternalInput").ap()

        self.xT = din("xT", [D, L])
        self.pflag = din("pflag", [128, 1])
        self.wq_h = din("wq", [D, D]); self.wk_h = din("wk", [D, D])
        self.wv_h = din("wv", [D, D]); self.wb_h = din("wb", [D, H])
        self.convf_h = din("convf", [128, 3 * DCH * 4])
        self.firl_h = din("firl", [128, DCH * 64])
        self.firs_h = din("firs", [128, DCH * 5])
        self.w1a_h = din("w1a", [D, D]); self.w1b_h = din("w1b", [16, D])
        self.w1bv_h = din("w1bv", [4, D])
        self.b1_h = din("b1", [128, DCH]); self.w2_h = din("w2", [D, H])
        self.b2row_h = din("b2row", [1, 4]); self.bbrow_h = din("bbrow", [1, 16])
        self.ltrow_h = din("ltrow", [1, 4]); self.crlrow_h = din("crlrow", [1, 4])
        self.rmsw_h = din("rmsw", [128, 2]); self.wo_h = din("wo", [D, D])
        self.gw_h = din("gw", [D, NE])
        self.w1e_h = din("w1e", [NE, D, HF])
        self.b1e_h = din("b1e", [128, NE * (HF // 128)])
        self.w2e_h = din("w2e", [NE, HF, D])
        self.b2e_h = din("b2e", [NE, D])
        self.ws1_h = din("ws1", [D, HF]); self.bs1_h = din("bs1", [128, HF // 128])
        self.ws2_h = din("ws2", [HF, D]); self.bs2row_h = din("bs2row", [D])
        self.rsc_h = din("rsc", [1, 1])
        self.c_h = din("cvec", [128, DCH])
        self.wmod_h = din("wmod", [D, 6 * D])
        self.out_h = nc.dram_tensor("out", [D, TOWN], F32, kind="ExternalOutput").ap()
        self.relay_h = nc.dram_tensor("relayscr", [1 << 18], F32,
                                      kind="Internal").ap()
        self.relay_off = 0
        self.xmid_h = nc.dram_tensor("xmid", [D, TOWN], F32, kind="Internal").ap()
        self.dbg_h = None
        if DBG_POINT is not None:
            self.dbg_h = nc.dram_tensor("dbg", list(DBG_SHAPE), F32,
                                        kind="ExternalOutput").ap()

        self.iden_c = nc.inline_tensor(np.eye(128, dtype=BF), "iden_c").ap()
        self.neglow_c = nc.inline_tensor(
            (-np.tril(np.ones((128, 128), np.float32), -1)).astype(BF),
            "neglow_c").ap()
        self.upin_c = nc.inline_tensor(
            np.triu(np.ones((128, 128), np.float32)).astype(BF), "upin_c").ap()
        epsp = np.zeros((128, 16), np.float32)
        epsp[:, 0::4] = 0.05
        epsp[:, 1::4] = 0.05
        self.epsp_c = nc.inline_tensor(epsp, "epsp_c").ap()
        est = np.zeros((128, 12 * 36), np.float32)
        for r in range(12):
            est[:, r * 36 + r] = 1.0
            if 8 <= r < 12:
                est[:, r * 36 + 24 + r] = 1.0
        self.estat_c = nc.inline_tensor(est.astype(BF), "estat_c").ap()
        e4 = np.zeros((128, 16), np.float32)
        for r in range(4):
            e4[:, r * 4 + r] = 1.0
        self.e4_c = nc.inline_tensor(e4.astype(BF), "e4_c").ap()

    def alloc_persistent(self):
        sb = self.sbuf
        self.iden = sb("iden", [128, 128])
        self.neglow = sb("neglow", [128, 128])
        self.upin = sb("upin", [128, 128])
        self.estat = sb("estat", [128, 12 * 36])
        self.e4 = sb("e4", [128, 16])
        self.epsc = sb("epsc", [128, 16], F32)
        self.onescol = sb("onescol", [128, 1])
        self.onerow = sb("onerow", [1, 128])
        self.onerow_f = sb("onerow_f", [1, 128], F32)
        self.convf = sb("convf_s", [128, 3 * DCH * 4], F32)
        self.firlf = sb("firlf", [128, DCH * 64], F32)
        self.firsf = sb("firsf", [128, DCH * 5], F32)
        self.w1b = sb("w1b_s", [16, D])
        self.w1bv = sb("w1bv_s", [4, D])
        self.wbW = sb("wbW", [128, DCH * H])
        self.gwW = sb("gwW", [128, DCH * NE])
        self.w2s = sb("w2s", [128, DCH * H])
        self.b1s = sb("b1s", [128, DCH], F32)
        self.b1es = sb("b1es", [128, NE * (HF // 128)], F32)
        self.bs1s = sb("bs1s", [128, HF // 128], F32)
        self.rmsw = sb("rmsw_s", [128, 2], F32)
        self.cvec = sb("cvec_s", [128, DCH])
        self.pfl = sb("pfl", [128, 1], F32)
        self.carow = sb("carow", [1, D], F32)
        self.sa_p = sb("sa_p", [128, DCH], F32)
        self.ga_p = sb("ga_p", [128, DCH], F32)
        self.sbp = sb("sbp", [128, DCH], F32)
        self.gb_p = sb("gb_p", [128, DCH], F32)
        self.S_sb = [sb(f"S_sb{h}", [128, 512]) for h in range(H)]
        self.vtail = sb("vtail", [128, DCH * 64])
        self.pret = [sb(f"pret{i}", [128, DCH * PADC]) for i in range(3)]
        self.wslot = [sb(f"wslot{i}", [128, DCH * 1024]) for i in range(2)]
        self.smallrow = sb("smallrow", [1, 64], F32)
        self.invtemp = sb("invtemp", [128, 16], F32)
        self.bbrep = sb("bbrep", [128, 16], F32)
        self.crlrep = sb("crlrep", [128, 4], F32)
        self.b2e_s = sb("b2e_s", [NE, D])
        self.r4scr = sb("r4scr", [4, 6 * SEG], F32)
        self.silc = sb("silc", [128, DCH])
        self.xf32 = [sb(f"xf32_{i}", [128, SEG], F32) for i in range(2)]
        self.xmo = [sb(f"xmo_{i}", [128, SEG], F32) for i in range(2)]
        self.rsc8 = sb("rsc8", [NE, 1], F32)
        self.kcon = sb("kcon", [128, 4], F32)
        self.bs2p = self.sbuf("bs2p", [128, DCH], F32)

        if DBG_POINT is not None:
            self.dbgscr = sb("dbgscr", [128, 2048], F32)
        self.PW = self.ctx.enter_context(self.nc.psum_tensor("PW", [128, 4096], F32))

    def pw(self, i, n=512, off=0):
        return self.PW[:, i * 512 + off:i * 512 + off + n]

    def brw(self, *bs):
        return [e for b in bs for e in self.bankr[b]]

    def brs(self, b, *ents):
        self.bankr[b] = list(ents)

    def load_consts(self):
        d = self.dma
        d("sync", self.iden[:], self.iden_c[:, :])
        d("sync", self.neglow[:], self.neglow_c[:, :])
        d("sync", self.upin[:], self.upin_c[:, :])
        d("sync", self.estat[:], self.estat_c[:, :])
        d("sync", self.e4[:], self.e4_c[:, :])
        d("sync", self.epsc[:], self.epsp_c[:, :])
        d("sync", self.convf[:], self.convf_h[:, :])
        d("sync", self.firlf[:], self.firl_h[:, :])
        d("sync", self.firsf[:], self.firs_h[:, :])
        d("sync", self.b1s[:], self.b1_h[:, :])
        d("sync", self.b1es[:], self.b1e_h[:, :])
        d("sync", self.bs1s[:], self.bs1_h[:, :])
        d("sync", self.rmsw[:], self.rmsw_h[:, :])
        d("sync", self.pfl[:], self.pflag[:, :])
        d("gpsimd", self.w1b[:], self.w1b_h[:, :])
        d("gpsimd", self.w1bv[:], self.w1bv_h[:, :])
        d("gpsimd", self.cvec[:], self.c_h[:, :])
        d("gpsimd", self.w2s[:].rearrange("p (c h) -> p c h", c=DCH),
          self.w2_h[:, :].rearrange("(c p) h -> p c h", p=128))
        d("gpsimd", self.wbW[:].rearrange("p (c h) -> p c h", c=DCH),
          self.wb_h[:, :].rearrange("(c p) h -> p c h", p=128))
        d("gpsimd", self.gwW[:].rearrange("p (c e) -> p c e", c=DCH),
          self.gw_h[:, :].rearrange("(c p) e -> p c e", p=128))
        d("gpsimd", self.b2e_s[:], self.b2e_h[:, :])
        self.c_sync = self.qtok()
        self.c_g = self.qtok()
        self.v(lambda E: E.memset(self.onescol[:], 1.0))
        self.v(lambda E: E.memset(self.onerow[:], 1.0))
        self.v(lambda E: E.memset(self.onerow_f[:], 1.0))
        self.v(lambda E: E.memset(self.vtail[:], 0.0))
        for i in range(3):
            self.v(lambda E, i=i: E.memset(self.pret[i][:], 0.0))
        self.v(lambda E: E.memset(self.r4scr[:], 0.0))
        self.v(lambda E: E.memset(self.smallrow[:], 0.0))
        self.v(lambda E: E.tensor_scalar_mul(self.w1bv[:], self.w1bv[:], -1.0),
               waits=list(self.c_g))
        self.v(lambda E: E.memset(self.kcon[:, 0:1], 1e-5))
        self.v(lambda E: E.memset(self.kcon[:, 1:2], 1e-6))
        self.init_v = self.v(lambda E: E.memset(self.kcon[:, 2:3], 1.0), inc=True)

    def phase_mod(self):
        sT, sS, sV = self.sT, self.sS, self.sV
        MODV = 8192
        sil_done = self.s(
            lambda E: E.activation(self.silc[:], self.cvec[:], AF.Silu),
            waits=self.c_g + [(sV, self.init_v)], inc=True)
        evacs = []
        wr_d = []
        wr_all = []
        for it in range(12):
            slot = self.wslot[it % 2][:, 0:DCH * 512]
            w = [(sT, evacs[it - 2][0])] if it >= 2 else []
            lv = self.dma("gpsimd", slot.rearrange("p (c t) -> p c t", c=DCH),
                          self.wmod_h[:, it * 512:(it + 1) * 512]
                          .rearrange("(c p) t -> p c t", p=128), waits=w)
            lt = 0
            for k in range(DCH):
                stat = self.silc[:, k:k + 1]
                mov = slot[:, k * 512:(k + 1) * 512]
                dstp = self.pw(it % 2)[0:1, :]
                lt = self.t(lambda E, s=stat, m=mov, d=dstp, k=k: E.matmul(
                    d, s, m, start=(k == 0), stop=(k == DCH - 1)),
                    waits=lv + [(sS, sil_done)] +
                          ([(sS, evacs[it - 2][1])] if it >= 2 else []),
                    inc=(k == DCH - 1))
            ev = self.s(lambda E, srcp=self.pw(it % 2)[0:1, :]: E.copy(
                self.r4scr[0:1, 0:512], srcp),
                waits=[(sT, lt)] + wr_d, inc=True)
            wr_d = self.dma("scalar", self.relay_h[MODV + it * 512:
                                                   MODV + (it + 1) * 512],
                            self.r4scr[0:1, 0:512], waits=[(sS, ev)])
            wr_all += wr_d
            evacs.append((lt, ev))
        pp_d = []
        for (dst, r0) in ((self.sa_p, 0), (self.ga_p, 2), (self.sbp, 3),
                          (self.gb_p, 5)):
            pp_d = pp_d + self.dma("scalar", dst[:],
                            self.relay_h[MODV + r0 * D:MODV + (r0 + 1) * D]
                            .rearrange("(c p) -> p c", p=128),
                            waits=wr_all, slow=True)
        t_sg = self.invtemp[:, 0:DCH]
        t_mg = self.invtemp[:, 8:8 + DCH]
        gprev = 0
        for g in (self.ga_p, self.gb_p):
            self.s(lambda E, g=g: E.sign(t_sg, g[:]),
                   waits=pp_d + [(self.sV, gprev)])
            sm = self.s(lambda E, g=g: E.activation(t_mg, g[:], AF.Abs),
                        inc=True)
            mxv = self.v(lambda E: E.tensor_scalar_max(t_mg, t_mg, 0.25),
                         waits=[(sS, sm)], inc=True)
            gprev = self.v(lambda E, g=g: E.tensor_mul(g[:], t_sg, t_mg),
                           waits=[(sV, mxv)], inc=True)
        self.mod_v = self.sV.v
        self.MODV = MODV

    def phase_small_consts(self):
        sS, sV, sT = self.sS, self.sV, self.sT
        ltr = self.sbuf("ltr", [1, 4], F32)
        crlr = self.sbuf("crlr", [1, 4], F32)
        b2r = self.sbuf("b2r", [1, 4], F32)
        bbr = self.sbuf("bbr", [1, 16], F32)
        rscr = self.sbuf("rscr", [1, 1], F32)
        lv = self.dma("sync", ltr[:], self.ltrow_h[:, :])
        lv = lv + self.dma("sync", crlr[:], self.crlrow_h[:, :])
        lv = lv + self.dma("sync", b2r[:], self.b2row_h[:, :])
        lv = lv + self.dma("sync", rscr[:], self.rsc_h[:, :])
        lv = lv + self.dma("sync", bbr[:], self.bbrow_h[:, :])
        sr = self.smallrow
        t_ab, t_ex, t_ln, t_re = (sr[0:1, 0:4], sr[0:1, 4:8], sr[0:1, 8:12],
                                  sr[0:1, 12:16])
        t_sp, t_it, t_cs, t_it4 = (sr[0:1, 16:20], sr[0:1, 20:24], sr[0:1, 24:28],
                                   sr[0:1, 28:44])
        e1 = self.s(lambda E: E.activation(t_ab, ltr[:], AF.Abs),
                    waits=lv + [(sV, self.init_v)], inc=True)
        e2 = self.s(lambda E: E.activation(t_ex, t_ab, AF.Exp, scale=-1.0),
                    waits=[(sS, e1)], inc=True)
        e3 = self.s(lambda E: E.activation(t_ln, t_ex, AF.Ln,
                                           bias=self.kcon[0:1, 2:3]),
                    waits=[(sS, e2)], inc=True)
        self.s(lambda E: E.activation(t_re, ltr[:], AF.Relu), inc=True)
        e4 = self.v(lambda E: E.tensor_add(t_sp, t_re, t_ln),
                    waits=[(sS, self.sS.v)], inc=True)
        e5 = self.v(lambda E: E.tensor_scalar_add(t_sp, t_sp, 1e-4),
                    waits=[(sV, e4)], inc=True)
        e6 = self.v(lambda E: E.reciprocal(t_it, t_sp), waits=[(sV, e5)],
                    inc=True)
        self.v(lambda E: E.tensor_copy(
            t_it4.rearrange("o (h c) -> o h c", h=4),
            t_it.rearrange("o (h c) -> o h c", h=4).to_broadcast([1, 4, 4])),
            waits=[(sV, e6)], inc=True)
        e7 = self.s(lambda E: E.activation(t_cs, crlr[:], AF.Sigmoid),
                    waits=list(lv), inc=True)
        self.s(lambda E: E.mul(t_cs, t_cs, 0.5), waits=[(sS, e7)], inc=True)
        self.v(lambda E: E.tensor_add(
            bbr[:].rearrange("o (h c) -> o h c", h=4),
            bbr[:].rearrange("o (h c) -> o h c", h=4),
            b2r[:].rearrange("o (h c) -> o h c", h=1).to_broadcast([1, 4, 4])),
            waits=list(lv), inc=True)
        for (src, dstp, n) in ((t_it4, self.invtemp[:, 0:16], 16),
                               (bbr[0:1, :], self.bbrep[:, 0:16], 16),
                               (t_cs, self.crlrep[:, 0:4], 4),
                               (rscr[0:1, :], None, 1)):
            lt = self.t(lambda E, src=src, n=n: E.matmul(
                self.pw(3)[:, 0:n], self.onerow_f[:, :], src, start=True,
                stop=True),
                waits=[(sV, self.sV.v), (sS, self.sS.v)] + self.c_sync,
                inc=True)
            if dstp is None:
                self.s(lambda E: E.copy(self.rsc8[:, 0:1], self.pw(3)[0:NE, 0:1]),
                       waits=[(sT, lt)], inc=True)
            else:
                self.s(lambda E, dstp=dstp, n=n: E.copy(dstp, self.pw(3)[:, 0:n]),
                       waits=[(sT, lt)], inc=True)
        self.maybe_dbg("sc", lambda: self.invtemp[:, 0:16])
        self.maybe_dbg("sc2", lambda: self.bbrep[:, 0:16])
        self.maybe_dbg("sc3", lambda: self.smallrow[0:1, 0:44])

    def ln_build(self, xb, hf, modoff, s_pp, sqbuf, ready):
        sT, sV, sS = self.sT, self.sV, self.sS
        r4 = self.r4scr
        r_sx, r_sx2 = r4[0:1, 1024:1536], r4[0:1, 1536:2048]
        r_mu, r_mur = r4[0:1, 2048:2560], r4[0:1, 2560:3072]
        r_r = r4[0:1, 1536:2048]
        r_t = r4[0:1, 1024:1536]
        cv = self.dma("sync", self.carow[0:1, :],
                      self.relay_h[self.MODV + modoff:self.MODV + modoff + D],
                      waits=list(ready) + [(sT, self.sT.v)])
        ca1v = self.v(lambda E: E.tensor_scalar_add(
            self.carow[0:1, :], self.carow[0:1, :], 1.0),
            waits=cv, inc=True)
        sq_done = 0
        for c in range(DCH):
            sq_done = self.s(lambda E, d=sqbuf(c), s=xb[:, c * SEG:(c + 1) * SEG]:
                             E.activation(d, s, AF.Square),
                             waits=list(ready) + self.scratch_busy, inc=True)
        mm_done = 0
        for (movf, bank) in ((lambda c: xb[:, c * SEG:(c + 1) * SEG], 0),
                             (sqbuf, 1)):
            for c in range(DCH):
                mm_done = self.t(lambda E, m=movf(c), bank=bank, c=c: E.matmul(
                    self.pw(bank)[0:1, 0:SEG], self.onescol[:, :], m,
                    start=(c == 0), stop=(c == DCH - 1)),
                    waits=[(sS, sq_done), (sV, self.init_v)] + list(ready) +
                          self.brw(0, 1), inc=(c == DCH - 1))
        self.s(lambda E: E.copy(r_sx, self.pw(0)[0:1, 0:SEG]),
               waits=[(sT, mm_done)] + self.r4_busy)
        ln_sums = self.s(lambda E: E.copy(r_sx2, self.pw(1)[0:1, 0:SEG]), inc=True)
        self.brs(0, (sS, ln_sums))
        self.brs(1, (sS, ln_sums))
        self.v(lambda E: E.tensor_scalar_mul(r_mu, r_sx, 1.0 / D),
               waits=[(sS, ln_sums)])
        self.v(lambda E: E.tensor_mul(r_t, r_mu, r_mu), waits=self.beta_busy)
        vv = self.v(lambda E: E.scalar_tensor_tensor(
            r_t, r_sx2, 1.0 / D, r_t, ALU.mult, ALU.subtract), inc=True)
        sv = self.s(lambda E: E.activation(r_t, r_t, AF.Sqrt, bias=self.kcon[0:1, 0:1]),
                    waits=[(sV, vv)], inc=True)
        self.v(lambda E: E.reciprocal(r_r, r_t), waits=[(sS, sv)])
        ln_rows = self.v(lambda E: E.tensor_mul(r_mur, r_mu, r_r), inc=True)
        h_done = 0
        for c in range(DCH):
            ca1 = self.carow[0:1, c * 128:(c + 1) * 128]
            hc = hf(c)
            xc = xb[:, c * SEG:(c + 1) * SEG]
            spp = s_pp[:, c:c + 1]
            self.t(lambda E, ca1=ca1: E.matmul(
                self.pw(2)[:, 0:SEG], ca1, r_r, start=True, stop=True),
                waits=[(sV, ln_rows), (sV, ca1v), (sV, self.mod_v)] +
                      self.brw(2, 3))
            bc_t = self.t(lambda E, ca1=ca1: E.matmul(
                self.pw(3)[:, 0:SEG], ca1, r_mur, start=True, stop=True), inc=True)
            self.v(lambda E, hc=hc, xc=xc: E.tensor_mul(
                hc, xc, self.pw(2)[:, 0:SEG]), waits=[(sT, bc_t)])
            h_done = self.v(lambda E, hc=hc, spp=spp: E.scalar_tensor_tensor(
                hc, hc, spp, self.pw(3)[:, 0:SEG], ALU.add, ALU.subtract),
                inc=True)
            self.t(lambda E: None, waits=[(sV, h_done)])
        self.brs(2, (sV, h_done))
        self.brs(3, (sV, h_done))
        return h_done

    def projection(self, w_h, hsrc, h_done, dst, dw, doff, cfo, prebuf, tl,
                   slot_i, porot, prev, dstg=()):
        sT, sV, sS, sG = self.sT, self.sV, self.sS, self.sG
        slot = self.wslot[slot_i]
        wv = self.dma("gpsimd", slot[:].rearrange("p (c m) -> p c m", c=DCH),
                      w_h[:, :].rearrange("(c p) m -> p c m", p=128),
                      waits=[(sT, self.slot_free[slot_i])])
        pre_free = [(sV, prev["conv_v"])] if prev else []
        evs = []
        for m in range(DCH):
            bank = m % 2
            mm = 0
            for kk in range(DCH):
                stat = slot[:, kk * 1024 + m * 128:kk * 1024 + (m + 1) * 128]
                mov = hsrc[:, kk * SEG:(kk + 1) * SEG]
                mm = self.t(lambda E, s=stat, m2=mov, bank=bank, kk=kk: E.matmul(
                    self.pw(bank)[:, 0:SEG], s, m2,
                    start=(kk == 0), stop=(kk == DCH - 1)),
                    waits=wv + [(sV, h_done)] +
                          ([(sS, evs[m - 2])] if m >= 2
                           else self.brw(bank)),
                    inc=(kk == DCH - 1))
            dstp = prebuf[:, m * PREW + PADC:(m + 1) * PREW]
            ev = self.s(lambda E, d=dstp, bank=bank: E.copy(
                d, self.pw(bank)[:, 0:SEG]),
                waits=[(sT, mm)] + pre_free + self.scratch_busy, inc=True)
            evs.append(ev)
            self.brs(bank, (sS, ev))
        self.slot_free[slot_i] = self.sT.v
        ev_all = evs[-1]
        prer = prebuf[:].rearrange("p (c w) -> p c w", c=DCH)
        tlr = tl[:].rearrange("p (c w) -> p c w", c=DCH)
        self.v(lambda E, a=prer[:, :, 0:PADC], b=tlr[:, :, 0:PADC]:
               E.tensor_copy(a, b), waits=[(sS, ev_all)])
        tails_v = self.v(lambda E, a=tlr[:, :, 0:PADC],
                         b=prer[:, :, PREW - PADC:PREW]:
                         E.tensor_copy(a, b), inc=True)
        conv_v = 0
        comb_g = prev["comb_g"] if prev else 0
        silu_s = prev["silu_s"] if prev else 0
        for c in range(DCH):
            rc = c % 2
            f = [self.convf[:, cfo + c * 4 + j:cfo + c * 4 + j + 1]
                 for j in range(4)]
            s0 = prebuf[:, c * PREW + PADC:(c + 1) * PREW]
            s2 = prebuf[:, c * PREW + PADC - 2:(c + 1) * PREW - 2]
            s0x = prebuf[:, c * PREW + PADC - 2:(c + 1) * PREW]
            s2x = prebuf[:, c * PREW + PADC - 4:(c + 1) * PREW - 2]
            pec = dst[:, c * dw + doff:c * dw + doff + SEG]
            pox = porot[:, rc * POW + 2:rc * POW + 4 + SEG]
            posh = porot[:, rc * POW + 3:rc * POW + 3 + SEG]
            wt = [(sS, evs[c]), (sV, tails_v), (sS, silu_s),
                  (sG, comb_g)] + list(dstg)
            self.v(lambda E, pec=pec, s0=s0, f0=f[0]:
                   E.tensor_scalar_mul(pec, s0, f0), waits=wt)
            self.v(lambda E, pec=pec, s2=s2, f2=f[2]:
                   E.scalar_tensor_tensor(pec, s2, f2, pec, ALU.mult, ALU.add))
            self.v(lambda E, pox=pox, s0x=s0x, f1=f[1]:
                   E.tensor_scalar_mul(pox, s0x, f1))
            conv_v = self.v(lambda E, pox=pox, s2x=s2x, f3=f[3]:
                            E.scalar_tensor_tensor(pox, s2x, f3, pox,
                                                   ALU.mult, ALU.add), inc=True)
            comb_g = self.g(lambda E, pec=pec, posh=posh:
                            E.tensor_tensor(pec, pec, posh, ALU.add),
                            waits=[(sV, conv_v)], inc=True)
            silu_s = self.s(lambda E, d=pec: E.activation(
                d, d, AF.Silu), waits=[(sG, comb_g)], inc=True)
        return dict(conv_v=conv_v, comb_g=comb_g, silu_s=silu_s, ev_all=ev_all)

    def segment(self, tglob, own, glob_first):
        sT, sV, sS, sG = self.sT, self.sV, self.sS, self.sG
        self.relay_off = self.relay_base + (tglob // SEG % 2) * 32768
        ctx = ExitStack()
        tag = f"_{tglob}_{int(own)}"
        sb = lambda n, s, dt=BF16: self.sbuf(n + tag, s, dt, ctx)

        q = sb("q", [128, DCH * SEG]) if own else None
        k = sb("k", [128, DCH * SEG])
        vv = sb("vv", [128, DCH * VW])
        v_tm = sb("v_tm", [128, H * NCH * 256])
        d_tm = sb("d_tm", [128, H * NCH * 256]) if own else None
        Pg = sb("Pg", [128, DCH * SEG]) if own else None
        fs_tm = sb("fs_tm", [128, H * NCH * 256]) if own else None
        fl_tm = sb("fl_tm", [128, H * NCH * 256]) if own else None
        ktmb = sb("ktmb", [128, 2 * H * 256])
        beta_tm = sb("beta_tm", [128, H * NCH], F32)
        nbeta_tm = sb("nbeta_tm", [128, H * NCH], F32)
        rq_tm = sb("rq_tm", [128, H * NCH], F32) if own else None
        powb = [sb(f"powb{h}", [128, 9 * 128]) for h in range(H)]
        uwA = [sb(f"uwA{h}", [128, 512]) for h in range(H)]
        rhs = [sb(f"rhs{h}", [128, 512]) for h in range(H)]
        ut = [sb(f"ut{h}", [128, 256]) for h in range(H)]
        wfm = [sb(f"wfm{h}", [128, 256]) for h in range(H)]
        r4 = self.r4scr
        beta = r4[0:4, 0:512]
        krow = r4[0:1, 1024:3072]

        subA = ExitStack()
        pre = self.sbuf("pre" + tag, [128, DCH * PREW], BF16, subA)
        porot = self.sbuf("porot" + tag, [128, 2 * POW], BF16, subA)

        def sq0(c):
            return pre[:, c * PREW + PADC:c * PREW + PADC + SEG]

        xb = self.sbuf("xb" + tag, [128, DCH * SEG], BF16, subA)
        h_ = self.sbuf("h" + tag, [128, DCH * SEG], BF16, subA)

        xv = self.dma("gpsimd", xb[:].rearrange("p (c t) -> p c t", c=DCH),
                      self.xT[:, tglob:tglob + SEG]
                      .rearrange("(c p) t -> p c t", p=128),
                      waits=self.subB_busy + self.stats_busy)

        h_done = self.ln_build(xb, lambda c: h_[:, c * SEG:(c + 1) * SEG],
                               D if own else D, self.sa_p, sq0, xv)
        self.maybe_dbg("h", lambda: h_[:, 0:DBG_SHAPE[1]])
        if STAGE < 2:
            self.barrier(); subA.close(); ctx.close(); return

        rq_res = None
        if own:
            rq_res = self.projection(self.wq_h, h_, h_done, q, SEG, 0, 0,
                                     pre, self.pret[0], 0, porot, None,
                                     dstg=self.drv_busy)
        rk_res = self.projection(self.wk_h, h_, h_done, k, SEG, 0, DCH * 4,
                                 pre, self.pret[1], 1, porot, rq_res,
                                 dstg=[(sT, self.ktm_busy[0])])
        rv_res = self.projection(self.wv_h, h_, h_done, vv, VW, 64, 2 * DCH * 4,
                                 pre, self.pret[2], 0, porot, rk_res,
                                 dstg=self.stats_busy)
        svv = rv_res["silu_s"]
        self.maybe_dbg("v", lambda: vv[:, 64:64 + DBG_SHAPE[1]])

        if (not own) and tglob == TOWN - SEG:
            slot = self.wslot[0]
            wv = self.dma("gpsimd", slot[:].rearrange("p (c m) -> p c m", c=DCH),
                          self.wq_h[:, :].rearrange("(c p) m -> p c m", p=128),
                          waits=[(sT, self.slot_free[0])])
            for m in range(DCH):
                mm = 0
                for kk in range(DCH):
                    stat = slot[:, kk * 1024 + m * 128:kk * 1024 + (m + 1) * 128]
                    mov = h_[:, kk * SEG + SEG - PADC:(kk + 1) * SEG]
                    mm = self.t(lambda E, s=stat, m2=mov, m3=m, kk=kk: E.matmul(
                        self.pw(m3 % 2)[:, 0:PADC], s, m2,
                        start=(kk == 0), stop=(kk == DCH - 1)),
                        waits=wv + [(sV, h_done), (sS, self.sS.v)],
                        inc=(kk == DCH - 1))
                self.s(lambda E, m=m: E.copy(
                    self.pret[0][:, m * PADC:(m + 1) * PADC],
                    self.pw(m % 2)[:, 0:PADC]), waits=[(sT, mm)], inc=True)
            self.slot_free[0] = self.sT.v

        bmm = 0
        for c in range(DCH):
            stat = self.wbW[:, c * H:(c + 1) * H]
            mov = h_[:, c * SEG:(c + 1) * SEG]
            bmm = self.t(lambda E, s=stat, m=mov, c=c: E.matmul(
                self.pw(2)[0:4, 0:SEG], s, m,
                start=(c == 0), stop=(c == DCH - 1)),
                waits=[(sV, h_done)] + self.c_g + self.brw(2),
                inc=(c == DCH - 1))
        bs = self.s(lambda E, b=beta: E.activation(b, self.pw(2)[0:4, 0:SEG],
                                                   AF.Sigmoid),
                    waits=[(sT, bmm)] + self.r4_busy, inc=True)
        self.brs(2, (sS, bs))
        bv = 0
        if not own:
            bv = self.v(lambda E, b=beta: E.tensor_scalar_mul(
                b, b, self.pfl[0:4, 0:1]),
                waits=[(sS, bs)] + self.c_sync, inc=True)
        bt_d = []
        for hh in range(H):
            bt_d = bt_d + self.relay(beta[hh:hh + 1, :],
                              beta_tm[:, hh * NCH:(hh + 1) * NCH], SEG,
                              waits=[(sS, bs), (sV, bv)] + self.btm_busy)
        beta_v = self.v(lambda E: E.tensor_scalar_mul(
            nbeta_tm[:], beta_tm[:], -1.0), waits=bt_d, inc=True)

        def l2rows(src_slices, dst_rows, ready):
            sqd = 0
            mmv = 0
            for cc in range(DCH):
                sqd = self.s(lambda E, d=sq0(cc), s=src_slices(cc):
                             E.activation(d, s, AF.Square),
                             waits=list(ready) + [(sT, mmv),
                                                  (sV, rv_res["conv_v"])],
                             inc=True)
                hh = cc // 2
                mmv = self.t(lambda E, cc=cc, hh=hh, d=sq0(cc): E.matmul(
                    self.pw(3)[0:4, 0:SEG], self.e4[:, hh * 4:(hh + 1) * 4], d,
                    start=(cc == 0), stop=(cc == DCH - 1)),
                    waits=[(sS, sqd)] + self.brw(3),
                    inc=(cc == DCH - 1))
            sv2 = self.s(lambda E, d=dst_rows: E.activation(
                d, self.pw(3)[0:4, 0:SEG], AF.Sqrt,
                bias=self.kcon[0:4, 1:2]),
                waits=[(sT, mmv)] + self.r4_busy, inc=True)
            self.brs(3, (sS, sv2))
            rdone = self.v(lambda E, d=dst_rows: E.reciprocal(d, d),
                           waits=[(sS, sv2)], inc=True)
            return rdone

        kr = r4[0:4, 0:512]
        qr = r4[0:4, 512:1024]
        kr_done = l2rows(lambda cc: k[:, cc * SEG:(cc + 1) * SEG], kr,
                         [(sS, rk_res["silu_s"]), (sV, self.sV.v)] + bt_d)
        off = self.relay_off
        self.relay_off += 2048
        wv2 = self.dma("scalar", self.relay_h[off:off + 2048], kr,
                       waits=[(sV, kr_done)])
        kr_read = wv2
        kb_d = self.dma("scalar", krow, self.relay_h[off:off + 2048],
                        waits=wv2 + self.r4_busy)
        kscaled_v = 0
        for hh in range(H):
            mov = krow[0:1, hh * SEG:(hh + 1) * SEG]
            bct = self.t(lambda E, m=mov: E.matmul(
                self.pw(2)[:, 0:SEG], self.onerow_f[:, :], m,
                start=True, stop=True),
                waits=kb_d + self.brw(2), inc=True)
            for c2 in range(2):
                cc = 2 * hh + c2
                kc = k[:, cc * SEG:(cc + 1) * SEG]
                kscaled_v = self.v(lambda E, kc=kc: E.tensor_mul(
                    kc, kc, self.pw(2)[:, 0:SEG]), waits=[(sT, bct)], inc=True)
            self.brs(2, (sV, kscaled_v))
            self.t(lambda E: None, waits=[(sV, kscaled_v)])
        ktoks = {}

        def ktm_flip(ch2, extra):
            tok = []
            for hh in range(H):
                for c2 in range(2):
                    col = ((ch2 % 2) * H + hh) * 256 + c2 * 128
                    tok = tok + self.dmaT(
                        ktmb[:, col:col + 128],
                        k[:, (2 * hh + c2) * SEG + ch2 * 128:
                          (2 * hh + c2) * SEG + (ch2 + 1) * 128],
                        waits=[(sV, kscaled_v)] + extra)
            ktoks[ch2] = tok

        for ch2 in range(min(2, NCH)):
            ktm_flip(ch2, [(sT, self.ktm_busy[0])])
        rq_d = []
        if own:
            qr_done = l2rows(lambda cc: q[:, cc * SEG:(cc + 1) * SEG], qr,
                             [(sS, rq_res["silu_s"]), (sV, kscaled_v)])
            for hh in range(H):
                rq_d = rq_d + self.relay(qr[hh:hh + 1, :],
                                  rq_tm[:, hh * NCH:(hh + 1) * NCH], SEG,
                                  waits=[(sV, qr_done)])
        self.maybe_dbg("kn", lambda: k[:, 0:DBG_SHAPE[1]])

        tm_d = []
        for hh in range(H):
            for c2 in range(2):
                cc = 2 * hh + c2
                for tc in range(NCH):
                    col = (hh * NCH + tc) * 256 + c2 * 128
                    tm_d = tm_d + self.dmaT(
                        v_tm[:, col:col + 128],
                        vv[:, cc * VW + 64 + tc * 128:cc * VW + 64 + (tc + 1) * 128],
                        waits=[(sS, svv)] + self.vtm_busy)

        gateP_done = 0
        wo_ready = 0
        if own:
            slot = self.wslot[1]
            wv = self.dma("gpsimd", slot[:].rearrange("p (c m) -> p c m", c=DCH),
                          self.w1a_h[:, :].rearrange("(c p) m -> p c m", p=128),
                          waits=[(sT, self.slot_free[1])])
            evs = []
            for m in range(DCH):
                mm = 0
                for kk in range(DCH):
                    stat = slot[:, kk * 1024 + m * 128:kk * 1024 + (m + 1) * 128]
                    mov = h_[:, kk * SEG:(kk + 1) * SEG]
                    mm = self.t(lambda E, s=stat, m2=mov, m3=m, kk=kk: E.matmul(
                        self.pw(m3 % 2)[:, 0:SEG], s, m2,
                        start=(kk == 0), stop=(kk == DCH - 1)),
                        waits=wv +
                              ([(sS, evs[m - 2])] if m >= 2
                               else self.brw(m % 2)),
                        inc=(kk == DCH - 1))
                ev = self.s(lambda E, m=m, d=Pg[:, m * SEG:(m + 1) * SEG]:
                            E.copy(d, self.pw(m % 2)[:, 0:SEG]),
                            waits=[(sT, mm)] + self.drv_busy, inc=True)
                evs.append(ev)
                self.brs(m % 2, (sS, ev))
            self.slot_free[1] = self.sT.v
            gateP_done = evs[-1]
            wov = self.dma("gpsimd",
                           self.wslot[0][:].rearrange("p (c m) -> p c m", c=DCH),
                           self.wo_h[:, :].rearrange("(c p) m -> p c m", p=128),
                           waits=[(sT, self.slot_free[0])])
            wsr = self.wslot[0][:].rearrange("p (c m) -> p c m", c=DCH)
            self.v(lambda E, a=wsr[:, 0::2, :]: E.tensor_scalar_mul(
                a, a, self.rmsw[:, 0:1]), waits=wov + self.c_sync)
            wo_ready = self.v(lambda E, a=wsr[:, 1::2, :]: E.tensor_scalar_mul(
                a, a, self.rmsw[:, 1:2]), inc=True)

        if STAGE < 3:
            self.barrier(); subA.close(); ctx.close(); return
        self.subB_busy = [(sT, self.sT.v), (sS, self.sS.v)]

        if STAGE < 4:
            self.barrier(); subA.close(); ctx.close(); return
        for ch in range(NCH):
            gfirst = glob_first and ch == 0
            kf = [[k[:, (2 * hh + c2) * SEG + ch * 128:
                     (2 * hh + c2) * SEG + (ch + 1) * 128]
                   for c2 in range(2)] for hh in range(H)]
            if own:
                qf = [[q[:, (2 * hh + c2) * SEG + ch * 128:
                         (2 * hh + c2) * SEG + (ch + 1) * 128]
                       for c2 in range(2)] for hh in range(H)]
            vtm = [v_tm[:, (hh * NCH + ch) * 256:(hh * NCH + ch + 1) * 256]
                   for hh in range(H)]
            ktm = [ktmb[:, ((ch % 2) * H + hh) * 256:
                        ((ch % 2) * H + hh + 1) * 256] for hh in range(H)]
            bpp = [beta_tm[:, hh * NCH + ch:hh * NCH + ch + 1] for hh in range(H)]

            def ce(hh):
                return (self.s, sS) if hh % 2 == 0 else (self.v, sV)

            np_v = []
            for hh in range(H):
                bA = hh
                w0 = [(sV, kscaled_v), (sV, beta_v)] + self.brw(bA)
                self.t(lambda E, a=kf[hh][0], bA=bA: E.matmul(
                    self.pw(bA, 128)[:, :], a, a, start=True, stop=False),
                    waits=w0)
                gr = self.t(lambda E, a=kf[hh][1], bA=bA: E.matmul(
                    self.pw(bA, 128)[:, :], a, a, start=False, stop=True),
                    inc=True)
                npp0 = powb[hh][:, 7 * 128:8 * 128]
                np_v.append(self.v(
                    lambda E, d=npp0, bA=bA, b=bpp[hh]:
                    E.scalar_tensor_tensor(d, self.pw(bA, 128)[:, :], b,
                                           self.neglow[:], ALU.mult, ALU.mult),
                    waits=[(sT, gr)], inc=True))
                self.brs(bA, (sV, np_v[hh]))
            if DSTAGE < 2:
                continue
            mp_s = []
            for hh in range(H):
                bB = 4 + hh
                eng, esem = ce(hh)
                pb = self.pw(bB, 128).bitcast(BF16)[:, 0:128]
                tt = self.t(lambda E, s=powb[hh][:, 7 * 128:8 * 128], pb=pb:
                            E.transpose(pb, s, self.iden[:]),
                            waits=[(sV, np_v[hh])] + self.brw(bB),
                            inc=True)
                mp_s.append(eng(lambda E, d=powb[hh][:, 0:128], pb=pb:
                                _cp(E, d, pb), waits=[(sT, tt)],
                                inc=True))
                self.brs(bB, (esem, mp_s[hh]))
            if DSTAGE < 3:
                continue
            ch_s = [[(ce(h)[1], mp_s[h])] for h in range(H)]
            for j in range(1, 7):
                for hh in range(H):
                    bA = hh
                    eng, esem = ce(hh)
                    n0 = powb[hh][:, (7 + (j - 1) % 2) * 128:
                                  (8 + (j - 1) % 2) * 128]
                    m0 = powb[hh][:, (j - 1) * 128:j * 128]
                    w0 = ch_s[hh] + ([(sV, np_v[hh])] if j == 1 else [])
                    self.t(lambda E, n0=n0, m0=m0, bA=bA: E.matmul(
                        self.pw(bA, 128)[:, :], m0, n0, start=True,
                        stop=False), waits=w0)
                    tt = self.t(lambda E, n0=n0, m0=m0, bA=bA: E.matmul(
                        self.pw(bA, 128, off=128)[:, :], n0, m0,
                        start=False, stop=True), inc=True)
                    eng(lambda E, d=powb[hh][:, (7 + j % 2) * 128:
                                             (8 + j % 2) * 128], bA=bA:
                        _cp(E, d, self.pw(bA, 128)[:, :]),
                        waits=[(sT, tt)])
                    cs2 = eng(
                        lambda E, d=powb[hh][:, j * 128:(j + 1) * 128],
                        bA=bA: _cp(E, d, self.pw(bA, 128, off=128)[:, :]),
                        inc=True)
                    ch_s[hh] = [(esem, cs2)]
            if DSTAGE < 4:
                continue
            sol_s = []
            for hh in range(H):
                bB = 4 + hh
                eng, esem = ce(hh)
                self.v(lambda E, d=rhs[hh][:, 0:256], s=vtm[hh], b=bpp[hh]:
                       E.tensor_scalar_mul(d, s, b),
                       waits=tm_d + ktoks[ch] + [(sV, beta_v),
                              (sT, self.rhs_busy[hh])])
                rhs_v = self.v(lambda E, d=rhs[hh][:, 256:512], s=ktm[hh],
                               b=bpp[hh]: E.tensor_scalar_mul(d, s, b),
                               inc=True)
                self.t(lambda E, bB=bB, r=rhs[hh]: E.matmul(
                    self.pw(bB)[:, :], self.iden[:], r[:], start=True,
                    stop=False, skip_group_check=True),
                    waits=[(sV, rhs_v), (esem, mp_s[hh])] + ch_s[hh])
                cur = rhs[hh][:]
                ev = []
                tt = 0
                for fj in range(6, -1, -1):
                    stat = powb[hh][:, fj * 128:(fj + 1) * 128]
                    tt = self.t(lambda E, s=stat, cur=cur, bB=bB, fj=fj:
                                E.matmul(self.pw(bB)[:, :], s, cur,
                                         start=False, stop=(fj == 0),
                                         skip_group_check=True),
                                waits=ev, inc=True)
                    if fj > 0:
                        nxt = (uwA[hh] if fj % 2 == 0 else rhs[hh])[:]
                        ev = [(esem, eng(lambda E, d=nxt, bB=bB: _cp(
                            E, d, self.pw(bB)[:, :]),
                            waits=[(sT, tt)], inc=True))]
                        cur = nxt
                self.rhs_busy[hh] = tt
                eng(lambda E, d=uwA[hh][:, 0:256], bB=bB: _cp(
                    E, d, self.pw(bB, 256)[:, :]), waits=[(sT, tt)])
                if hh % 2 == 0:
                    sv = self.s(lambda E, d=uwA[hh][:, 256:512], bB=bB:
                                E.activation(d, self.pw(bB, 256, off=256)[:, :],
                                             AF.Identity, scale=-1.0),
                                inc=True)
                else:
                    sv = self.v(lambda E, d=uwA[hh][:, 256:512], bB=bB:
                                E.tensor_scalar_mul(
                                    d, self.pw(bB, 256, off=256)[:, :], -1.0),
                                inc=True)
                sol_s.append((esem, sv))
            if DSTAGE < 5:
                continue
            wfm_d = []
            for hh in range(H):
                dd = []
                for c2 in range(2):
                    dd = dd + self.dmaT(
                        wfm[hh][:, c2 * 128:(c2 + 1) * 128],
                        uwA[hh][:, 256 + c2 * 128:256 + (c2 + 1) * 128],
                        waits=[sol_s[hh]])
                wfm_d.append(dd)
            for hh in range(H):
                bA = hh
                bB = 4 + hh
                eng, esem = ce(hh)
                if not gfirst:
                    self.t(lambda E, bA=bA, w0=wfm[hh][:, 0:128],
                           s0=self.S_sb[hh][:, 0:256]: E.matmul(
                        self.pw(bA, 256)[:, :], w0, s0, start=True, stop=False),
                        waits=wfm_d[hh] + self.scp_tok[hh] + ch_s[hh])
                    self.t(lambda E, bA=bA, w1=wfm[hh][:, 128:256],
                           s1=self.S_sb[hh][:, 256:512]: E.matmul(
                        self.pw(bA, 256)[:, :], w1, s1, start=False,
                        stop=False))
                    utt = self.t(lambda E, bA=bA, u=uwA[hh][:, 0:256]:
                                 E.matmul(self.pw(bA, 256)[:, :],
                                          self.iden[:], u,
                                          start=False, stop=True),
                                 waits=[sol_s[hh]], inc=True)
                else:
                    utt = self.t(lambda E, bA=bA, u=uwA[hh][:, 0:256]:
                                 E.matmul(self.pw(bA, 256)[:, :],
                                          self.iden[:], u,
                                          start=True, stop=True),
                                 waits=[sol_s[hh]] + ch_s[hh], inc=True)
                ut_s = eng(lambda E, d=ut[hh][:], bA=bA: _cp(
                    E, d, self.pw(bA, 256)[:, :]), waits=[(sT, utt)], inc=True)
                self.brs(bA, (esem, ut_s))
                d_s = None
                if own and DSTAGE >= 6:
                    alt_h = powb[hh][:, 8 * 128:9 * 128]
                    self.t(lambda E, a=kf[hh][0], b=qf[hh][0], bB=bB:
                           E.matmul(self.pw(bB, 128)[:, :], a, b,
                                    start=True, stop=False),
                           waits=[sol_s[hh], (sS, rq_res["silu_s"])])
                    at = self.t(lambda E, a=kf[hh][1], b=qf[hh][1], bB=bB:
                                E.matmul(self.pw(bB, 128)[:, :], a, b,
                                         start=False, stop=True), inc=True)
                    alt_v = self.v(lambda E, d=alt_h, bB=bB:
                                   E.tensor_mul(d, self.pw(bB, 128)[:, :],
                                                self.upin[:]),
                                   waits=[(sT, at)], inc=True)
                    if not gfirst:
                        self.t(lambda E, a=qf[hh][0], s0=self.S_sb[hh][:, 0:256],
                               bB=bB: E.matmul(
                            self.pw(bB, 256)[:, :], a, s0,
                            start=True, stop=False),
                            waits=[(sV, alt_v)] + self.scp_tok[hh])
                        self.t(lambda E, a=qf[hh][1], s1=self.S_sb[hh][:, 256:512],
                               bB=bB: E.matmul(
                            self.pw(bB, 256)[:, :], a, s1,
                            start=False, stop=False))
                        ot = self.t(lambda E, a=alt_h, u=ut[hh][:],
                                    bB=bB: E.matmul(
                            self.pw(bB, 256)[:, :], a, u,
                            start=False, stop=True),
                            waits=[(esem, ut_s)], inc=True)
                    else:
                        ot = self.t(lambda E, a=alt_h, u=ut[hh][:],
                                    bB=bB: E.matmul(
                            self.pw(bB, 256)[:, :], a, u,
                            start=True, stop=True),
                            waits=[(sV, alt_v), (esem, ut_s)], inc=True)
                    dcol = d_tm[:, (hh * NCH + ch) * 256:(hh * NCH + ch + 1) * 256]
                    rqs = rq_tm[:, hh * NCH + ch:hh * NCH + ch + 1]
                    if hh % 2 == 0:
                        d_s = (sS, self.s(lambda E, d=dcol, rqs=rqs, bB=bB:
                                          E.activation(d, self.pw(bB, 256)[:, :],
                                                       AF.Identity, scale=rqs),
                                          waits=[(sT, ot)] + rq_d +
                                                self.dtm_busy, inc=True))
                    else:
                        d_s = (sV, self.v(lambda E, d=dcol, rqs=rqs, bB=bB:
                                          E.tensor_scalar_mul(
                                              d, self.pw(bB, 256)[:, :], rqs),
                                          waits=[(sT, ot)] + rq_d +
                                                self.dtm_busy, inc=True))
                if DSTAGE < 7:
                    continue
                sw = [(esem, ut_s)] + ktoks[ch] + \
                     ([d_s] if d_s is not None else [(sV, alt_v)]
                      if (own and DSTAGE >= 6) else [sol_s[hh]])
                if not gfirst:
                    self.t(lambda E, bB=bB, s0=self.S_sb[hh][:, 0:256]:
                           E.matmul(self.pw(bB, 256)[:, :], self.iden[:], s0,
                                    start=True, stop=False),
                           waits=sw + self.scp_tok[hh])
                    self.t(lambda E, s=ktm[hh][:, 0:128], u=ut[hh][:], bB=bB:
                           E.matmul(self.pw(bB, 256)[:, :], s, u,
                                    start=False, stop=False))
                    self.t(lambda E, bB=bB, s1=self.S_sb[hh][:, 256:512]:
                           E.matmul(self.pw(bB, 256, off=256)[:, :],
                                    self.iden[:], s1, start=False, stop=False))
                    st2 = self.t(lambda E, s=ktm[hh][:, 128:256], u=ut[hh][:],
                                 bB=bB: E.matmul(
                        self.pw(bB, 256, off=256)[:, :], s, u,
                        start=False, stop=True), inc=True)
                else:
                    self.t(lambda E, s=ktm[hh][:, 0:128], u=ut[hh][:], bB=bB:
                           E.matmul(self.pw(bB, 256)[:, :], s, u,
                                    start=True, stop=False), waits=sw)
                    st2 = self.t(lambda E, s=ktm[hh][:, 128:256], u=ut[hh][:],
                                 bB=bB: E.matmul(
                        self.pw(bB, 256, off=256)[:, :], s, u,
                        start=False, stop=True), inc=True)
                sc2 = eng(lambda E, hh=hh, bB=bB: _cp(
                    E, self.S_sb[hh][:], self.pw(bB)[:, :]),
                    waits=[(sT, st2)], inc=True)
                self.scp_tok[hh] = [(esem, sc2)]
                self.brs(bB, (esem, sc2))
            if ch + 2 < NCH:
                ktm_flip(ch + 2, [(sT, self.sT.v)])
        self.ktm_busy[0] = self.sT.v
        self.btm_busy = [(sV, self.sV.v)]
        if HBAR & 1:
            self.barrier()

        if own:
            self.maybe_dbg("dtm", lambda: d_tm[:, 0:DBG_SHAPE[1]])
        if STAGE < 5 and own:
            self.barrier(); subA.close(); ctx.close(); return
        if not own:
            self.v(lambda E,
                   a=self.vtail[:].rearrange("p (c w) -> p c w", c=DCH),
                   b=vv[:].rearrange("p (c w) -> p c w", c=DCH)[:, :, SEG:VW]:
                   E.tensor_copy(a, b), waits=[(sS, svv)], inc=True)
            self.r4_busy = [(sT, self.sT.v)] + bt_d + kr_read
            self.beta_busy = bt_d
            self.scratch_busy = [(sT, self.sT.v)]
            self.btm_busy = [(sV, self.sV.v)]
            subA.close()
            ctx.close()
            return

        subC = ExitStack()
        fl = self.sbuf("fl" + tag, [128, DCH * SEG], BF16, subC)
        fs = self.sbuf("fs" + tag, [128, DCH * SEG], BF16, subC)
        d_fm = xb
        drv = q[0:36, 0:2048]
        mu2b = q[0:4, 2048:4096]
        stsb = drv

        halo_v = 0
        for c in range(DCH):
            halo_v = self.v(lambda E, a=vv[:, c * VW:c * VW + 64],
                            b=self.vtail[:, c * 64:(c + 1) * 64]:
                            E.tensor_copy(a, b), waits=[(sS, svv)], inc=True)
        fir_g = rv_res["comb_g"]
        fir_v = 0
        for c in range(DCH):
            rc = c % 2
            pox = porot[:, rc * POW + 2:rc * POW + 4 + SEG]
            posh = porot[:, rc * POW + 3:rc * POW + 3 + SEG]
            flc = fl[:, c * SEG:(c + 1) * SEG]
            fsc = fs[:, c * SEG:(c + 1) * SEG]
            wt = [(sS, svv), (sG, fir_g), (sV, halo_v)] + tm_d + \
                 self.fir_guard
            first = True
            for j in range(0, 64, 2):
                fj = self.firlf[:, c * 64 + j:c * 64 + j + 1]
                s = vv[:, c * VW + 64 - j:c * VW + 64 - j + SEG]
                if first:
                    self.v(lambda E, d=flc, s=s, fj=fj:
                           E.tensor_scalar_mul(d, s, fj), waits=wt)
                    first = False
                else:
                    self.v(lambda E, d=flc, s=s, fj=fj:
                           E.scalar_tensor_tensor(d, s, fj, d,
                                                  ALU.mult, ALU.add))
            first = True
            for j in range(1, 64, 2):
                fj = self.firlf[:, c * 64 + j:c * 64 + j + 1]
                s = vv[:, c * VW + 62 - (j - 1):c * VW + 64 - (j - 1) + SEG]
                if first:
                    self.v(lambda E, d=pox, s=s, fj=fj:
                           E.tensor_scalar_mul(d, s, fj), waits=wt)
                    first = False
                else:
                    self.v(lambda E, d=pox, s=s, fj=fj:
                           E.scalar_tensor_tensor(d, s, fj, d,
                                                  ALU.mult, ALU.add))
            flv = self.v(lambda E: None, inc=True)
            flg = self.g(lambda E, d=flc, posh=posh:
                         E.tensor_tensor(d, d, posh, ALU.add),
                         waits=[(sV, flv)], inc=True)
            self.v(lambda E, d=fsc, s=vv[:, c * VW + 64:c * VW + 64 + SEG],
                   fj=self.firsf[:, c * 5:c * 5 + 1]:
                   E.tensor_scalar_mul(d, s, fj), waits=[(sG, flg)])
            for j in (2, 4):
                self.v(lambda E, d=fsc,
                       s=vv[:, c * VW + 64 - j:c * VW + 64 - j + SEG],
                       fj=self.firsf[:, c * 5 + j:c * 5 + j + 1]:
                       E.scalar_tensor_tensor(d, s, fj, d, ALU.mult, ALU.add))
            self.v(lambda E, d=pox, s=vv[:, c * VW + 62:c * VW + 64 + SEG],
                   fj=self.firsf[:, c * 5 + 1:c * 5 + 2]:
                   E.tensor_scalar_mul(d, s, fj))
            fsv = self.v(lambda E, d=pox,
                         s=vv[:, c * VW + 60:c * VW + 62 + SEG],
                         fj=self.firsf[:, c * 5 + 3:c * 5 + 4]:
                         E.scalar_tensor_tensor(d, s, fj, d,
                                                ALU.mult, ALU.add), inc=True)
            fir_g = self.g(lambda E, d=fsc, posh=posh:
                           E.tensor_tensor(d, d, posh, ALU.add),
                           waits=[(sV, fsv)], inc=True)
            fir_v = fsv
        vtl = self.v(lambda E,
                     a=self.vtail[:].rearrange("p (c w) -> p c w", c=DCH),
                     b=vv[:].rearrange("p (c w) -> p c w", c=DCH)[:, :, SEG:VW]:
                     E.tensor_copy(a, b),
                     waits=[(sS, svv), (sG, fir_g)], inc=True)
        if HBAR & 2:
            self.barrier()
        self.maybe_dbg("fl", lambda: fl[:, 0:DBG_SHAPE[1]])

        if OSTAGE < 2:
            self.barrier(); subC.close(); subA.close(); ctx.close(); return
        post_tm = []
        for hh in range(H):
            for c2 in range(2):
                cc = 2 * hh + c2
                for tc in range(NCH):
                    col = (hh * NCH + tc) * 256 + c2 * 128
                    fcol = cc * SEG + tc * 128
                    post_tm = post_tm + self.dmaT(
                        fs_tm[:, col:col + 128], fs[:, fcol:fcol + 128],
                        waits=[(sG, fir_g)])
                    post_tm = post_tm + self.dmaT(
                        fl_tm[:, col:col + 128], fl[:, fcol:fcol + 128])
                    post_tm = post_tm + self.dmaT(
                        d_fm[:, fcol:fcol + 128], d_tm[:, col:col + 128],
                        waits=[(sS, self.sS.v), (sV, self.sV.v),
                               (sT, self.sT.v)])

        if OSTAGE < 3:
            self.barrier(); subC.close(); subA.close(); ctx.close(); return
        streams = [lambda cc: fs[:, cc * SEG:(cc + 1) * SEG],
                   lambda cc: fl[:, cc * SEG:(cc + 1) * SEG],
                   lambda cc: d_fm[:, cc * SEG:(cc + 1) * SEG],
                   lambda cc: vv[:, cc * VW + 64:cc * VW + 64 + SEG]]
        stat_w = [(sG, fir_g), (sV, fir_v), (sS, svv)] + post_tm
        stP = self.PW[0:36, :]
        first_in_bank = [True] * H
        for st_i in range(4):
            stf = streams[st_i]
            for hh in range(H):
                for c2 in range(2):
                    cc = 2 * hh + c2
                    fib = first_in_bank[hh]
                    first_in_bank[hh] = False
                    self.t(lambda E, st_i=st_i, m=stf(cc), hh=hh, fib=fib:
                           E.matmul(stP[:, hh * SEG:(hh + 1) * SEG],
                                    self.estat[:, st_i * 36:st_i * 36 + 36], m,
                                    start=fib, stop=False),
                           waits=stat_w + (self.brw(hh) if fib else []))
        sq_t = 0
        ab_t = 0
        for st_i in range(4):
            stf = streams[st_i]
            for hh in range(H):
                for c2 in range(2):
                    cc = 2 * hh + c2
                    rc = cc % 2
                    sqd = self.s(lambda E, d=sq0(cc), m=stf(cc):
                                 E.activation(d, m, AF.Square),
                                 waits=stat_w + [(sT, sq_t)], inc=True)
                    sq_t = self.t(lambda E, st_i=st_i, hh=hh, d=sq0(cc):
                                  E.matmul(stP[:, hh * SEG:(hh + 1) * SEG],
                                           self.estat[:, (8 + st_i) * 36:
                                                      (8 + st_i) * 36 + 36],
                                           d, start=False, stop=False),
                                  waits=[(sS, sqd)], inc=True)
                    ab = porot[:, rc * POW + 2:rc * POW + 2 + SEG]
                    abd = self.v(lambda E, d=ab, m=stf(cc):
                                 E.scalar_tensor_tensor(d, m, -1.0, m,
                                                        ALU.mult, ALU.max),
                                 waits=stat_w + [(sT, ab_t)], inc=True)
                    last = (st_i == 3 and c2 == 1)
                    ab_t = self.t(lambda E, st_i=st_i, hh=hh, d=ab, last=last:
                                  E.matmul(stP[:, hh * SEG:(hh + 1) * SEG],
                                           self.estat[:, (4 + st_i) * 36:
                                                      (4 + st_i) * 36 + 36],
                                           d, start=False, stop=last),
                                  waits=[(sV, abd)], inc=True)
        stat_t = self.sT.v
        self.scratch_busy = [(sT, stat_t)]
        self.stats_busy = [(sT, stat_t), (sS, self.sS.v), (sV, self.sV.v)]
        stv = self.s(lambda E: E.copy(stsb, stP[0:36, 0:4 * SEG]),
                     waits=[(sT, stat_t)], inc=True)
        for b in range(4):
            self.brs(b, (sS, stv))
        drv_v = self.v(lambda E: E.tensor_scalar_mul(
            drv[0:12, 0:2048], drv[0:12, 0:2048], 1.0 / DV),
            waits=[(sS, stv)], inc=True)
        self.s(lambda E: E.activation(mu2b, drv[0:4, 0:2048], AF.Square),
               waits=[(sV, drv_v)])
        drv_s = self.s(lambda E: E.activation(drv[32:36, 0:2048],
                                              drv[32:36, 0:2048], AF.Sqrt),
                       waits=[(sS, stv)], inc=True)
        off_l2 = self.relay_off
        self.relay_off += 8192
        l2w = self.dma("gpsimd", self.relay_h[off_l2:off_l2 + 8192],
                       drv[32:36, 0:2048], waits=[(sS, drv_s)])
        l2_d = self.dma("gpsimd", drv[12:16, 0:2048],
                        self.relay_h[off_l2:off_l2 + 8192]
                        .rearrange("(p t) -> p t", p=4),
                        waits=l2w)
        if HBAR & 4:
            self.barrier()
        self.maybe_dbg("drv", lambda: drv[0:16, 0:DBG_SHAPE[1]])
        if OSTAGE < 4:
            self.barrier(); subC.close(); subA.close(); ctx.close(); return
        subC.close()

        subD = ExitStack()
        h1 = self.sbuf("h1" + tag, [128, DCH * SEG], BF16, subD)
        p_tm = self.sbuf("p_tm" + tag, [128, NCH * 16], F32, subD)
        mx = self.sbuf("mx" + tag, [128, NCH * 4], F32, subD)
        osq = self.sbuf("osq" + tag, [128, 256], BF16, subD)
        osc = self.sbuf("osc" + tag, [128, 256], BF16, subD)
        osum = self.sbuf("osum" + tag, [128, 4], F32, subD)
        lbuf = r4[0:4, 0:2048]

        lg_s = []
        for hh in range(H):
            evs = []
            for m in range(DCH):
                self.t(lambda E, m=m, hh=hh,
                       s=self.w1b[0:16, m * 128:(m + 1) * 128],
                       mo=drv[0:16, hh * SEG:(hh + 1) * SEG]: E.matmul(
                    self.pw(m % 2)[:, 0:SEG], s, mo, start=True, stop=False),
                    waits=[(sV, drv_v), (sS, drv_s), (sS, gateP_done)] +
                          l2_d +
                          ([(sS, evs[m - 2])] if m >= 2
                           else self.brw(m % 2)))
                if not (GSKIP & 2):
                    self.t(lambda E, m=m, hh=hh,
                           s=self.w1bv[0:4, m * 128:(m + 1) * 128],
                           mo=mu2b[0:4, hh * SEG:(hh + 1) * SEG]: E.matmul(
                        self.pw(m % 2)[:, 0:SEG], s, mo, start=False,
                        stop=False))
                mm = self.t(lambda E, m=m, mo=Pg[:, m * SEG:(m + 1) * SEG]:
                            E.matmul(self.pw(m % 2)[:, 0:SEG], self.iden[:],
                                     mo, start=False, stop=True), inc=True)
                ev = self.s(lambda E, m=m, d=h1[:, m * SEG:(m + 1) * SEG]:
                            E.activation(d, self.pw(m % 2)[:, 0:SEG],
                                         AF.Gelu,
                                         bias=self.b1s[:, m:m + 1]),
                            waits=[(sT, mm)] + self.h1_busy, inc=True)
                evs.append(ev)
                self.brs(m % 2, (sS, ev))
            lmm = 0
            for c in range(DCH):
                lmm = self.t(lambda E, c=c, hh=hh,
                             s=self.w2s[:, c * H:(c + 1) * H],
                             mo=h1[:, c * SEG:(c + 1) * SEG]: E.matmul(
                    self.pw(2 + hh % 2)[0:4, 0:SEG], s, mo,
                    start=(c == 0), stop=(c == DCH - 1)),
                    waits=[(sS, evs[-1])] + self.brw(2 + hh % 2),
                    inc=(c == DCH - 1))
            lg = self.s(lambda E, hh=hh,
                        d=lbuf[0:4, hh * SEG:(hh + 1) * SEG]:
                        E.copy(d, self.pw(2 + hh % 2)[0:4, 0:SEG]),
                        waits=[(sT, lmm)] + self.r4_busy + rq_d + kr_read,
                        inc=True)
            lg_s.append(lg)
            self.brs(2 + hh % 2, (sS, lg))
        logits_s = self.sS.v
        self.maybe_dbg("lg", lambda: lbuf[0:4, 0:DBG_SHAPE[1]])
        if OSTAGE < 5:
            self.barrier(); subD.close(); subA.close(); ctx.close(); return
        p_d = []
        for hh in range(H):
            for comp in range(4):
                dst = p_tm[:, :].rearrange("p (t g) -> p t g", g=16)[:, :,
                                                                    hh * 4 + comp]
                p_d = p_d + self.relay(
                    lbuf[comp:comp + 1, hh * SEG:(hh + 1) * SEG],
                    dst, SEG, waits=[(sS, logits_s)] + self.dtm_busy)
        self.maybe_dbg("ptm0", lambda: p_tm[:, 0:DBG_SHAPE[1]])
        v44 = p_tm[:, :].rearrange("p (t h c) -> p t h c", h=4, c=4)
        v16 = p_tm[:, :].rearrange("p (t g) -> p t g", g=16)
        mxv = mx[:, :].rearrange("p (t h) -> p t h", h=4)
        mxb = mx[:, :].rearrange("p (t h o) -> p t h o", h=4, o=1) \
            .to_broadcast([128, NCH, 4, 4])
        self.v(lambda E: E.tensor_add(
            v16, v16, self.bbrep[:, :].rearrange("p (o g) -> p o g", o=1)
            .to_broadcast([128, NCH, 16])), waits=p_d)
        self.v(lambda E: E.tensor_mul(
            v16, v16, self.invtemp[:, :].rearrange("p (o g) -> p o g", o=1)
            .to_broadcast([128, NCH, 16])))
        f1 = self.v(lambda E: E.reduce_max(mxv, v44, axis=AX.X), inc=True)
        sv3 = self.v(lambda E: E.tensor_sub(v44, v44, mxb),
                     waits=[(sV, f1)], inc=True)
        se = self.s(lambda E: E.activation(p_tm[:], p_tm[:], AF.Exp),
                    waits=[(sV, sv3)], inc=True)
        f2 = self.v(lambda E: E.reduce_sum(mxv, v44, axis=AX.X),
                    waits=[(sS, se)], inc=True)
        f3 = self.v(lambda E: E.reciprocal(mx[:], mx[:]),
                    waits=[(sV, f2)], inc=True)
        self.v(lambda E: E.tensor_mul(v44, v44, mxb), waits=[(sV, f3)])
        self.v(lambda E: E.tensor_max(
            v16, v16, self.epsc[:, :].rearrange("p (o g) -> p o g", o=1)
            .to_broadcast([128, NCH, 16])))
        f4 = self.v(lambda E: E.reduce_sum(mxv, v44, axis=AX.X), inc=True)
        f5 = self.v(lambda E: E.reciprocal(mx[:], mx[:]),
                    waits=[(sV, f4)], inc=True)
        self.v(lambda E: E.tensor_mul(v44, v44, mxb), waits=[(sV, f5)])
        p_done = self.v(lambda E: E.tensor_add(
            v44[:, :, :, 0], v44[:, :, :, 0],
            self.crlrep[:, :].rearrange("p (o h) -> p o h", o=1)
            .to_broadcast([128, NCH, 4])), inc=True)
        if HBAR & 8:
            self.barrier()
        self.maybe_dbg("p", lambda: p_tm[:, 0:DBG_SHAPE[1]])

        if OSTAGE < 6:
            self.barrier(); subD.close(); subA.close(); ctx.close(); return
        omn_s = []
        for hh in range(H):
            for tc in range(NCH):
                col = (hh * NCH + tc) * 256
                pj = [p_tm[:, tc * 16 + hh * 4 + c:tc * 16 + hh * 4 + c + 1]
                      for c in range(4)]
                vtile = v_tm[:, col:col + 256]
                self.v(lambda E, s=fs_tm[:, col:col + 256], p=pj[0]:
                       E.tensor_scalar_mul(osc[:], s, p),
                       waits=[(sV, p_done),
                              (sS, self.sS.v)] + post_tm)
                self.v(lambda E, s=fl_tm[:, col:col + 256], p=pj[1]:
                       E.scalar_tensor_tensor(osc[:], s, p, osc[:],
                                              ALU.mult, ALU.add))
                self.v(lambda E, s=d_tm[:, col:col + 256], p=pj[2]:
                       E.scalar_tensor_tensor(osc[:], s, p, osc[:],
                                              ALU.mult, ALU.add))
                ov = self.v(lambda E, vt=vtile, p=pj[3]:
                            E.scalar_tensor_tensor(vt, vt, p, osc[:],
                                                   ALU.mult, ALU.add),
                            inc=True)
                sq1 = self.s(lambda E, vt=vtile: E.activation(
                    osq[:], vt, AF.Square, accum_out=osum[:, 0:1]),
                    waits=[(sV, ov)], inc=True)
                ss2 = self.s(lambda E: E.activation(
                    osum[:, 1:2], osum[:, 0:1], AF.Sqrt, scale=1.0 / DV,
                    bias=self.kcon[:, 0:1]), waits=[(sS, sq1)], inc=True)
                rv2 = self.v(lambda E: E.reciprocal(osum[:, 2:3], osum[:, 1:2]),
                             waits=[(sS, ss2)], inc=True)
                omn_s.append(self.s(lambda E, vt=vtile: E.activation(
                    vt, vt, AF.Identity, scale=osum[:, 2:3]),
                    waits=[(sV, rv2)], inc=True))
        if HBAR & 16:
            self.barrier()
        om_fm = h1
        omfm_d = []
        for hh in range(H):
            for c2 in range(2):
                cc = 2 * hh + c2
                for tc in range(NCH):
                    col = (hh * NCH + tc) * 256 + c2 * 128
                    fcol = cc * SEG + tc * 128
                    omfm_d = omfm_d + self.dmaT(om_fm[:, fcol:fcol + 128],
                                       v_tm[:, col:col + 128],
                                       waits=[(sS, omn_s[hh * NCH + tc])])
        self.maybe_dbg("omfm", lambda: om_fm[:, 0:DBG_SHAPE[1]])

        if OSTAGE < 7:
            self.barrier(); subD.close(); subA.close(); ctx.close(); return
        oo = tglob % TOWN
        for m in range(DCH):
            xv2 = self.dma("sync", self.xf32[m % 2][:],
                           self.xT[m * 128:(m + 1) * 128, tglob:tglob + SEG],
                           waits=[(sV, self.xf_free[m % 2])])
            mm = 0
            for kk in range(DCH):
                stat = self.wslot[0][:, kk * 1024 + m * 128:
                                     kk * 1024 + (m + 1) * 128]
                mov = om_fm[:, kk * SEG:(kk + 1) * SEG]
                mm = self.t(lambda E, s=stat, mo=mov, m3=m, kk=kk: E.matmul(
                    self.pw(m3 % 2)[:, 0:SEG], s, mo,
                    start=(kk == 0), stop=(kk == DCH - 1)),
                    waits=omfm_d + [(sV, wo_ready)] + self.brw(m % 2),
                    inc=(kk == DCH - 1))
            sv4 = self.v(lambda E, m=m: E.scalar_tensor_tensor(
                self.xmo[m % 2][:], self.pw(m % 2)[:, 0:SEG],
                self.ga_p[:, m:m + 1], self.xf32[m % 2][:], ALU.mult, ALU.add),
                waits=[(sT, mm)] + xv2 + self.xmo_free[m % 2],
                inc=True)
            self.xf_free[m % 2] = sv4
            self.brs(m % 2, (sV, sv4))
            self.xmo_free[m % 2] = self.dma(
                "sync", self.xmid_h[m * 128:(m + 1) * 128, oo:oo + SEG],
                self.xmo[m % 2][:], waits=[(sV, sv4)])
        self.slot_free[0] = self.sT.v
        self.drv_busy = [(sT, self.sT.v)]
        self.h1_busy = [(sT, self.sT.v)]
        self.r4_busy = [(sT, self.sT.v)] + p_d + rq_d + kr_read + bt_d
        self.beta_busy = bt_d + [(sV, bv)] if not own else bt_d
        self.dtm_busy = [(sV, self.sV.v)] + post_tm
        self.vtm_busy = omfm_d
        self.fir_guard = self.stats_busy + post_tm
        subD.close()
        subA.close()
        ctx.close()

    def ffn(self):
        sT, sV, sS, sG = self.sT, self.sV, self.sS, self.sG
        self.relay_off = self.relay_base + 2 * 32768
        ctx = ExitStack()
        sb = lambda n, s, dt=BF16: self.sbuf(n + "_f", s, dt, ctx)
        h2 = sb("h2", [128, DCH * TOWN])
        gsc = sb("gsc", [NE, TOWN], F32)
        gfb = sb("gfb", [NE, TOWN])
        growb = sb("growb", [1, SEG], F32)
        g_tm = sb("g_tm", [128, 16 * NE], F32)
        hs = sb("hs", [128, (HF // 128) * SEG])
        h1e = sb("h1e", [128, NE * (HF // 128) * SEG])
        wsl = [sb(f"wsl{i}", [128, (HF // 128) * 1024]) for i in range(2)]
        gb = sb("gbc", [128, SEG])

        self.gro_free = 0
        sub = ExitStack()
        xb2 = self.sbuf("xb2_f", [128, DCH * SEG], BF16, sub)
        sq2 = self.sbuf("sq2_f", [128, DCH * SEG], BF16, sub)
        for ti in range(4):
            xv = self.dma("gpsimd", xb2[:].rearrange("p (c t) -> p c t", c=DCH),
                          self.xmid_h[:, ti * SEG:(ti + 1) * SEG]
                          .rearrange("(c p) t -> p c t", p=128),
                          waits=self.snap() if ti == 0 else
                          [(sT, self.sT.v), (sV, self.sV.v)])
            self.r4_busy = [(sT, self.sT.v), (sV, self.sV.v)]
            self.scratch_busy = [(sT, self.sT.v)]
            self.ln_build(
                xb2,
                lambda c, ti=ti: h2[:, c * TOWN + ti * SEG:
                                    c * TOWN + (ti + 1) * SEG],
                4 * D, self.sbp, lambda c: sq2[:, c * SEG:(c + 1) * SEG],
                xv)
        sub.close()
        h2c = lambda c, ti: h2[:, c * TOWN + ti * SEG:c * TOWN + (ti + 1) * SEG]
        self.maybe_dbg("h2", lambda: h2[:, 0:DBG_SHAPE[1]])

        gmm = 0
        for ti in range(4):
            for c in range(DCH):
                gmm = self.t(lambda E, c=c, ti=ti,
                             s=self.gwW[:, c * NE:(c + 1) * NE],
                             m=h2c(c, ti): E.matmul(
                    self.pw(2)[0:NE, 0:SEG], s, m,
                    start=(c == 0), stop=(c == DCH - 1)),
                    waits=[(sG, self.sG.v), (sV, self.sV.v),
                           (sS, self.sS.v)],
                    inc=(c == DCH - 1))
            gse = self.s(lambda E, ti=ti, d=gsc[0:NE, ti * SEG:(ti + 1) * SEG]:
                         E.activation(d, self.pw(2)[0:NE, 0:SEG], AF.Sigmoid),
                         waits=[(sT, gmm)], inc=True)
        gsv = self.v(lambda E: E.tensor_scalar_mul(gsc[:], gsc[:],
                                                   self.rsc8[:, 0:1]),
                     waits=[(sS, self.sS.v)], inc=True)
        gt_d = []
        for e in range(NE):
            dst = g_tm[:, :].rearrange("p (t e) -> p t e", e=NE)[:, :, e]
            gt_d = gt_d + self.relay(gsc[e:e + 1, :], dst, TOWN,
                                     waits=[(sV, gsv)])
        gv = g_tm[:, :].rearrange("p (t e) -> p t e", e=NE)
        m1b = sb("m1b", [128, 16], F32)
        m2b = sb("m2b", [128, 16], F32)
        isx = sb("isx", [128, 16 * NE], F32)
        iv = isx[:, :].rearrange("p (t e) -> p t e", e=NE)
        m1v = m1b[:, :].rearrange("p (t o) -> p t o", o=1)
        m2v = m2b[:, :].rearrange("p (t o) -> p t o", o=1)
        g1 = self.v(lambda E: E.reduce_max(m1v, gv, axis=AX.X),
                    waits=gt_d, inc=True)
        self.v(lambda E: E.tensor_tensor(iv, gv, m1v.to_broadcast([128, 16, NE]),
                                         ALU.is_ge), waits=[(sV, g1)])
        g2 = self.v(lambda E: E.scalar_tensor_tensor(iv, iv, -1e30, gv,
                                                     ALU.mult, ALU.add),
                    inc=True)
        g3 = self.v(lambda E: E.reduce_max(m2v, iv, axis=AX.X),
                    waits=[(sV, g2)], inc=True)
        self.v(lambda E: E.tensor_tensor(iv, gv, m2v.to_broadcast([128, 16, NE]),
                                         ALU.is_ge), waits=[(sV, g3)])
        self.v(lambda E: E.tensor_mul(iv, iv, gv))
        g4 = self.v(lambda E: E.tensor_add(m1b[:], m1b[:], m2b[:]), inc=True)
        g5 = self.v(lambda E: E.reciprocal(m1b[:], m1b[:]),
                    waits=[(sV, g4)], inc=True)
        gtv = self.v(lambda E: E.tensor_mul(
            iv, iv, m1v.to_broadcast([128, 16, NE])), waits=[(sV, g5)],
            inc=True)
        gf_d = []
        gscr = []
        for e in range(NE):
            off = self.relay_off
            self.relay_off += 2048
            scr = self.relay_h[off:off + 2048]
            wv2 = self.dma("scalar", scr,
                           isx[:, :].rearrange("p (t e) -> p t e", e=NE)[:, :, e],
                           waits=[(sV, gtv)], slow=True)
            gd = self.dma("gpsimd", gfb[e:e + 1, :],
                          scr.rearrange("(p t) -> t p", t=16),
                          waits=wv2, slow=True)
            gf_d = gf_d + wv2 + gd
            gscr.append(scr)
        self.maybe_dbg("gf", lambda: gsc[0:NE, 0:DBG_SHAPE[1]])

        HB = HF // 128
        wv_pend = {}
        for ti in range(4):
            for e in range(NE):
                si = e % 2
                if (ti, e) not in wv_pend:
                    pass
                wv = self.dma("gpsimd",
                              wsl[si][:].rearrange("p (c m) -> p c m", c=DCH),
                              self.w1e_h[e, :, :]
                              .rearrange("(c p) m -> p c m", p=128),
                              waits=[(sT, self.wsl_free.get(si, 0))])
                for m in range(HB):
                    mm = 0
                    for kk in range(DCH):
                        stat = wsl[si][:, kk * HF + m * 128:kk * HF + (m + 1) * 128]
                        mov = h2c(kk, ti)
                        mm = self.t(lambda E, s=stat, mo=mov, m3=m, kk=kk:
                                    E.matmul(self.pw(m3 % 2)[:, 0:SEG], s, mo,
                                             start=(kk == 0),
                                             stop=(kk == DCH - 1)),
                                    waits=wv + [(sS, self.sS.v),
                                                (sV, self.sV.v)],
                                    inc=(kk == DCH - 1))
                    dcol = h1e[:, (e * HB + m) * SEG:(e * HB + m + 1) * SEG]
                    self.s(lambda E, d=dcol, m3=m, b=self.b1es[:, e * HB + m:
                                                              e * HB + m + 1]:
                           E.activation(d, self.pw(m3 % 2)[:, 0:SEG], AF.Gelu,
                                        bias=b), waits=[(sT, mm)], inc=True)
                self.wsl_free[si] = self.sT.v
                grv = self.dma(
                    "sync", growb[0:1, :],
                    gscr[e].rearrange("(p t) -> t p", t=16)[4 * ti:4 * ti + 4, :],
                    waits=gf_d + [(sT, self.gro_free)], slow=True)
                bct = self.t(lambda E, m=growb[0:1, :]:
                             E.matmul(self.pw(2)[:, 0:SEG], self.onerow_f[:, :],
                                      m, start=True, stop=True),
                             waits=grv + [(sS, self.sS.v)], inc=True)
                self.gro_free = bct
                gbe = self.s(lambda E: E.copy(gb[:], self.pw(2)[:, 0:SEG]),
                             waits=[(sT, bct)], inc=True)
                gw_v = 0
                for m in range(HB):
                    dcol = h1e[:, (e * HB + m) * SEG:(e * HB + m + 1) * SEG]
                    gw_v = self.v(lambda E, d=dcol: E.tensor_mul(d, d, gb[:]),
                                  waits=[(sS, gbe)], inc=True)
                self.t(lambda E: None, waits=[(sV, gw_v)])
            for (w_h, dst, bias, nb) in ((self.ws1_h, hs, self.bs1s, HB),):
                si = 0
                wv = self.dma("gpsimd",
                              wsl[si][:, 0:DCH * HF]
                              .rearrange("p (c m) -> p c m", c=DCH),
                              w_h[:, :].rearrange("(c p) m -> p c m", p=128),
                              waits=[(sT, self.wsl_free.get(si, 0))])
                for m in range(nb):
                    mm = 0
                    for kk in range(DCH):
                        stat = wsl[si][:, kk * HF + m * 128:kk * HF + (m + 1) * 128]
                        mov = h2c(kk, ti)
                        mm = self.t(lambda E, s=stat, mo=mov, m3=m, kk=kk:
                                    E.matmul(self.pw(m3 % 2)[:, 0:SEG], s, mo,
                                             start=(kk == 0),
                                             stop=(kk == DCH - 1)),
                                    waits=wv + [(sS, self.sS.v)],
                                    inc=(kk == DCH - 1))
                    dcol = dst[:, m * SEG:(m + 1) * SEG]
                    self.s(lambda E, d=dcol, m3=m, b=bias[:, m:m + 1]:
                           E.activation(d, self.pw(m3 % 2)[:, 0:SEG], AF.Gelu,
                                        bias=b), waits=[(sT, mm)], inc=True)
                self.wsl_free[si] = self.sT.v
            h1_done = self.sS.v
            gwv_done = self.sV.v
            for mg in range(2):
                ymm = [0] * 4
                for src_i in range(NE + 1):
                    si = src_i % 2
                    if src_i < NE:
                        wv = self.dma("gpsimd",
                                      wsl[si][:, 0:2048]
                                      .rearrange("p (c m) -> p c m", c=HB),
                                      self.w2e_h[src_i, :,
                                                 mg * 512:(mg + 1) * 512]
                                      .rearrange("(c p) m -> p c m", p=128),
                                      waits=[(sT, self.wsl_free.get(si, 0))])
                        hsrc = lambda kk, src_i=src_i: h1e[
                            :, (src_i * HB + kk) * SEG:(src_i * HB + kk + 1) * SEG]
                    else:
                        wv = self.dma("gpsimd",
                                      wsl[si][:, 0:2048]
                                      .rearrange("p (c m) -> p c m", c=HB),
                                      self.ws2_h[:, mg * 512:(mg + 1) * 512]
                                      .rearrange("(c p) m -> p c m", p=128),
                                      waits=[(sT, self.wsl_free.get(si, 0))])
                        hsrc = lambda kk: hs[:, kk * SEG:(kk + 1) * SEG]
                    for m in range(4):
                        mm = 0
                        for kk in range(HB):
                            stat = wsl[si][:, kk * 512 + m * 128:
                                           kk * 512 + (m + 1) * 128]
                            mov = hsrc(kk)
                            mm = self.t(lambda E, s=stat, mo=mov, m3=m,
                                        first=(src_i == 0 and kk == 0):
                                        E.matmul(self.pw(m3)[:, 0:SEG], s, mo,
                                                 start=first, stop=False),
                                        waits=wv + [(sS, h1_done),
                                               (sV, gwv_done),
                                               (sS, self.sS.v),
                                               (sV, self.sV.v)],
                                        inc=(kk == HB - 1))
                        ymm[m] = mm
                    self.wsl_free[si] = self.sT.v
                for m in range(4):
                    mc = mg * 4 + m
                    bmm = self.t(lambda E, m=m, mc=mc,
                                 s=self.b2e_s[0:NE, mc * 128:(mc + 1) * 128],
                                 mo=gfb[0:NE, ti * SEG:(ti + 1) * SEG]:
                                 E.matmul(self.pw(m)[:, 0:SEG], s, mo,
                                          start=False, stop=True),
                                 waits=gf_d + self.c_g, inc=True)
                    xv2 = self.dma("sync", self.xf32[m % 2][:],
                                   self.xmid_h[mc * 128:(mc + 1) * 128,
                                               ti * SEG:(ti + 1) * SEG],
                                   waits=[(sV, self.xf_free[m % 2])])
                    sv5 = self.s(lambda E, m=m, mc=mc: E.activation(
                        self.pw(m, SEG)[:, :], self.pw(m, SEG)[:, :],
                        AF.Identity,
                        bias=self.bs2p[:, mc:mc + 1]),
                        waits=[(sT, bmm)] + self.bs2_tok, inc=True)
                    sv4 = self.v(lambda E, m=m, mc=mc: E.scalar_tensor_tensor(
                        self.xmo[m % 2][:], self.pw(m, SEG)[:, :],
                        self.gb_p[:, mc:mc + 1], self.xf32[m % 2][:],
                        ALU.mult, ALU.add),
                        waits=[(sS, sv5)] + xv2 +
                              self.xmo_free[m % 2], inc=True)
                    self.xf_free[m % 2] = sv4
                    self.xmo_free[m % 2] = self.dma(
                        "sync", self.out_h[mc * 128:(mc + 1) * 128,
                                           ti * SEG:(ti + 1) * SEG],
                        self.xmo[m % 2][:], waits=[(sV, sv4)])
            self.barrier()
        ctx.close()

    def build(self):
        self.declare_io()
        self.alloc_persistent()
        self.load_consts()
        self.bs2_tok = self.dma(
            "sync", self.bs2p[:],
            self.bs2row_h[:].rearrange("(c p) -> p c", p=128), slow=True)
        self.relay_off = 8192
        self.relay_base = 8192 + 32768
        self.phase_mod()
        self.phase_small_consts()
        self.scp_tok = [[] for _ in range(H)]
        self.bankr = {b: [] for b in range(8)}
        self.r4_busy = []
        self.beta_busy = []
        self.drv_busy = []
        self.scratch_busy = []
        self.subB_busy = []
        self.dtm_busy = []
        self.vtm_busy = []
        self.stats_busy = []
        self.h1_busy = []
        self.btm_busy = []
        self.fir_guard = []
        self.ktm_busy = [0, 0]
        self.rhs_busy = [0] * H
        self.xf_free = {0: 0, 1: 0}
        self.xmo_free = {0: [], 1: []}
        self.slot_free = [0, 0]
        self.wsl_free = {}
        for i in range(N_PRE):
            self.segment(i * SEG, own=False, glob_first=(i == 0))
        gt = 0
        for tl in [self.vtail] + self.pret:
            gt = self.v(lambda E, tl=tl: E.tensor_scalar_mul(
                tl[:], tl[:], self.pfl[:, 0:1]), waits=self.snap(), inc=True)
        self.barrier()
        for i in range(N_OWN):
            self.segment(TOWN + i * SEG, own=True,
                         glob_first=(N_PRE == 0 and i == 0))
        if RUN_FFN:
            self.ffn()
        fin = self.snap()
        for eng in Prog.ENGINES:
            self.P.op(eng, lambda E: None, waits=fin)
        self.emit()
        return self.nc

    def emit(self):
        P, nc = self.P, self.nc
        with nc.Block() as blk:
            @blk.tensor
            def _(t):
                for th in P.q["tensor"]:
                    th(t)

            @blk.vector
            def _(v):
                for th in P.q["vector"]:
                    th(v)

            @blk.scalar
            def _(s):
                for th in P.q["scalar"]:
                    th(s)

            @blk.gpsimd
            def _(g):
                for th in P.q["gpsimd"]:
                    th(g)

            @blk.sync
            def _(sy):
                for th in P.q["sync"]:
                    th(sy)


def _host_inputs(inputs, b, half):
    i = inputs
    xb_ = np.asarray(i["L0_attn_x"], dtype=np.float32)[b]
    if half == 0:
        xT = np.ascontiguousarray(
            np.concatenate([xb_[0:TOWN], xb_[0:TOWN]], 0).T)
    else:
        xT = np.ascontiguousarray(xb_.T)

    def fm(a, w):
        return np.ascontiguousarray(
            a.reshape(DCH, 128, w).transpose(1, 0, 2).reshape(128, DCH * w))

    refk = [0, 2, 1, 3]
    w1 = np.asarray(i["L0_attn_W1"], dtype=np.float32)
    w1b = np.stack([w1[1024 + (r % 4) * 4 + refk[r // 4]] for r in range(16)])
    w1bv = np.stack([w1[1024 + st * 4 + 1] for st in range(4)])
    d = {
        "xT": xT,
        "pflag": np.full((128, 1), float(half), np.float32),
        "wq": i["L0_attn_Wq"], "wk": i["L0_attn_Wk"], "wv": i["L0_attn_Wv"],
        "wb": i["L0_attn_Wb"],
        "convf": np.concatenate(
            [fm(np.asarray(i[f"L0_attn_conv_{p}"],
                           dtype=np.float32)[:, ::-1], 4)
             for p in "qkv"], axis=1),
        "firl": fm(np.asarray(i["L0_attn_fir_long"],
                              dtype=np.float32).reshape(1024, 64)[:, ::-1], 64),
        "firs": fm(np.asarray(i["L0_attn_fir_short"],
                              dtype=np.float32).reshape(1024, 5)[:, ::-1], 5),
        "w1a": w1[:1024], "w1b": w1b, "w1bv": w1bv,
        "b1": np.asarray(i["L0_attn_b1"], dtype=np.float32).reshape(DCH, 128).T,
        "w2": i["L0_attn_W2"],
        "b2row": np.asarray(i["L0_attn_b2"], dtype=np.float32).reshape(1, 4),
        "bbrow": np.asarray(i["L0_attn_base_bias"],
                            dtype=np.float32).reshape(1, 16),
        "ltrow": np.asarray(i["L0_attn_log_temp"],
                            dtype=np.float32).reshape(1, 4),
        "crlrow": np.asarray(i["L0_attn_conv_residual_logit"],
                             dtype=np.float32).reshape(1, 4),
        "rmsw": np.asarray(i["L0_attn_rms_w"],
                           dtype=np.float32).reshape(2, 128).T,
        "wo": i["L0_attn_Wo"],
        "gw": i["L0_ffn_gate_W"],
        "w1e": i["L0_ffn_W1"],
        "b1e": np.asarray(i["L0_ffn_b1"], dtype=np.float32)
        .reshape(NE, 4, 128).transpose(2, 0, 1).reshape(128, NE * 4),
        "w2e": i["L0_ffn_W2"],
        "b2e": i["L0_ffn_b2"],
        "ws1": i["L0_ffn_Ws1"],
        "bs1": np.asarray(i["L0_ffn_bs1"], dtype=np.float32).reshape(4, 128).T,
        "ws2": i["L0_ffn_Ws2"],
        "bs2row": np.asarray(i["L0_ffn_bs2"], dtype=np.float32).reshape(1024),
        "rsc": np.asarray(i["L0_ffn_route_scale"],
                          dtype=np.float32).reshape(1, 1),
        "cvec": np.asarray(i["L0_c"], dtype=np.float32)[b].reshape(DCH, 128).T,
        "wmod": i["L0_wmod"],
    }
    return {k: np.ascontiguousarray(np.asarray(v, dtype=np.float32))
            for k, v in d.items()}


def kernel(**inputs):
    bk = BK()
    nc = bk.build()
    in_maps = [_host_inputs(inputs, c // 2, c % 2) for c in range(8)]
    res = run_bass_kernel_spmd(nc, in_maps, core_ids=list(range(8)))
    out = np.zeros((B, L, D), np.float32)
    for c in range(8):
        b, half = c // 2, c % 2
        out[b, half * TOWN:(half + 1) * TOWN, :] = res.results[c]["out"].T
    return out
